# version 21
# baseline (speedup 1.0000x reference)
import numpy as np
import ml_dtypes
import concourse.bass as bass
import concourse.mybir as mybir
from concourse.bass_utils import run_bass_kernel_spmd
from concourse.alu_op_type import AluOpType as ALU

from contextlib import ExitStack

F32 = mybir.dt.float32
BF16 = mybir.dt.bfloat16
AF = mybir.ActivationFunctionType
AXX = mybir.AxisListType.X
BF = ml_dtypes.bfloat16

D = 2048
DC = 16
H = 16
HD = 128
FH = 5632
FHC = 44
TOWN = 1024
THALO = 512
TTOT = 1536
NBLK = 2
SCALE = 1.0 / np.sqrt(128.0)
NEG = -1e30
LN_EPS = 1e-5
FFN_EPS = 1e-8
NW12 = FH * D
NW3 = D * FH

MBASE = 16512
SBUF_LIMIT = 212832


def build_program():
    nc = bass.Bass("TRN2", target_bir_lowering=False)

    def din(name, shape, dt=F32):
        return nc.dram_tensor(name, shape, dt, kind="ExternalInput").ap()

    x_own_d = din("x_own", [128, DC, TOWN])
    x_halo_d = din("x_halo", [4, 128, D])
    wq_d = din("wq", [H, 128, D])
    wk_d = din("wk", [4, 128, D])
    wv_d = din("wv", [DC, 128, 512])
    wo_d = din("wo", [DC, 128, D])
    w1_d = din("w1", [FHC, 128, D])
    w2_d = din("w2", [FHC, 128, D])
    w3_d = din("w3", [DC, 128, FH])
    msk_d = din("msk", [128, 2, 128], BF16)
    hmsk_d = din("hmsk", [128, 4, 512], BF16)
    idn_d = din("idn", [128, 128], BF16)

    y_d = nc.dram_tensor("y", [128, DC, TOWN], F32, kind="ExternalOutput").ap()
    xsp_d = nc.dram_tensor("xsp", [128, DC, TOWN], F32, kind="Internal").ap()

    gsh12_d = din("gsh12", [22, 128, 1024])
    gsh3_d = din("gsh3", [8, 128, 1408])
    gbin_d = nc.dram_tensor("gbin", [1, 4], F32, kind="Internal").ap()
    gbout_d = nc.dram_tensor("gbout", [1, 4], F32, kind="Internal").ap()

    def sb(name, shape, dt, off):
        elts = 1
        for s in shape[1:]:
            elts *= s
        nbytes = elts * (4 if dt == F32 else 2)
        assert off + nbytes <= SBUF_LIMIT, (name, off, nbytes, off + nbytes)
        return nc.alloc_sbuf_tensor_at(name, shape, dt, offset=MBASE + off).ap()

    x_sb = sb("x", [128, DC, TOWN], F32, 0)
    hid_sb = sb("hid", [128, FHC, TOWN], BF16, 0)
    haloraw_sb = [sb(f"hraw{i}", [128, DC, 128], F32, 65536 + i * 8192) for i in range(2)]
    xnh_sb = sb("xnh", [128, DC, THALO], BF16, 81920)
    wqbf_sb = [sb(f"wqbf{i}", [128, DC, 128], BF16, 188416 + i * 4096) for i in range(2)]
    o_sb = sb("o", [128, DC, TOWN], BF16, 65536)
    outst_sb = [sb(f"outst{i}", [128, TOWN], F32, 90112 + i * 4096) for i in range(2)]
    xn_sb = sb("xn", [128, DC, TOWN], BF16, 98304)
    at_sb2 = [sb(f"at{i}", [128, 8, 512], BF16, 131072 + i * 8192) for i in range(2)]
    pexp_sb = sb("pexp", [128, 2, 640], F32, 147456)
    abf_sb = sb("abf", [128, 4, 640], BF16, 152576)
    hmsk_sb = sb("hmsk", [128, 4, 512], BF16, 157696)
    xrld_sb = [sb(f"xrld{i}", [128, TOWN], F32, 98304 + i * 4096) for i in range(2)]
    xbscr2_sb = [sb(f"xbscr2_{i}", [128, TOWN], BF16, 106496 + i * 2048) for i in range(2)]
    xsqscr2_sb = [sb(f"xsqscr2_{i}", [128, TOWN], BF16, 110592 + i * 2048) for i in range(2)]
    hxb_sb = [sb(f"hxb{i}", [128, D], BF16, (131072, 147456)[i]) for i in range(2)]
    hxq_sb = [sb(f"hxq{i}", [128, D], BF16, (135168, 151552)[i]) for i in range(2)]
    wvbf_sb = sb("wvbf", [128, DC, 512], BF16, 131072)
    wkbf_sb = [sb(f"wkbf{i}", [128, DC, 128], BF16, 147456 + i * 4096) for i in range(2)]
    qh_sb = [sb(f"qh_{i}", [128, 512], BF16, 161792 + i * 1024) for i in range(2)]
    w1raw_sb = [sb(f"w1raw{i}", [128, DC, 128], F32, 131072 + i * 8192) for i in range(2)]
    qh1_sb = [sb(f"qh1{i}", [128, DC, 128], BF16, 147456 + i * 4096) for i in range(2)]
    ascr_sb = sb("ascr", [128, D], BF16, 155648)
    ascr2_sb = sb("ascr2", [128, D], BF16, 90112)
    sig_sb = [sb(f"sig{i}", [128, TOWN], BF16, 159744 + i * 2048) for i in range(2)]
    k_sb = sb("k", [128, 4, TTOT], BF16, 163840)
    wobf_sb = [sb(f"wobf{i}", [128, DC, 128], BF16, 180224 + i * 4096) for i in range(2)]
    v_sb = sb("v", [128, 12, 512], BF16, 176128)
    w2raw_sb = [sb(f"w2raw{i}", [128, DC, 128], F32, 163840 + i * 8192) for i in range(2)]
    qh2_sb = [sb(f"qh2{i}", [128, DC, 128], BF16, 180224 + i * 4096) for i in range(2)]
    gsh_sb = [sb(f"gsh{i}", [128, 1408], F32, (196608, 205312)[i]) for i in range(2)]
    w3raw_sb = [sb(f"w3raw{i}", [128, FHC, 128], F32, 106496 + i * 22528) for i in range(2)]
    qh3_sb = [sb(f"qh3{i}", [128, FHC, 128], BF16, 151552 + i * 11264) for i in range(2)]
    a3scr_sb = sb("a3scr", [128, FH], BF16, 174080)
    msk_sb = sb("msk", [128, 2, 128], BF16, 204800)
    mrow_sb = sb("mrow", [128, TOWN], F32, 155648)
    sqrow_sb = sb("sqrow", [128, TOWN], F32, 159744)
    idn_sb = sb("idn", [128, 128], BF16, 211488)
    ones_sb = sb("ones", [128, 1], BF16, 211744)
    onesf_sb = sb("onesf", [128, 1], F32, 211776)
    onesrow_sb = sb("onesrow", [1, 128], F32, 211808)
    vecs_sb = sb("vecs", [128, 24], F32, 212320)
    gsum_sb = sb("gsum", [1, 4], F32, 212416)
    gsuma_sb = sb("gsuma", [1, 4], F32, 212448)
    ones2d_sb = sb("ones2d", [128, 128], BF16, 212480)
    gthr_sb = sb("gthr", [128, 4], F32, 212736)
    w3pre_sb = sb("w3pre", [128, FHC, 128], F32, 188416)

    def vs(i):
        return vecs_sb[:, i:i + 1]
    V_MAX, V_NEGMAX, V_SUMEXP, V_RECIP = 0, 2, 4, 6
    V_T1, V_NT1, V_T2, V_NT2, V_T3, V_NT3 = 8, 9, 10, 11, 12, 13
    V_GACC = {"w1": 14, "w2": 15, "w3": 16}
    V_GTMP = 17
    V_EPS = 18
    V_TG = 19

    es_top = ExitStack()
    SEMS = {n: es_top.enter_context(nc.semaphore(n))
            for n in ("T", "V", "S", "Lx0", "Lx1", "Lx2", "Lx3", "Lmsk", "Lidn", "Lhmsk",
                      "Lh0", "Lh1", "Lwv0", "Lwv1", "Lkq0", "Lkq1",
                      "Lwo0", "Lwo1", "Lw10", "Lw11", "Lw20", "Lw21",
                      "Lw30", "Lw31", "Lg0", "Lg1", "Lsp",
                      "Lxr0", "Lxr1", "Lst0", "Lst1", "G")}
    CNT = {k: 0 for k in SEMS}

    def inc(instr, name, k=1):
        instr.then_inc(SEMS[name], k)
        CNT[name] += k
        return CNT[name]

    def wt(eng, name, tokv, future=False):
        assert future or tokv <= CNT[name], (name, tokv, CNT[name])
        if tokv > 0:
            eng.wait_ge(SEMS[name], tokv)

    blk = es_top.enter_context(nc.Block())
    tok = {}

    def bcast(ap, n):
        return bass.AP(tensor=ap.tensor, offset=ap.offset,
                       ap=[list(ap.ap[0]), [0, n], list(ap.ap[1])])

    def _(sync):
        for s in range(2):
            i = sync.dma_start(out=haloraw_sb[s].rearrange("p a b -> p (a b)"),
                               in_=x_halo_d[s])
            tok[f"hraw{s}"] = inc(i, f"Lh{s}", 16)
        for g in range(4):
            i = sync.dma_start(out=x_sb[:, 4 * g:4 * (g + 1), :], in_=x_own_d[:, 4 * g:4 * (g + 1), :])
            tok[f"xg{g}"] = inc(i, f"Lx{g}", 16)
        i = sync.dma_start(out=msk_sb[:], in_=msk_d)
        tok["mskd"] = inc(i, "Lmsk", 16)
        i = sync.dma_start(out=idn_sb[:], in_=idn_d)
        tok["idnd"] = inc(i, "Lidn", 16)
    blk.sync(_)

    def _(v):
        v.memset(ones_sb[:], 1.0)
        v.memset(onesf_sb[:], 1.0)
        v.memset(onesrow_sb[:], 1.0)
        v.memset(ones2d_sb[:], 1.0)
        v.memset(vecs_sb[:, 0:V_EPS], 0.0)
        v.memset(vecs_sb[:, V_EPS + 1:], 0.0)
        v.memset(vecs_sb[:, V_EPS:V_EPS + 1], LN_EPS)
        i = v.memset(gsum_sb[0:1, 3:4], 0.0)
        tok["vinit"] = inc(i, "V")
    blk.vector(_)


    def emit_ln_finalize(pst_m, pst_q, ntok, name, extra=()):
        def _(v):
            wt(v, "T", tok[f"{name}st{DC-1}"])
            for semn, tv in extra:
                wt(v, semn, tv)
            v.drain()
            v.tensor_scalar(mrow_sb[:, 0:ntok], pst_m[:], 1.0 / D, None, ALU.mult)
            v.drain()
            v.tensor_tensor(sqrow_sb[:, 0:ntok], mrow_sb[:, 0:ntok],
                            mrow_sb[:, 0:ntok], ALU.mult)
            v.drain()
            i = v.scalar_tensor_tensor(pst_q[:], pst_q[:], 1.0 / D,
                                       sqrow_sb[:, 0:ntok], ALU.mult, ALU.subtract)
            tok[f"{name}var"] = inc(i, "V")
        blk.vector(_)

    def emit_ln_normalize(pst_q, xsrc, xdst, ntok, name, wide=False, between=None):
        esn = ExitStack()
        pw = ntok if wide else 512
        pa = esn.enter_context(nc.psum_tensor(f"pa_{name}", [128, pw], F32))
        pb = esn.enter_context(nc.psum_tensor(f"pb_{name}", [128, pw], F32))
        if wide:
            def _(sc):
                wt(sc, "V", tok[f"{name}var"])
                i = None
                for b2 in range(ntok // 512):
                    sl = slice(b2 * 512, (b2 + 1) * 512)
                    i = sc.activation(pa[:, sl], pst_q[:, sl], AF.Sqrt,
                                      bias=vecs_sb[:, V_EPS:V_EPS + 1])
                for b2 in range(ntok // 512):
                    tok[f"{name}sd{b2}"] = CNT["S"] + 1
                inc(i, "S")
            blk.scalar(_)
        for b2 in range(ntok // 512):
            sl = slice(b2 * 512, (b2 + 1) * 512)
            pl = sl if wide else slice(0, 512)

            if not wide:
                def _(sc):
                    if b2 == 0:
                        wt(sc, "V", tok[f"{name}var"])
                    else:
                        wt(sc, "V", tok[f"{name}nrm{b2-1}"])
                    i = sc.activation(pa[:], pst_q[:, sl], AF.Sqrt,
                                      bias=vecs_sb[:, V_EPS:V_EPS + 1])
                    tok[f"{name}sd{b2}"] = inc(i, "S")
                blk.scalar(_)

            def _(v):
                wt(v, "S", tok[f"{name}sd{b2}"])
                v.reciprocal(pa[:, pl], pa[:, pl])
                v.drain()
                v.scalar_tensor_tensor(pb[:, pl], mrow_sb[:, sl], -1.0, pa[:, pl],
                                       ALU.mult, ALU.mult)
                v.drain()
                dst = xdst[:, :, sl]
                v.tensor_tensor(dst, xsrc[:, :, sl], bcast(pa[:, pl], DC), ALU.mult)
                v.drain()
                i = v.tensor_tensor(dst, dst, bcast(pb[:, pl], DC), ALU.add)
                v.drain()
                tok[f"{name}nrm{b2}"] = inc(i, "V")
            blk.vector(_)
            if between is not None:
                between(b2)
        esn.close()

    esKVH = ExitStack()
    pkH = [esKVH.enter_context(nc.psum_tensor(f"pkH{i}", [128, 512], F32)) for i in range(2)]
    esA2 = ExitStack()
    pstat_m = esA2.enter_context(nc.psum_tensor("pstat_m", [128, TOWN], F32))
    pstat_q = esA2.enter_context(nc.psum_tensor("pstat_q", [128, TOWN], F32))
    esA1 = ExitStack()
    pstH_m = esA1.enter_context(nc.psum_tensor("pstH_m", [128, 128], F32))
    pstH_q = esA1.enter_context(nc.psum_tensor("pstH_q", [128, 128], F32))

    def h_front(s):
        slot = s % 2
        if s >= 2:
            def _(sync):
                wt(sync, "V", tok[f"hnorm{s-2}"])
                i = sync.dma_start(out=haloraw_sb[slot].rearrange("p a b -> p (a b)"),
                                   in_=x_halo_d[s])
                tok[f"hraw{s}"] = inc(i, f"Lh{slot}", 16)
            blk.sync(_)

        def _(v):
            wt(v, f"Lh{slot}", tok[f"hraw{s}"])
            if s >= 2:
                wt(v, "T", tok[f"hstat{s-2}"])
            raw = haloraw_sb[slot].rearrange("p a b -> p (a b)")
            v.tensor_copy(hxb_sb[slot][:], raw)
            i = v.tensor_tensor(hxq_sb[slot][:], raw, raw, ALU.mult)
            tok[f"hxbq{s}"] = inc(i, "V")
        blk.vector(_)

    def h_stats(s):
        slot = s % 2

        def _(t):
            wt(t, "V", tok[f"hxbq{s}"])
            if s >= 1:
                wt(t, "V", tok[f"hnorm{s-1}"])
            for dc in range(DC):
                t.matmul(pstH_m[:], ones2d_sb[:], hxb_sb[slot][:, dc * 128:(dc + 1) * 128],
                         start=(dc == 0), stop=(dc == DC - 1))
            for dc in range(DC):
                i = t.matmul(pstH_q[:], ones2d_sb[:], hxq_sb[slot][:, dc * 128:(dc + 1) * 128],
                             start=(dc == 0), stop=(dc == DC - 1))
            tok[f"hstat{s}"] = inc(i, "T")
        blk.tensor(_)

    def h_fin(s):
        def _(v):
            wt(v, "T", tok[f"hstat{s}"])
            i = v.tensor_scalar(mrow_sb[:, 0:128], pstH_m[:], 1.0 / D, None, ALU.mult)
            tok[f"hm{s}"] = inc(i, "V")
        blk.vector(_)

        def _(sc):
            wt(sc, "V", tok[f"hm{s}"])
            i = sc.activation(sqrow_sb[:, 0:128], mrow_sb[:, 0:128], AF.Square)
            tok[f"hsq{s}"] = inc(i, "S")
        blk.scalar(_)

        def _(v):
            wt(v, "S", tok[f"hsq{s}"])
            i = v.scalar_tensor_tensor(pstH_q[:], pstH_q[:], 1.0 / D,
                                       sqrow_sb[:, 0:128], ALU.mult, ALU.subtract)
            tok[f"hvar{s}"] = inc(i, "V")
        blk.vector(_)

        def _(sc):
            wt(sc, "V", tok[f"hvar{s}"])
            i = sc.activation(pstH_m[:], pstH_q[:], AF.Sqrt,
                              bias=vecs_sb[:, V_EPS:V_EPS + 1])
            tok[f"hsd{s}"] = inc(i, "S")
        blk.scalar(_)

    def h_norm(s):
        slot = s % 2

        def _(v):
            wt(v, "S", tok[f"hsd{s}"])
            v.reciprocal(pstH_m[:], pstH_m[:])
            v.drain()
            v.scalar_tensor_tensor(pstH_q[:], mrow_sb[:, 0:128], -1.0,
                                   pstH_m[:], ALU.mult, ALU.mult)
            v.drain()
            dst = xnh_sb[:, :, s * 128:(s + 1) * 128]
            v.tensor_tensor(dst, haloraw_sb[slot][:], bcast(pstH_m[:], DC), ALU.mult)
            v.drain()
            i = v.tensor_tensor(dst, dst, bcast(pstH_q[:], DC), ALU.add)
            v.drain()
            tok[f"hnorm{s}"] = inc(i, "V")
        blk.vector(_)

    def a2_cast(dc):
        slot = dc % 2

        def _(v):
            wt(v, f"Lx{dc // 4}", tok[f"xg{dc // 4}"])
            if dc >= 2:
                wt(v, "T", tok[f"ln1st{dc-2}"])
            v.tensor_copy(xbscr2_sb[slot][:], x_sb[:, dc, :])
            i = v.tensor_tensor(xsqscr2_sb[slot][:], x_sb[:, dc, :], x_sb[:, dc, :],
                                ALU.mult)
            tok[f"ln1xbq{dc}"] = inc(i, "V")
        blk.vector(_)

    def a2_stats(dc):
        slot = dc % 2

        def _(t):
            wt(t, "V", tok[f"ln1xbq{dc}"])
            for b2 in range(2):
                sl = slice(b2 * 512, (b2 + 1) * 512)
                t.matmul(pstat_m[:, sl], ones2d_sb[:], xbscr2_sb[slot][:, sl],
                         start=(dc == 0), stop=(dc == DC - 1))
                i = t.matmul(pstat_q[:, sl], ones2d_sb[:], xsqscr2_sb[slot][:, sl],
                             start=(dc == 0), stop=(dc == DC - 1))
            tok[f"ln1st{dc}"] = inc(i, "T")
        blk.tensor(_)

    h_front(0); h_front(1)
    halo_stages = []
    for s in range(4):
        halo_stages.append(lambda s=s: h_stats(s))
        halo_stages.append(lambda s=s: h_fin(s))
        if s + 2 < 4:
            halo_stages.append(lambda s=s: (h_norm(s), h_front(s + 2)))
        else:
            halo_stages.append(lambda s=s: h_norm(s))
    for dc in range(DC):
        a2_cast(dc)
        a2_stats(dc)
        if halo_stages:
            halo_stages.pop(0)()
    while halo_stages:
        halo_stages.pop(0)()
    esA1.close()
    emit_ln_finalize(pstat_m, pstat_q, TOWN, "ln1", extra=[("S", tok["hsd3"])])
    emit_ln_normalize(pstat_q, x_sb, xn_sb, TOWN, "ln1")
    esA2.close()

    gsched = ([("w1", i, 1024) for i in range(11)]
              + [("w2", i, 1024) for i in range(11)]
              + [("w3", i, 1408) for i in range(8)])
    gstate = {"pos": 0, "guards": [], "guardn": 0}

    def gamma_pump(n):
        for _n in range(n):
            gi = gstate["pos"]
            if gi >= len(gsched):
                return
            gstate["pos"] += 1
            which, ich, width = gsched[gi]
            slot = gi % 2
            src = gsh3_d[ich] if which == "w3" else gsh12_d[gi]

            def _(sync):
                wt(sync, "V", tok.get(f"gv{gi-2}", 0))
                i = sync.dma_start(out=gsh_sb[slot][:, 0:width], in_=src)
                tok[f"gd{gi}"] = inc(i, f"Lg{slot}", 16)
            blk.sync(_)

            def _(v):
                wt(v, f"Lg{slot}", tok[f"gd{gi}"])
                v.drain()
                v.tensor_reduce(vs(V_GTMP), gsh_sb[slot][:, 0:width], AXX, ALU.add,
                                apply_absolute_value=True)
                v.drain()
                acc = V_GACC[which]
                i = v.tensor_tensor(vs(acc), vs(acc), vs(V_GTMP), ALU.add)
                tok[f"gv{gi}"] = inc(i, "V")
            blk.vector(_)

    def emit_gamma_reduce_head():
        def _(sync):
            wt(sync, "V", tok[f"gv{len(gsched)-1}"])
            wt(sync, "T", last_av)
            i = sync.dma_start(out=mrow_sb[0:1, 0:384], in_=vecs_sb[:, V_GACC["w1"]:V_GACC["w1"] + 3])
            tok["grow"] = inc(i, "Lmsk", 16)
        blk.sync(_)

        def _(v):
            wt(v, "Lmsk", tok["grow"])
            i = None
            for wi in range(3):
                src = bass.AP(tensor=mrow_sb.tensor, offset=mrow_sb.offset + wi,
                              ap=[[mrow_sb.ap[0][0], 1], [3, 128]])
                i = v.tensor_reduce(gsum_sb[0:1, wi:wi + 1], src, AXX, ALU.add)
            tok["gsum"] = inc(i, "V")
        blk.vector(_)

        def _(sync):
            wt(sync, "V", tok["gsum"])
            i = sync.dma_start(out=gbin_d, in_=gsum_sb[0:1, :])
            tok["gbin"] = inc(i, "Lmsk", 16)
        blk.sync(_)

    def emit_gamma_allreduce():
        def _(g):
            wt(g, "Lmsk", tok["gbin"])
            i = g.collective_compute(
                "AllReduce", ALU.add,
                replica_groups=[list(range(8))],
                ins=[gbin_d.opt()], outs=[gbout_d.opt()])
            tok["gcc"] = inc(i, "G")
        blk.gpsimd(_)

    def emit_gamma_back():
        def _(sync):
            wt(sync, "G", tok["gcc"])
            bc = bass.AP(tensor=gbout_d.tensor, offset=gbout_d.offset,
                         ap=[[0, 128], [1, 4]])
            i = sync.dma_start(out=gthr_sb[:], in_=bc)
            tok["gback"] = inc(i, "Lmsk", 16)
        blk.sync(_)

    def emit_gamma_finalize(which, nelem, tslot, ntslot):
        wi = {"w1": 0, "w2": 1, "w3": 2}[which]

        def _(v):
            wt(v, "Lmsk", tok["gback"])
            v.tensor_scalar(vs(tslot), gthr_sb[:, wi:wi + 1],
                            0.5 / nelem, 0.5 * FFN_EPS, ALU.mult, ALU.add)
            v.drain()
            i = v.tensor_scalar(vs(ntslot), vs(tslot), -1.0, None, ALU.mult)
            tok[f"gfin{which}"] = inc(i, "V")
        blk.vector(_)

    def emit_wk_dma(ofc):
        slot = ofc % 2

        def _(g):
            if ofc == 0:
                wt(g, "T", tok["hstat3"])
            if ofc >= 2:
                wt(g, "T", tok[f"kmmO2_{ofc-2}"])
            i = g.dma_start(out=wkbf_sb[slot].rearrange("p a b -> p (a b)"), in_=wk_d[ofc])
            tok[f"wkd{ofc}"] = inc(i, f"Lkq{slot}", 16)
        blk.gpsimd(_)

    emit_wk_dma(0)
    emit_wk_dma(1)

    def _(g):
        wt(g, "T", tok["hstat2"])
        for dc in range(DC):
            i = g.dma_start(out=wvbf_sb[:, dc, :], in_=wv_d[dc])
            tok[f"wvd{dc}"] = inc(i, "Lwv0", 16)
    blk.gpsimd(_)

    def emit_kmmH(ofc):
        slot = ofc % 2

        def _(t):
            wt(t, f"Lkq{slot}", tok[f"wkd{ofc}"])
            if ofc < 2:
                wt(t, "V", tok["hnorm3"])
            if ofc >= 2:
                wt(t, "V", tok[f"kevH{ofc-2}"])
            for dc in range(DC):
                i = t.matmul(pkH[slot][:], wkbf_sb[slot][:, dc, :],
                             xnh_sb[:, dc, :], start=(dc == 0), stop=(dc == DC - 1))
            tok[f"kmmH{ofc}"] = inc(i, "T")
        blk.tensor(_)

        def _(v):
            wt(v, "T", tok[f"kmmH{ofc}"])
            i = v.tensor_copy(k_sb[:, ofc, 0:512], pkH[slot][:])
            tok[f"kevH{ofc}"] = inc(i, "V")
        blk.vector(_)

    esKV = ExitStack()
    pkO = [esKV.enter_context(nc.psum_tensor(f"pkO{i}", [128, 1024], F32)) for i in range(2)]

    def emit_kmmO1(ofc):
        slot = ofc % 2

        def _(t):
            if ofc == 0:
                wt(t, "V", tok["ln1nrm0"])
            if ofc >= 2:
                wt(t, "V", tok[f"kevO{ofc-2}"])
            for dc in range(DC):
                i = t.matmul(pkO[slot][:, 0:512], wkbf_sb[slot][:, dc, :],
                             xn_sb[:, dc, 0:512],
                             start=(dc == 0), stop=(dc == DC - 1))
            tok[f"kmmO1_{ofc}"] = inc(i, "T")
        blk.tensor(_)

    def emit_kmmO2(ofc):
        slot = ofc % 2

        def _(t):
            if ofc == 0:
                wt(t, "V", tok["ln1nrm1"])
            for dc in range(DC):
                i = t.matmul(pkO[slot][:, 512:1024], wkbf_sb[slot][:, dc, :],
                             xn_sb[:, dc, 512:1024],
                             start=(dc == 0), stop=(dc == DC - 1))
            tok[f"kmmO2_{ofc}"] = inc(i, "T")
        blk.tensor(_)

        def _(v):
            wt(v, "T", tok[f"kmmO2_{ofc}"])
            i = v.tensor_copy(k_sb[:, ofc, 512:1536], pkO[slot][:])
            tok[f"kevO{ofc}"] = inc(i, "V")
        blk.vector(_)

    emit_kmmH(0)
    emit_kmmH(1)
    emit_kmmO1(0)
    emit_kmmO1(1)
    emit_kmmO2(0)
    emit_kmmO2(1)
    emit_wk_dma(2)
    emit_wk_dma(3)
    emit_kmmH(2)
    emit_kmmH(3)
    emit_kmmO1(2)
    emit_kmmO1(3)
    emit_kmmO2(2)
    emit_kmmO2(3)

    pv = [esKV.enter_context(nc.psum_tensor(f"pv{i}", [128, 512], F32)) for i in range(2)]
    for tc in range(12):
        slot = tc % 2

        def _(t):
            if tc == 0:
                wt(t, "Lwv0", tok[f"wvd{DC-1}"])
            if tc < 4:
                wt(t, "V", tok[f"hnorm{tc}"])
            if tc >= 2:
                wt(t, "V", tok[f"vev{tc-2}"])
            for dc in range(DC):
                if tc < 4:
                    lhs = xnh_sb[:, dc, tc * 128:(tc + 1) * 128]
                else:
                    lhs = xn_sb[:, dc, (tc - 4) * 128:(tc - 3) * 128]
                i = t.matmul(pv[slot][:], lhs, wvbf_sb[:, dc, :],
                             start=(dc == 0), stop=(dc == DC - 1))
            tok[f"vmm{tc}"] = inc(i, "T")
        blk.tensor(_)

        def _(v):
            wt(v, "T", tok[f"vmm{tc}"])
            i = v.tensor_copy(v_sb[:, tc, :], pv[slot][:])
            tok[f"vev{tc}"] = inc(i, "V")
        blk.vector(_)
    esKV.close()
    esKVH.close()

    esB = ExitStack()
    pS = [esB.enter_context(nc.psum_tensor(f"pS{i}", [128, 640], F32)) for i in range(2)]
    pT = [esB.enter_context(nc.psum_tensor(f"pT{i}", [128, 640], BF16)) for i in range(2)]
    pAV = [esB.enter_context(nc.psum_tensor("pAV0", [128, 512], F32))]
    pQ = esB.enter_context(nc.psum_tensor("pQ", [128, 512], F32))

    def _(sync):
        wt(sync, "V", tok["ln1nrm1"])
        i = sync.dma_start(out=hmsk_sb[:], in_=hmsk_d)
        tok["hmskd"] = inc(i, "Lhmsk", 16)
    blk.sync(_)

    exp_par = {0: 0, 1: 0}
    exp_ops = []
    S_PLAN = [CNT["S"]]
    oev_par = {0: 0, 1: 0}
    av_par = {0: 0, 1: 0}
    atev_par = {0: 0, 1: 0}
    last_av = 0
    pending_av = [None]

    def emit_av(t):
        (ph, pb, pavpar, pkvh, pbb) = pending_av[0]
        at = at_sb2[pbb]
        wt(t, "V", tok[f"atev{ph * NBLK + pb}_3"])
        wt(t, "V", oev_par[0])
        i = None
        for c in range(8):
            vt = v_sb[:, 4 * pb + c, pkvh * 128:(pkvh + 1) * 128]
            if c == 0:
                i = t.matmul(pAV[0][:, 0:128], vt, at[:, 0, 0:128],
                             start=True, stop=False)
            elif c < 4:
                i = t.matmul(pAV[0][:, 0:128 * c], vt, at[:, c, 0:128 * c],
                             start=False, stop=False)
                i = t.matmul(pAV[0][:, 128 * c:128 * (c + 1)], vt,
                             at[:, c, 128 * c:128 * (c + 1)],
                             start=False, stop=False)
            else:
                cp = c - 4
                i = t.matmul(pAV[0][:, 128 * cp:512], vt,
                             at[:, c, 128 * cp:512],
                             start=False, stop=(c == 7))
        tok[f"av{ph}_{pb}"] = inc(i, "T")
        av_par[pbb] = tok[f"av{ph}_{pb}"]

    def emit_oev(v):
        (ph, pb, pavpar, pkvh, pbb) = pending_av.pop(0)
        wt(v, "T", tok[f"av{ph}_{pb}"])
        i = v.tensor_copy(o_sb[:, ph, 512 * pb:512 * (pb + 1)], pAV[0][:])
        tok[f"oev{ph}_{pb}"] = inc(i, "V")
        oev_par[0] = tok[f"oev{ph}_{pb}"]

    for h in range(H):
        kvh = h // 4
        hslot = h % 2

        def _(g, h=h, hslot=hslot):
            if h < 2:
                wt(g, "T", tok[f"kmmH{2+h}"])
            else:
                wt(g, "T", tok[f"q1mm{h-2}"])
            i = g.dma_start(out=wqbf_sb[hslot].rearrange("p a b -> p (a b)"),
                            in_=wq_d[h])
            tok[f"wqd{h}"] = inc(i, f"Lkq{hslot}", 16)
        blk.gpsimd(_)

        def emit_qmm(t, half):
            wt(t, f"Lkq{hslot}", tok[f"wqd{h}"])
            if half == 0 and h == 0:
                wt(t, "V", tok["vev11"])
            if half == 0 and h >= 1:
                wt(t, "V", tok[f"qev1_{h-1}"])
            if half == 1:
                wt(t, "V", tok[f"qev0_{h}"])
            for dc in range(DC):
                lhs = wqbf_sb[hslot][:, dc, :]
                i = t.matmul(pQ[:], lhs, xn_sb[:, dc, 512 * half:512 * (half + 1)],
                             start=(dc == 0), stop=(dc == DC - 1))
            tok[f"q{half}mm{h}"] = inc(i, "T")

        def emit_qev(v, half):
            wt(v, "T", tok[f"q{half}mm{h}"])
            if h >= 1:
                wt(v, "T", tok[f"smm{h-1}_{half}_3"])
            i = v.tensor_scalar(qh_sb[half][:], pQ[:], SCALE, None, ALU.mult)
            tok[f"qev{half}_{h}"] = inc(i, "V")

        def _(t, h=h):
            emit_qmm(t, 0)
        blk.tensor(_)

        def _(v, h=h):
            emit_qev(v, 0)
        blk.vector(_)

        for b in range(NBLK):
            avpar = (h * NBLK + b) % 2
            bb = (h * NBLK + b) % 2
            at_sb = at_sb2[bb]

            def emit_smm(t, j):
                par = j % 2
                k0 = 512 * b + 128 * j
                wt(t, "S", exp_par[par], future=True)
                wt(t, "V", tok[f"qev{b}_{h}"])
                if h == 0 and b == 0 and j == 0:
                    wt(t, "V", tok["kevO3"])
                    wt(t, "V", tok["vev11"])
                    wt(t, "Lidn", tok["idnd"])
                    wt(t, "Lhmsk", tok["hmskd"])
                    wt(t, "Lmsk", tok["mskd"])
                lhs = qh_sb[b][:, 128 * j: 128 * (j + 1)]
                t.matmul(pS[par][:, 0:512], lhs, k_sb[:, kvh, k0:k0 + 512],
                         start=True, stop=False)
                t.matmul(pS[par][:, 512:640], lhs, k_sb[:, kvh, k0 + 512:k0 + 640],
                         start=True, stop=False)
                t.matmul(pS[par][:, 0:128], idn_sb[:], msk_sb[:, 0, :],
                         start=False, stop=(b != 0))
                i = t.matmul(pS[par][:, 512:640], idn_sb[:], msk_sb[:, 1, :],
                             start=False, stop=True)
                if b == 0:
                    i = t.matmul(pS[par][:, 0:512], idn_sb[:], hmsk_sb[:, j, :],
                                 start=False, stop=True)
                tok[f"smm{h}_{b}_{j}"] = inc(i, "T")

            def emit_red(v, j):
                par = j % 2
                wt(v, "T", tok[f"smm{h}_{b}_{j}"])
                i = v.tensor_reduce(vs(V_NEGMAX + par), pS[par][:], AXX, ALU.max,
                                    negate=True)
                tok[f"msk{h}_{b}_{j}"] = inc(i, "V")

            def emit_exp(sc_unused, j):
                par = j % 2
                wtok = tok[f"msk{h}_{b}_{j}"]
                S_PLAN[0] += 1
                tok[f"exp{h}_{b}_{j}"] = S_PLAN[0]
                exp_par[par] = S_PLAN[0]

                def op(sc, wtok=wtok, par=par):
                    wt(sc, "V", wtok)
                    i = sc.activation(pexp_sb[:, par, :], pS[par][:], AF.Exp,
                                      bias=vs(V_NEGMAX + par),
                                      accum_out=vs(V_SUMEXP + par))
                    inc(i, "S")
                exp_ops.append(op)

            def emit_recip(v, j):
                par = j % 2
                wt(v, "S", tok[f"exp{h}_{b}_{j}"], future=True)
                i = v.reciprocal(vs(V_RECIP + par), vs(V_SUMEXP + par))
                tok[f"rcp{h}_{b}_{j}"] = inc(i, "V")

            def emit_nrm(v, j):
                par = j % 2
                jj = (h * NBLK + b) * 4 + j
                par4 = jj % 4
                if jj >= 4:
                    wt(v, "T", tok[f"tr{jj-4}"])
                i = v.tensor_scalar(abf_sb[:, par4, :], pexp_sb[:, par, :],
                                    vs(V_RECIP + par), None, ALU.mult)
                tok[f"nrm{jj}"] = inc(i, "V")

            def emit_tr(t, jj):
                par = jj % 2
                wt(t, "V", tok[f"nrm{jj}"])
                wt(t, "V", atev_par[par])
                for c in range(5):
                    i = t.transpose(pT[par][:, c * 128:(c + 1) * 128],
                                    abf_sb[:, jj % 4, c * 128:(c + 1) * 128], idn_sb[:])
                tok[f"tr{jj}"] = inc(i, "T")

            def emit_atev(v, jj):
                par = jj % 2
                j = jj % 4
                blki = jj // 4
                wt(v, "T", tok[f"tr{jj}"])
                if j == 0:
                    wt(v, "T", av_par[blki % 2])
                i = v.tensor_copy(at_sb2[blki % 2][:, j:j + 5, 128 * j:128 * (j + 1)],
                                  pT[par].rearrange("p (c m) -> p c m", c=5))
                tok[f"atev{blki}_{j}"] = inc(i, "V")
                atev_par[par] = tok[f"atev{blki}_{j}"]

            blkid = h * NBLK + b

            def _(t):
                emit_smm(t, 0); emit_smm(t, 1)
                if blkid >= 1:
                    emit_tr(t, (blkid - 1) * 4 + 2)
                    emit_tr(t, (blkid - 1) * 4 + 3)
            blk.tensor(_)

            def _(v):
                emit_red(v, 0); emit_red(v, 1)
                if blkid >= 1:
                    emit_atev(v, (blkid - 1) * 4 + 2)
                    emit_atev(v, (blkid - 1) * 4 + 3)
            blk.vector(_)

            emit_exp(None, 0); emit_exp(None, 1)

            def _(v):
                emit_recip(v, 0); emit_recip(v, 1)
                v.drain()
                emit_nrm(v, 0); emit_nrm(v, 1)
            blk.vector(_)

            def _(t):
                emit_smm(t, 2); emit_smm(t, 3)
                emit_tr(t, blkid * 4 + 0); emit_tr(t, blkid * 4 + 1)
                if b == 0:
                    emit_qmm(t, 1)
                if pending_av[0] is not None:
                    emit_av(t)
            blk.tensor(_)

            def _(v):
                emit_red(v, 2); emit_red(v, 3)
                emit_atev(v, blkid * 4 + 0); emit_atev(v, blkid * 4 + 1)
                if b == 0:
                    emit_qev(v, 1)
                if pending_av[0] is not None:
                    emit_oev(v)
            blk.vector(_)

            emit_exp(None, 2); emit_exp(None, 3)

            def _(v):
                emit_recip(v, 2); emit_recip(v, 3)
                v.drain()
                emit_nrm(v, 2); emit_nrm(v, 3)
            blk.vector(_)

            pending_av.insert(0, (h, b, avpar, kvh, bb))
            gamma_pump(1)

    def _(sc):
        for op in exp_ops:
            op(sc)
    blk.scalar(_)
    assert CNT["S"] == S_PLAN[0], (CNT["S"], S_PLAN[0])

    lastblk = H * NBLK - 1

    for jj in (lastblk * 4 + 2, lastblk * 4 + 3):
        def _(t, jj=jj):
            par = jj % 2
            t.wait_ge(SEMS["V"], tok[f"nrm{jj}"])
            t.wait_ge(SEMS["V"], atev_par[par])
            for c in range(5):
                i = t.transpose(pT[par][:, c * 128:(c + 1) * 128],
                                abf_sb[:, jj % 4, c * 128:(c + 1) * 128], idn_sb[:])
            tok[f"tr{jj}"] = inc(i, "T")
        blk.tensor(_)

        def _(v, jj=jj):
            par = jj % 2
            j = jj % 4
            wt(v, "T", tok[f"tr{jj}"])
            i = v.tensor_copy(at_sb2[lastblk % 2][:, j:j + 5, 128 * j:128 * (j + 1)],
                              pT[par].rearrange("p (c m) -> p c m", c=5))
            tok[f"atev{lastblk}_{j}"] = inc(i, "V")
            atev_par[par] = tok[f"atev{lastblk}_{j}"]
        blk.vector(_)

    def _(t):
        emit_av(t)
    blk.tensor(_)

    def _(v):
        emit_oev(v)
    blk.vector(_)
    last_av = tok[f"av{H-1}_{NBLK-1}"]

    esB.close()

    emit_gamma_reduce_head()
    emit_gamma_allreduce()

    esC2 = ExitStack()
    pstat2_m = esC2.enter_context(nc.psum_tensor("pstat2_m", [128, TOWN], F32))
    pstat2_q = esC2.enter_context(nc.psum_tensor("pstat2_q", [128, TOWN], F32))
    esC = ExitStack()
    pwo = [esC.enter_context(nc.psum_tensor(f"pwo{i}", [128, 1024], F32)) for i in range(2)]

    def c_front(doc):
        slot = doc % 2

        def _(g):
            if doc == 0:
                wt(g, "T", last_av)
            if doc >= 2:
                wt(g, "T", tok[f"womm{doc-2}"])
            i = g.dma_start(out=wobf_sb[slot].rearrange("p a b -> p (a b)"), in_=wo_d[doc])
            tok[f"wod{doc}"] = inc(i, f"Lwo{slot}", 16)
        blk.gpsimd(_)

    def c_body(doc):
        slot = doc % 2

        def _(t):
            wt(t, f"Lwo{slot}", tok[f"wod{doc}"])
            if doc == 0:
                wt(t, "V", tok[f"oev{H-1}_{NBLK-1}"])
            if doc >= 2:
                wt(t, "V", tok[f"res{doc-2}"])
            for fc in range(DC):
                lhs = wobf_sb[slot][:, fc, :]
                t.matmul(pwo[slot][:, 0:512], lhs, o_sb[:, fc, 0:512],
                         start=(fc == 0), stop=(fc == DC - 1))
                i = t.matmul(pwo[slot][:, 512:1024], lhs, o_sb[:, fc, 512:1024],
                             start=(fc == 0), stop=(fc == DC - 1))
            tok[f"womm{doc}"] = inc(i, "T")
        blk.tensor(_)

        def _(v):
            wt(v, "T", tok[f"womm{doc}"])
            i = v.tensor_tensor(x_sb[:, doc, :], x_sb[:, doc, :], pwo[slot][:], ALU.add)
            tok[f"res{doc}"] = inc(i, "V")
        blk.vector(_)

        def _(sc):
            wt(sc, "V", tok[f"res{doc}"])
            i = sc.dma_start(out=xsp_d[:, doc, :], in_=x_sb[:, doc, :])
            tok[f"spill{doc}"] = inc(i, "Lsp", 16)
        blk.scalar(_)

        def _(v):
            if doc >= 2:
                wt(v, "T", tok[f"ln2st{doc-2}"])
            v.drain()
            v.tensor_copy(xbscr2_sb[slot][:], x_sb[:, doc, :])
            v.drain()
            i = v.tensor_tensor(xsqscr2_sb[slot][:], x_sb[:, doc, :],
                                x_sb[:, doc, :], ALU.mult)
            tok[f"ln2xbq{doc}"] = inc(i, "V")
        blk.vector(_)

    def c_stats(doc):
        slot = doc % 2

        def _(t):
            wt(t, "V", tok[f"ln2xbq{doc}"])
            for b2 in range(2):
                sl = slice(b2 * 512, (b2 + 1) * 512)
                t.matmul(pstat2_m[:, sl], ones2d_sb[:], xbscr2_sb[slot][:, sl],
                         start=(doc == 0), stop=(doc == DC - 1))
                i = t.matmul(pstat2_q[:, sl], ones2d_sb[:], xsqscr2_sb[slot][:, sl],
                             start=(doc == 0), stop=(doc == DC - 1))
            tok[f"ln2st{doc}"] = inc(i, "T")
        blk.tensor(_)

    def d1_dma(hc):
        slot = hc % 2

        def _(sync):
            if hc < 2:
                wt(sync, "T", last_av)
            else:
                wt(sync, "V", tok[f"qz1_{hc-2}"])
            i = sync.dma_start(out=w1raw_sb[slot].rearrange("p a b -> p (a b)"), in_=w1_d[hc])
            tok[f"w1d{hc}"] = inc(i, f"Lw1{slot}", 16)
            if hc >= 2:
                wt(sync, "V", tok[f"qz2_{hc-2}"])
            i = sync.dma_start(out=w2raw_sb[slot].rearrange("p a b -> p (a b)"), in_=w2_d[hc])
            tok[f"w2d{hc}"] = inc(i, f"Lw2{slot}", 16)
        blk.sync(_)

    def d1_qz1(hc):
        slot = hc % 2

        def _(v):
            wt(v, f"Lw1{slot}", tok[f"w1d{hc}"])
            if hc == 0:
                wt(v, "V", tok["gfinw1"])
                wt(v, "T", last_av)
            if hc >= 2:
                wt(v, "T", tok[f"x1mm{hc-2}"])
            w1r = w1raw_sb[slot].rearrange("p a b -> p (a b)")
            q1r = qh1_sb[slot].rearrange("p a b -> p (a b)")
            v.drain()
            v.tensor_scalar(ascr_sb[:], w1r, vs(V_T1), None, ALU.is_ge)
            v.drain()
            i = v.scalar_tensor_tensor(q1r, w1r, vs(V_NT1), ascr_sb[:], ALU.is_le,
                                       ALU.subtract)
            tok[f"qz1_{hc}"] = inc(i, "V")
        blk.vector(_)

    def d1_qz2(hc):
        slot = hc % 2

        def _(v):
            wt(v, f"Lw2{slot}", tok[f"w2d{hc}"])
            if hc == 0:
                wt(v, "V", tok["gfinw2"])
            if hc < 2:
                wt(v, "T", tok[f"womm{DC-1}"])
            else:
                wt(v, "T", tok[f"x2mm{hc-2}"])
            w2r = w2raw_sb[slot].rearrange("p a b -> p (a b)")
            q2r = qh2_sb[slot].rearrange("p a b -> p (a b)")
            v.drain()
            v.tensor_scalar(ascr2_sb[:], w2r, vs(V_T2), None, ALU.is_ge)
            v.drain()
            i = v.scalar_tensor_tensor(q2r, w2r, vs(V_NT2), ascr2_sb[:], ALU.is_le,
                                       ALU.subtract)
            tok[f"qz2_{hc}"] = inc(i, "V")
        blk.vector(_)

    c_front(0)
    for doc in range(DC):
        if doc + 1 < DC:
            c_front(doc + 1)
        if doc >= 2:
            c_stats(doc - 2)
        c_body(doc)
        if doc == 8:
            d1_dma(0)
        if doc == 9:
            d1_dma(1)
        if doc == 12:
            emit_gamma_back()
        if doc == 14:
            emit_gamma_finalize("w1", NW12, V_T1, V_NT1)
            emit_gamma_finalize("w2", NW12, V_T2, V_NT2)
            emit_gamma_finalize("w3", NW3, V_T3, V_NT3)
            d1_qz1(0)
            d1_qz1(1)
    c_stats(DC - 2)
    c_stats(DC - 1)

    esC.close()
    emit_ln_finalize(pstat2_m, pstat2_q, TOWN, "ln2")
    emit_ln_normalize(pstat2_q, x_sb, xn_sb, TOWN, "ln2", wide=True,
                      between=lambda b2: d1_qz2(b2))
    esC2.close()

    def _(sync):
        wt(sync, "V", tok[f"gv{len(gsched)-1}"])
        i = sync.dma_start(out=w3pre_sb.rearrange("p a b -> p (a b)"), in_=w3_d[0])
        tok["w3pre"] = inc(i, "Lw30", 16)
    blk.sync(_)

    esD1 = ExitStack()
    px1 = [esD1.enter_context(nc.psum_tensor(f"px1{i}", [128, 1024], F32)) for i in range(2)]
    px2 = [esD1.enter_context(nc.psum_tensor(f"px2{i}", [128, 1024], F32)) for i in range(2)]

    def d1_body(hc):
        slot = hc % 2

        def _(t):
            wt(t, "V", tok[f"qz1_{hc}"])
            if hc >= 2:
                wt(t, "S", tok[f"sig{hc-2}"], future=True)
            for dc in range(DC):
                lhs = qh1_sb[slot][:, dc, :]
                t.matmul(px1[slot][:, 0:512], lhs, xn_sb[:, dc, 0:512],
                         start=(dc == 0), stop=(dc == DC - 1))
                i = t.matmul(px1[slot][:, 512:1024], lhs, xn_sb[:, dc, 512:1024],
                             start=(dc == 0), stop=(dc == DC - 1))
            tok[f"x1mm{hc}"] = inc(i, "T")
        blk.tensor(_)

        def _(t):
            wt(t, "V", tok[f"qz2_{hc}"])
            if hc >= 2:
                wt(t, "V", tok[f"hid{hc-2}"])
            for dc in range(DC):
                lhs = qh2_sb[slot][:, dc, :]
                t.matmul(px2[slot][:, 0:512], lhs, xn_sb[:, dc, 0:512],
                         start=(dc == 0), stop=(dc == DC - 1))
                i = t.matmul(px2[slot][:, 512:1024], lhs, xn_sb[:, dc, 512:1024],
                             start=(dc == 0), stop=(dc == DC - 1))
            tok[f"x2mm{hc}"] = inc(i, "T")
        blk.tensor(_)

        waits = [("T", tok[f"x1mm{hc}"])]
        if hc >= 2:
            waits.append(("V", tok[f"hid{hc-2}"]))
        collect_sig(hc, waits)

        def _(v):
            wt(v, "T", tok[f"x2mm{hc}"])
            wt(v, "S", tok[f"sig{hc}"], future=True)
            if hc == 0:
                wt(v, "Lsp", tok[f"spill{DC-1}"])
            i = v.scalar_tensor_tensor(hid_sb[:, hc, :], px2[slot][:], -1.0,
                                       sig_sb[slot][:], ALU.mult, ALU.mult)
            tok[f"hid{hc}"] = inc(i, "V")
        blk.vector(_)

    def _(t):
        wt(t, "V", tok["qz1_0"])
        wt(t, "V", tok["ln2nrm0"])
        for dc in range(DC):
            i = t.matmul(px1[0][:, 0:512], qh1_sb[0][:, dc, :],
                         xn_sb[:, dc, 0:512], start=(dc == 0), stop=(dc == DC - 1))
        tok["x1mm0a"] = inc(i, "T")
    blk.tensor(_)

    def _(t):
        wt(t, "V", tok["qz1_1"])
        for dc in range(DC):
            i = t.matmul(px1[1][:, 0:512], qh1_sb[1][:, dc, :],
                         xn_sb[:, dc, 0:512], start=(dc == 0), stop=(dc == DC - 1))
        tok["x1mm1a"] = inc(i, "T")
    blk.tensor(_)

    def _(t):
        wt(t, "V", tok["ln2nrm1"])
        for dc in range(DC):
            i = t.matmul(px1[0][:, 512:1024], qh1_sb[0][:, dc, :],
                         xn_sb[:, dc, 512:1024], start=(dc == 0), stop=(dc == DC - 1))
        tok["x1mm0"] = inc(i, "T")
    blk.tensor(_)

    def _(t):
        for dc in range(DC):
            i = t.matmul(px1[1][:, 512:1024], qh1_sb[1][:, dc, :],
                         xn_sb[:, dc, 512:1024], start=(dc == 0), stop=(dc == DC - 1))
        tok["x1mm1"] = inc(i, "T")
    blk.tensor(_)

    sig_ops = []
    SIG_PLAN = [CNT["S"]]

    def collect_sig(hc, waits):
        SIG_PLAN[0] += 1
        tok[f"sig{hc}"] = SIG_PLAN[0]
        slot = hc % 2

        def op(sc, hc=hc, slot=slot, waits=tuple(waits)):
            for semn, tv in waits:
                wt(sc, semn, tv, future=True)
            i = sc.activation(sig_sb[slot][:], px1[slot][:], AF.Sigmoid, scale=-1.0)
            inc(i, "S")
        sig_ops.append(op)

    collect_sig(0, [("T", tok["x1mm0"])])

    def _(t):
        wt(t, "V", tok["qz2_0"])
        for dc in range(DC):
            lhs = qh2_sb[0][:, dc, :]
            t.matmul(px2[0][:, 0:512], lhs, xn_sb[:, dc, 0:512],
                     start=(dc == 0), stop=(dc == DC - 1))
            i = t.matmul(px2[0][:, 512:1024], lhs, xn_sb[:, dc, 512:1024],
                         start=(dc == 0), stop=(dc == DC - 1))
        tok["x2mm0"] = inc(i, "T")
    blk.tensor(_)

    collect_sig(1, [("T", tok["x1mm1"])])

    def _(t):
        wt(t, "V", tok["qz2_1"])
        for dc in range(DC):
            lhs = qh2_sb[1][:, dc, :]
            t.matmul(px2[1][:, 0:512], lhs, xn_sb[:, dc, 0:512],
                     start=(dc == 0), stop=(dc == DC - 1))
            i = t.matmul(px2[1][:, 512:1024], lhs, xn_sb[:, dc, 512:1024],
                         start=(dc == 0), stop=(dc == DC - 1))
        tok["x2mm1"] = inc(i, "T")
    blk.tensor(_)

    for hc in range(2):
        def _(v, hc=hc):
            wt(v, "T", tok[f"x2mm{hc}"])
            wt(v, "S", tok[f"sig{hc}"], future=True)
            if hc == 0:
                wt(v, "Lsp", tok[f"spill{DC-1}"])
            i = v.scalar_tensor_tensor(hid_sb[:, hc, :], px2[hc][:], -1.0,
                                       sig_sb[hc][:], ALU.mult, ALU.mult)
            tok[f"hid{hc}"] = inc(i, "V")
        blk.vector(_)

    d1_dma(2)
    d1_dma(3)
    d1_qz1(2)
    d1_qz2(2)
    d1_qz1(3)
    d1_qz2(3)
    for hc in range(2, FHC):
        if hc + 2 < FHC:
            d1_dma(hc + 2)
        d1_body(hc)
        if hc + 2 < FHC:
            d1_qz1(hc + 2)
            d1_qz2(hc + 2)

    def _(sc):
        for op in sig_ops:
            op(sc)
    blk.scalar(_)
    assert CNT["S"] == SIG_PLAN[0], (CNT["S"], SIG_PLAN[0])
    esD1.close()


    esD2 = ExitStack()
    pout = [esD2.enter_context(nc.psum_tensor(f"pout{i}", [128, 1024], F32)) for i in range(2)]
    pending_stores = []

    def d2_front(doc):
        slot = doc % 2
        pstore = pending_stores.pop(0) if pending_stores else None

        def _(sync):
            if doc < 2:
                wt(sync, "T", tok[f"x2mm{FHC-1}"])
                wt(sync, "V", tok[f"qz1_{FHC-1}"])
            else:
                wt(sync, "V", tok[f"qz3b_{doc-2}"])
            if doc != 0:
                i = sync.dma_start(out=w3raw_sb[slot].rearrange("p a b -> p (a b)"),
                                   in_=w3_d[doc])
                tok[f"w3d{doc}"] = inc(i, f"Lw3{slot}", 16)
            if pstore is not None:
                dprev, sprev = pstore
                wt(sync, "V", tok[f"fin{dprev}"])
                i = sync.dma_start(out=y_d[:, dprev, :], in_=outst_sb[sprev][:])
                tok[f"store{dprev}"] = inc(i, f"Lst{sprev}", 16)
        blk.sync(_)

        def _(sc):
            if doc < 2:
                wt(sc, "T", tok[f"x2mm{FHC-1}"])
            if doc >= 2:
                wt(sc, "V", tok[f"fin{doc-2}"])
            i = sc.dma_start(out=xrld_sb[slot][:], in_=xsp_d[:, doc, :])
            tok[f"xr{doc}"] = inc(i, f"Lxr{slot}", 16)
        blk.scalar(_)

        HB = FHC // 2 * 128
        w3src = w3pre_sb if doc == 0 else w3raw_sb[slot]
        w3r = w3src.rearrange("p a b -> p (a b)")
        q3r = qh3_sb[slot].rearrange("p a b -> p (a b)")

        def _(v):
            if doc == 0:
                wt(v, "Lw30", tok["w3pre"])
                wt(v, "V", tok["gfinw3"])
            else:
                wt(v, f"Lw3{slot}", tok[f"w3d{doc}"])
            if doc < 2:
                wt(v, "T", tok[f"x2mm{FHC-1}"])
            else:
                wt(v, "T", tok[f"omma_{doc-2}"])
            v.drain()
            v.tensor_scalar(a3scr_sb[:, 0:HB], w3r[:, 0:HB], vs(V_T3), None, ALU.is_ge)
            v.drain()
            i = v.scalar_tensor_tensor(q3r[:, 0:HB], w3r[:, 0:HB], vs(V_NT3),
                                       a3scr_sb[:, 0:HB], ALU.is_le, ALU.subtract)
            tok[f"qz3a_{doc}"] = inc(i, "V")
        blk.vector(_)

        def _(v):
            if doc >= 2:
                wt(v, "T", tok[f"omm{doc-2}"])
            v.drain()
            v.tensor_scalar(a3scr_sb[:, HB:], w3r[:, HB:], vs(V_T3), None, ALU.is_ge)
            v.drain()
            i = v.scalar_tensor_tensor(q3r[:, HB:], w3r[:, HB:], vs(V_NT3),
                                       a3scr_sb[:, HB:], ALU.is_le, ALU.subtract)
            tok[f"qz3b_{doc}"] = inc(i, "V")
        blk.vector(_)

    def d2_body(doc):
        slot = doc % 2

        def _(t):
            wt(t, "V", tok[f"qz3a_{doc}"])
            if doc == 0:
                wt(t, "V", tok[f"hid{FHC-1}"])
            if doc >= 2:
                wt(t, "V", tok[f"fin{doc-2}"])
            for hc in range(FHC // 2):
                lhs = qh3_sb[slot][:, hc, :]
                t.matmul(pout[slot][:, 0:512], lhs, hid_sb[:, hc, 0:512],
                         start=(hc == 0), stop=False)
                i = t.matmul(pout[slot][:, 512:1024], lhs, hid_sb[:, hc, 512:1024],
                             start=(hc == 0), stop=False)
            tok[f"omma_{doc}"] = inc(i, "T")
        blk.tensor(_)

        def _(t):
            wt(t, "V", tok[f"qz3b_{doc}"])
            for hc in range(FHC // 2, FHC):
                lhs = qh3_sb[slot][:, hc, :]
                t.matmul(pout[slot][:, 0:512], lhs, hid_sb[:, hc, 0:512],
                         start=False, stop=(hc == FHC - 1))
                i = t.matmul(pout[slot][:, 512:1024], lhs, hid_sb[:, hc, 512:1024],
                             start=False, stop=(hc == FHC - 1))
            tok[f"omm{doc}"] = inc(i, "T")
        blk.tensor(_)

        def _(v):
            wt(v, "T", tok[f"omm{doc}"])
            wt(v, f"Lxr{slot}", tok[f"xr{doc}"])
            if doc >= 2:
                wt(v, f"Lst{slot}", tok[f"store{doc-2}"])
            i = v.tensor_tensor(outst_sb[slot][:], xrld_sb[slot][:], pout[slot][:],
                                ALU.subtract)
            tok[f"fin{doc}"] = inc(i, "V")
        blk.vector(_)
        pending_stores.append((doc, slot))

    d2_front(0)
    for doc in range(DC):
        if doc + 1 < DC:
            d2_front(doc + 1)
        d2_body(doc)

    def _(sync):
        for dprev, sprev in pending_stores:
            wt(sync, "V", tok[f"fin{dprev}"])
            i = sync.dma_start(out=y_d[:, dprev, :], in_=outst_sb[sprev][:])
            tok[f"store{dprev}"] = inc(i, f"Lst{sprev}", 16)
        sync.wait_ge(SEMS["Lst0"], CNT["Lst0"])
        sync.wait_ge(SEMS["Lst1"], CNT["Lst1"])
    blk.sync(_)
    esD2.close()

    assert gstate["pos"] == len(gsched), gstate["pos"]
    es_top.close()
    return nc



def _stage_inputs(x, Wq, Wk, Wv, Wo, w1, w2, w3):
    x = np.ascontiguousarray(np.asarray(x, dtype=np.float32))
    xf = np.ascontiguousarray(x.reshape(8192, D).T)

    wq_t = np.ascontiguousarray(
        np.asarray(Wq, np.float32).reshape(DC, 128, H, 128).transpose(2, 1, 0, 3).reshape(H, 128, D))
    wk_t = np.ascontiguousarray(
        np.asarray(Wk, np.float32).reshape(DC, 128, 4, 128).transpose(2, 1, 0, 3).reshape(4, 128, D))
    wv_t = np.ascontiguousarray(np.asarray(Wv, np.float32).reshape(DC, 128, 512))
    wo_t = np.ascontiguousarray(
        np.asarray(Wo, np.float32).reshape(DC, 128, DC, 128).transpose(2, 1, 0, 3).reshape(DC, 128, D))
    w1_t = np.ascontiguousarray(
        np.asarray(w1, np.float32).reshape(FHC, 128, DC, 128).transpose(0, 3, 2, 1).reshape(FHC, 128, D))
    w2_t = np.ascontiguousarray(
        np.asarray(w2, np.float32).reshape(FHC, 128, DC, 128).transpose(0, 3, 2, 1).reshape(FHC, 128, D))
    w3_t = np.ascontiguousarray(
        np.asarray(w3, np.float32).reshape(DC, 128, FHC, 128).transpose(0, 3, 2, 1).reshape(DC, 128, FH))

    r = np.arange(128)[:, None]
    c = np.arange(128)[None, :]
    msk = np.zeros((128, 2, 128), np.float32)
    msk[:, 0, :] = np.where(c > r, 0.0, NEG)
    msk[:, 1, :] = np.where(c <= r, 0.0, NEG)
    msk = msk.astype(BF)
    idn = np.eye(128, dtype=BF)

    w12v = np.concatenate([
        w1_t.reshape(FHC, 128, 2, 1024).transpose(0, 2, 1, 3).reshape(2 * FHC, 128, 1024),
        w2_t.reshape(FHC, 128, 2, 1024).transpose(0, 2, 1, 3).reshape(2 * FHC, 128, 1024)])
    w3v = w3_t.reshape(DC, 128, 4, 1408).transpose(0, 2, 1, 3).reshape(4 * DC, 128, 1408)

    hm0 = np.zeros((128, 4, 512), np.float32)
    for j in range(4):
        kk = np.arange(512)[None, :]
        hm0[:, j, :] = np.where(kk < 512 - 128 * j, NEG, 0.0)
    hm0 = hm0.astype(BF)
    hmz = np.zeros((128, 4, 512), BF)

    in_maps = []
    for core in range(8):
        t0 = core * TOWN
        x_own = np.ascontiguousarray(
            xf[:, t0:t0 + TOWN].reshape(DC, 128, TOWN).transpose(1, 0, 2))
        if core % 4 == 0:
            halo = np.zeros((D, THALO), np.float32)
        else:
            halo = xf[:, t0 - THALO:t0]
        x_halo = np.ascontiguousarray(
            halo.reshape(DC, 128, 4, 128).transpose(2, 1, 0, 3).reshape(4, 128, D))
        gsh12 = np.ascontiguousarray(np.concatenate(
            [w12v[0:88][core::8], w12v[88:176][core::8]]))
        gsh3 = np.ascontiguousarray(w3v[core::8])
        in_maps.append(dict(
            x_own=x_own, x_halo=x_halo,
            wq=wq_t, wk=wk_t, wv=wv_t, wo=wo_t,
            w1=w1_t, w2=w2_t, w3=w3_t,
            gsh12=gsh12, gsh3=gsh3,
            msk=msk, hmsk=(hm0 if core % 4 == 0 else hmz), idn=idn,
        ))
    return in_maps


_PROG_CACHE = {}


def kernel(L0_attn_x, L0_attn_Wq, L0_attn_Wk, L0_attn_Wv, L0_attn_Wo,
           L0_ffn_w1, L0_ffn_w2, L0_ffn_w3):
    in_maps = _stage_inputs(L0_attn_x, L0_attn_Wq, L0_attn_Wk, L0_attn_Wv,
                            L0_attn_Wo, L0_ffn_w1, L0_ffn_w2, L0_ffn_w3)
    if "nc" not in _PROG_CACHE:
        _PROG_CACHE["nc"] = build_program()
    nc = _PROG_CACHE["nc"]
    res = run_bass_kernel_spmd(nc, in_maps, core_ids=list(range(8)))

    out = np.empty((D, 8192), np.float32)
    for core in range(8):
        shard = np.asarray(res.results[core]["y"])
        out[:, core * TOWN:(core + 1) * TOWN] = shard.transpose(1, 0, 2).reshape(D, TOWN)
    return np.ascontiguousarray(out.T.reshape(2, 4096, D))


# version 22
# speedup vs baseline: 1.0110x; 1.0110x over previous
import numpy as np
import ml_dtypes
import concourse.bass as bass
import concourse.mybir as mybir
from concourse.bass_utils import run_bass_kernel_spmd
from concourse.alu_op_type import AluOpType as ALU

from contextlib import ExitStack

F32 = mybir.dt.float32
BF16 = mybir.dt.bfloat16
AF = mybir.ActivationFunctionType
AXX = mybir.AxisListType.X
BF = ml_dtypes.bfloat16

D = 2048
DC = 16
H = 16
HD = 128
FH = 5632
FHC = 44
TOWN = 1024
THALO = 512
TTOT = 1536
NBLK = 2
SCALE = 1.0 / np.sqrt(128.0)
NEG = -1e30
LN_EPS = 1e-5
FFN_EPS = 1e-8
NW12 = FH * D
NW3 = D * FH

MBASE = 16512
SBUF_LIMIT = 212832


def build_program():
    nc = bass.Bass("TRN2", target_bir_lowering=False)

    def din(name, shape, dt=F32):
        return nc.dram_tensor(name, shape, dt, kind="ExternalInput").ap()

    x_own_d = din("x_own", [128, DC, TOWN])
    x_halo_d = din("x_halo", [4, 128, D])
    wq_d = din("wq", [H, 128, D])
    wk_d = din("wk", [4, 128, D])
    wv_d = din("wv", [DC, 128, 512])
    wo_d = din("wo", [DC, 128, D])
    w1_d = din("w1", [FHC, 128, D])
    w2_d = din("w2", [FHC, 128, D])
    w3_d = din("w3", [DC, 128, FH])
    msk_d = din("msk", [128, 2, 128], BF16)
    hmsk_d = din("hmsk", [128, 4, 512], BF16)
    idn_d = din("idn", [128, 128], BF16)

    y_d = nc.dram_tensor("y", [128, DC, TOWN], F32, kind="ExternalOutput").ap()
    xsp_d = nc.dram_tensor("xsp", [128, DC, TOWN], F32, kind="Internal").ap()

    gsh12_d = din("gsh12", [22, 128, 1024])
    gsh3_d = din("gsh3", [8, 128, 1408])
    gbin_d = nc.dram_tensor("gbin", [1, 4], F32, kind="Internal").ap()
    gbout_d = nc.dram_tensor("gbout", [1, 4], F32, kind="Internal").ap()

    def sb(name, shape, dt, off):
        elts = 1
        for s in shape[1:]:
            elts *= s
        nbytes = elts * (4 if dt == F32 else 2)
        assert off + nbytes <= SBUF_LIMIT, (name, off, nbytes, off + nbytes)
        return nc.alloc_sbuf_tensor_at(name, shape, dt, offset=MBASE + off).ap()

    x_sb = sb("x", [128, DC, TOWN], F32, 0)
    hid_sb = sb("hid", [128, FHC, TOWN], BF16, 0)
    haloraw_sb = [sb(f"hraw{i}", [128, DC, 128], F32, 65536 + i * 8192) for i in range(2)]
    xnh_sb = sb("xnh", [128, DC, THALO], BF16, 81920)
    wqbf_sb = [sb(f"wqbf{i}", [128, DC, 128], BF16, 188416 + i * 4096) for i in range(2)]
    o_sb = sb("o", [128, DC, TOWN], BF16, 65536)
    outst_sb = [sb(f"outst{i}", [128, TOWN], F32, 90112 + i * 4096) for i in range(2)]
    xn_sb = sb("xn", [128, DC, TOWN], BF16, 98304)
    at_sb2 = [sb(f"at{i}", [128, 8, 512], BF16, 131072 + i * 8192) for i in range(2)]
    pexp_sb = sb("pexp", [128, 2, 640], F32, 147456)
    abf_sb = sb("abf", [128, 4, 640], BF16, 152576)
    hmsk_sb = sb("hmsk", [128, 4, 512], BF16, 157696)
    xrld_sb = [sb(f"xrld{i}", [128, TOWN], F32, 98304 + i * 4096) for i in range(2)]
    xbscr2_sb = [sb(f"xbscr2_{i}", [128, TOWN], BF16, 106496 + i * 2048) for i in range(2)]
    xsqscr2_sb = [sb(f"xsqscr2_{i}", [128, TOWN], BF16, 110592 + i * 2048) for i in range(2)]
    hxb_sb = [sb(f"hxb{i}", [128, D], BF16, (131072, 147456)[i]) for i in range(2)]
    hxq_sb = [sb(f"hxq{i}", [128, D], BF16, (135168, 151552)[i]) for i in range(2)]
    wvbf_sb = sb("wvbf", [128, DC, 512], BF16, 131072)
    wkbf_sb = [sb(f"wkbf{i}", [128, DC, 128], BF16, 147456 + i * 4096) for i in range(2)]
    qh_sb = [sb(f"qh_{i}", [128, 512], BF16, 161792 + i * 1024) for i in range(2)]
    w1raw_sb = [sb(f"w1raw{i}", [128, DC, 128], F32, 131072 + i * 8192) for i in range(2)]
    qh1_sb = [sb(f"qh1{i}", [128, DC, 128], BF16, 147456 + i * 4096) for i in range(2)]
    ascr_sb = sb("ascr", [128, D], BF16, 155648)
    ascr2_sb = sb("ascr2", [128, D], BF16, 90112)
    sig_sb = [sb(f"sig{i}", [128, TOWN], BF16, 159744 + i * 2048) for i in range(2)]
    k_sb = sb("k", [128, 4, TTOT], BF16, 163840)
    wobf_sb = [sb(f"wobf{i}", [128, DC, 128], BF16, 180224 + i * 4096) for i in range(2)]
    v_sb = sb("v", [128, 12, 512], BF16, 176128)
    w2raw_sb = [sb(f"w2raw{i}", [128, DC, 128], F32, 163840 + i * 8192) for i in range(2)]
    qh2_sb = [sb(f"qh2{i}", [128, DC, 128], BF16, 180224 + i * 4096) for i in range(2)]
    gsh_sb = [sb(f"gsh{i}", [128, 1408], F32, (196608, 205312)[i]) for i in range(2)]
    w3raw_sb = [sb(f"w3raw{i}", [128, FHC, 128], F32, 106496 + i * 22528) for i in range(2)]
    qh3_sb = [sb(f"qh3{i}", [128, FHC, 128], BF16, 151552 + i * 11264) for i in range(2)]
    a3scr_sb = sb("a3scr", [128, FH], BF16, 174080)
    msk_sb = sb("msk", [128, 2, 128], BF16, 204800)
    mrow_sb = sb("mrow", [128, TOWN], F32, 155648)
    sqrow_sb = sb("sqrow", [128, TOWN], F32, 159744)
    idn_sb = sb("idn", [128, 128], BF16, 211488)
    ones_sb = sb("ones", [128, 1], BF16, 211744)
    onesf_sb = sb("onesf", [128, 1], F32, 211776)
    onesrow_sb = sb("onesrow", [1, 128], F32, 211808)
    vecs_sb = sb("vecs", [128, 24], F32, 212320)
    gsum_sb = sb("gsum", [1, 4], F32, 212416)
    gsuma_sb = sb("gsuma", [1, 4], F32, 212448)
    ones2d_sb = sb("ones2d", [128, 128], BF16, 212480)
    gthr_sb = sb("gthr", [128, 4], F32, 212736)
    w3pre_sb = sb("w3pre", [128, FHC, 128], F32, 188416)

    def vs(i):
        return vecs_sb[:, i:i + 1]
    V_MAX, V_NEGMAX, V_SUMEXP, V_RECIP = 0, 2, 4, 6
    V_T1, V_NT1, V_T2, V_NT2, V_T3, V_NT3 = 8, 9, 10, 11, 12, 13
    V_GACC = {"w1": 14, "w2": 15, "w3": 16}
    V_GTMP = 17
    V_EPS = 18
    V_TG = 19

    es_top = ExitStack()
    SEMS = {n: es_top.enter_context(nc.semaphore(n))
            for n in ("T", "V", "S", "Lx0", "Lx1", "Lx2", "Lx3", "Lmsk", "Lidn", "Lhmsk",
                      "Lh0", "Lh1", "Lwv0", "Lwv1", "Lkq0", "Lkq1",
                      "Lwo0", "Lwo1", "Lw10", "Lw11", "Lw20", "Lw21",
                      "Lw30", "Lw31", "Lg0", "Lg1", "Lsp",
                      "Lxr0", "Lxr1", "Lst0", "Lst1", "G")}
    CNT = {k: 0 for k in SEMS}

    def inc(instr, name, k=1):
        instr.then_inc(SEMS[name], k)
        CNT[name] += k
        return CNT[name]

    def wt(eng, name, tokv, future=False):
        assert future or tokv <= CNT[name], (name, tokv, CNT[name])
        if tokv > 0:
            eng.wait_ge(SEMS[name], tokv)

    blk = es_top.enter_context(nc.Block())
    tok = {}

    def bcast(ap, n):
        return bass.AP(tensor=ap.tensor, offset=ap.offset,
                       ap=[list(ap.ap[0]), [0, n], list(ap.ap[1])])

    def _(sync):
        for s in range(2):
            i = sync.dma_start(out=haloraw_sb[s].rearrange("p a b -> p (a b)"),
                               in_=x_halo_d[s])
            tok[f"hraw{s}"] = inc(i, f"Lh{s}", 16)
        for g in range(8):
            xsem = ("Lx0", "Lx1", "Lx2", "Lx3", "Lw10", "Lw11", "Lw20", "Lw21")[g]
            i = sync.dma_start(out=x_sb[:, 2 * g:2 * (g + 1), :], in_=x_own_d[:, 2 * g:2 * (g + 1), :])
            tok[f"xg{g}"] = inc(i, xsem, 16)
        i = sync.dma_start(out=msk_sb[:], in_=msk_d)
        tok["mskd"] = inc(i, "Lmsk", 16)
        i = sync.dma_start(out=idn_sb[:], in_=idn_d)
        tok["idnd"] = inc(i, "Lidn", 16)
    blk.sync(_)

    def _(v):
        v.memset(ones_sb[:], 1.0)
        v.memset(onesf_sb[:], 1.0)
        v.memset(onesrow_sb[:], 1.0)
        v.memset(ones2d_sb[:], 1.0)
        v.memset(vecs_sb[:, 0:V_EPS], 0.0)
        v.memset(vecs_sb[:, V_EPS + 1:], 0.0)
        v.memset(vecs_sb[:, V_EPS:V_EPS + 1], LN_EPS)
        i = v.memset(gsum_sb[0:1, 3:4], 0.0)
        tok["vinit"] = inc(i, "V")
    blk.vector(_)


    def emit_ln_finalize(pst_m, pst_q, ntok, name, extra=()):
        def _(v):
            wt(v, "T", tok[f"{name}st{DC-1}"])
            for semn, tv in extra:
                wt(v, semn, tv)
            v.drain()
            v.tensor_scalar(mrow_sb[:, 0:ntok], pst_m[:], 1.0 / D, None, ALU.mult)
            v.drain()
            v.tensor_tensor(sqrow_sb[:, 0:ntok], mrow_sb[:, 0:ntok],
                            mrow_sb[:, 0:ntok], ALU.mult)
            v.drain()
            i = v.scalar_tensor_tensor(pst_q[:], pst_q[:], 1.0 / D,
                                       sqrow_sb[:, 0:ntok], ALU.mult, ALU.subtract)
            tok[f"{name}var"] = inc(i, "V")
        blk.vector(_)

    def emit_ln_normalize(pst_q, xsrc, xdst, ntok, name, wide=False, between=None):
        esn = ExitStack()
        pw = ntok if wide else 512
        pa = esn.enter_context(nc.psum_tensor(f"pa_{name}", [128, pw], F32))
        pb = esn.enter_context(nc.psum_tensor(f"pb_{name}", [128, pw], F32))
        if wide:
            def _(sc):
                wt(sc, "V", tok[f"{name}var"])
                i = None
                for b2 in range(ntok // 512):
                    sl = slice(b2 * 512, (b2 + 1) * 512)
                    i = sc.activation(pa[:, sl], pst_q[:, sl], AF.Sqrt,
                                      bias=vecs_sb[:, V_EPS:V_EPS + 1])
                for b2 in range(ntok // 512):
                    tok[f"{name}sd{b2}"] = CNT["S"] + 1
                inc(i, "S")
            blk.scalar(_)
        for b2 in range(ntok // 512):
            sl = slice(b2 * 512, (b2 + 1) * 512)
            pl = sl if wide else slice(0, 512)

            if not wide:
                def _(sc):
                    if b2 == 0:
                        wt(sc, "V", tok[f"{name}var"])
                    else:
                        wt(sc, "V", tok[f"{name}nrm{b2-1}"])
                    i = sc.activation(pa[:], pst_q[:, sl], AF.Sqrt,
                                      bias=vecs_sb[:, V_EPS:V_EPS + 1])
                    tok[f"{name}sd{b2}"] = inc(i, "S")
                blk.scalar(_)

            def _(v):
                wt(v, "S", tok[f"{name}sd{b2}"])
                v.reciprocal(pa[:, pl], pa[:, pl])
                v.drain()
                v.scalar_tensor_tensor(pb[:, pl], mrow_sb[:, sl], -1.0, pa[:, pl],
                                       ALU.mult, ALU.mult)
                v.drain()
                dst = xdst[:, :, sl]
                v.tensor_tensor(dst, xsrc[:, :, sl], bcast(pa[:, pl], DC), ALU.mult)
                v.drain()
                i = v.tensor_tensor(dst, dst, bcast(pb[:, pl], DC), ALU.add)
                v.drain()
                tok[f"{name}nrm{b2}"] = inc(i, "V")
            blk.vector(_)
            if between is not None:
                between(b2)
        esn.close()

    esKVH = ExitStack()
    pkH = [esKVH.enter_context(nc.psum_tensor(f"pkH{i}", [128, 512], F32)) for i in range(2)]
    esA2 = ExitStack()
    pstat_m = esA2.enter_context(nc.psum_tensor("pstat_m", [128, TOWN], F32))
    pstat_q = esA2.enter_context(nc.psum_tensor("pstat_q", [128, TOWN], F32))
    esA1 = ExitStack()
    pstH_m = esA1.enter_context(nc.psum_tensor("pstH_m", [128, 128], F32))
    pstH_q = esA1.enter_context(nc.psum_tensor("pstH_q", [128, 128], F32))

    def h_front(s):
        slot = s % 2
        if s >= 2:
            def _(sync):
                wt(sync, "V", tok[f"hnorm{s-2}"])
                i = sync.dma_start(out=haloraw_sb[slot].rearrange("p a b -> p (a b)"),
                                   in_=x_halo_d[s])
                tok[f"hraw{s}"] = inc(i, f"Lh{slot}", 16)
            blk.sync(_)

        def _(v):
            wt(v, f"Lh{slot}", tok[f"hraw{s}"])
            if s >= 2:
                wt(v, "T", tok[f"hstat{s-2}"])
            raw = haloraw_sb[slot].rearrange("p a b -> p (a b)")
            v.tensor_copy(hxb_sb[slot][:], raw)
            i = v.tensor_tensor(hxq_sb[slot][:], raw, raw, ALU.mult)
            tok[f"hxbq{s}"] = inc(i, "V")
        blk.vector(_)

    def h_stats(s):
        slot = s % 2

        def _(t):
            wt(t, "V", tok[f"hxbq{s}"])
            if s >= 1:
                wt(t, "V", tok[f"hnorm{s-1}"])
            for dc in range(DC):
                t.matmul(pstH_m[:], ones2d_sb[:], hxb_sb[slot][:, dc * 128:(dc + 1) * 128],
                         start=(dc == 0), stop=(dc == DC - 1))
            for dc in range(DC):
                i = t.matmul(pstH_q[:], ones2d_sb[:], hxq_sb[slot][:, dc * 128:(dc + 1) * 128],
                             start=(dc == 0), stop=(dc == DC - 1))
            tok[f"hstat{s}"] = inc(i, "T")
        blk.tensor(_)

    def h_fin(s):
        def _(v):
            wt(v, "T", tok[f"hstat{s}"])
            i = v.tensor_scalar(mrow_sb[:, 0:128], pstH_m[:], 1.0 / D, None, ALU.mult)
            tok[f"hm{s}"] = inc(i, "V")
        blk.vector(_)

        def _(sc):
            wt(sc, "V", tok[f"hm{s}"])
            i = sc.activation(sqrow_sb[:, 0:128], mrow_sb[:, 0:128], AF.Square)
            tok[f"hsq{s}"] = inc(i, "S")
        blk.scalar(_)

        def _(v):
            wt(v, "S", tok[f"hsq{s}"])
            i = v.scalar_tensor_tensor(pstH_q[:], pstH_q[:], 1.0 / D,
                                       sqrow_sb[:, 0:128], ALU.mult, ALU.subtract)
            tok[f"hvar{s}"] = inc(i, "V")
        blk.vector(_)

        def _(sc):
            wt(sc, "V", tok[f"hvar{s}"])
            i = sc.activation(pstH_m[:], pstH_q[:], AF.Sqrt,
                              bias=vecs_sb[:, V_EPS:V_EPS + 1])
            tok[f"hsd{s}"] = inc(i, "S")
        blk.scalar(_)

    def h_norm(s):
        slot = s % 2

        def _(v):
            wt(v, "S", tok[f"hsd{s}"])
            v.reciprocal(pstH_m[:], pstH_m[:])
            v.drain()
            v.scalar_tensor_tensor(pstH_q[:], mrow_sb[:, 0:128], -1.0,
                                   pstH_m[:], ALU.mult, ALU.mult)
            v.drain()
            dst = xnh_sb[:, :, s * 128:(s + 1) * 128]
            v.tensor_tensor(dst, haloraw_sb[slot][:], bcast(pstH_m[:], DC), ALU.mult)
            v.drain()
            i = v.tensor_tensor(dst, dst, bcast(pstH_q[:], DC), ALU.add)
            v.drain()
            tok[f"hnorm{s}"] = inc(i, "V")
        blk.vector(_)

    def a2_cast(dc):
        slot = dc % 2

        def _(v):
            xsem = ("Lx0", "Lx1", "Lx2", "Lx3", "Lw10", "Lw11", "Lw20", "Lw21")[dc // 2]
            wt(v, xsem, tok[f"xg{dc // 2}"])
            if dc >= 2:
                wt(v, "T", tok[f"ln1st{dc-2}"])
            v.tensor_copy(xbscr2_sb[slot][:], x_sb[:, dc, :])
            i = v.tensor_tensor(xsqscr2_sb[slot][:], x_sb[:, dc, :], x_sb[:, dc, :],
                                ALU.mult)
            tok[f"ln1xbq{dc}"] = inc(i, "V")
        blk.vector(_)

    def a2_stats(dc):
        slot = dc % 2

        def _(t):
            wt(t, "V", tok[f"ln1xbq{dc}"])
            for b2 in range(2):
                sl = slice(b2 * 512, (b2 + 1) * 512)
                t.matmul(pstat_m[:, sl], ones2d_sb[:], xbscr2_sb[slot][:, sl],
                         start=(dc == 0), stop=(dc == DC - 1))
                i = t.matmul(pstat_q[:, sl], ones2d_sb[:], xsqscr2_sb[slot][:, sl],
                             start=(dc == 0), stop=(dc == DC - 1))
            tok[f"ln1st{dc}"] = inc(i, "T")
        blk.tensor(_)

    h_front(0); h_front(1)
    halo_stages = []
    for s in range(4):
        halo_stages.append(lambda s=s: h_stats(s))
        halo_stages.append(lambda s=s: h_fin(s))
        if s + 2 < 4:
            halo_stages.append(lambda s=s: (h_norm(s), h_front(s + 2)))
        else:
            halo_stages.append(lambda s=s: h_norm(s))
    for dc in range(DC):
        a2_cast(dc)
        a2_stats(dc)
        if halo_stages:
            halo_stages.pop(0)()
    while halo_stages:
        halo_stages.pop(0)()
    esA1.close()
    emit_ln_finalize(pstat_m, pstat_q, TOWN, "ln1", extra=[("S", tok["hsd3"])])
    emit_ln_normalize(pstat_q, x_sb, xn_sb, TOWN, "ln1")
    esA2.close()

    gsched = ([("w1", i, 1024) for i in range(11)]
              + [("w2", i, 1024) for i in range(11)]
              + [("w3", i, 1408) for i in range(8)])
    gstate = {"pos": 0, "guards": [], "guardn": 0}

    def gamma_pump(n):
        for _n in range(n):
            gi = gstate["pos"]
            if gi >= len(gsched):
                return
            gstate["pos"] += 1
            which, ich, width = gsched[gi]
            slot = gi % 2
            src = gsh3_d[ich] if which == "w3" else gsh12_d[gi]

            def _(sync):
                wt(sync, "V", tok.get(f"gv{gi-2}", 0))
                i = sync.dma_start(out=gsh_sb[slot][:, 0:width], in_=src)
                tok[f"gd{gi}"] = inc(i, f"Lg{slot}", 16)
            blk.sync(_)

            def _(v):
                wt(v, f"Lg{slot}", tok[f"gd{gi}"])
                v.drain()
                v.tensor_reduce(vs(V_GTMP), gsh_sb[slot][:, 0:width], AXX, ALU.add,
                                apply_absolute_value=True)
                v.drain()
                acc = V_GACC[which]
                i = v.tensor_tensor(vs(acc), vs(acc), vs(V_GTMP), ALU.add)
                tok[f"gv{gi}"] = inc(i, "V")
            blk.vector(_)

    def emit_gamma_reduce_head():
        def _(sync):
            wt(sync, "V", tok[f"gv{len(gsched)-1}"])
            wt(sync, "T", last_av)
            i = sync.dma_start(out=mrow_sb[0:1, 0:384], in_=vecs_sb[:, V_GACC["w1"]:V_GACC["w1"] + 3])
            tok["grow"] = inc(i, "Lmsk", 16)
        blk.sync(_)

        def _(v):
            wt(v, "Lmsk", tok["grow"])
            i = None
            for wi in range(3):
                src = bass.AP(tensor=mrow_sb.tensor, offset=mrow_sb.offset + wi,
                              ap=[[mrow_sb.ap[0][0], 1], [3, 128]])
                i = v.tensor_reduce(gsum_sb[0:1, wi:wi + 1], src, AXX, ALU.add)
            tok["gsum"] = inc(i, "V")
        blk.vector(_)

        def _(sync):
            wt(sync, "V", tok["gsum"])
            i = sync.dma_start(out=gbin_d, in_=gsum_sb[0:1, :])
            tok["gbin"] = inc(i, "Lmsk", 16)
        blk.sync(_)

    def emit_gamma_allreduce():
        def _(g):
            wt(g, "Lmsk", tok["gbin"])
            i = g.collective_compute(
                "AllReduce", ALU.add,
                replica_groups=[list(range(8))],
                ins=[gbin_d.opt()], outs=[gbout_d.opt()])
            tok["gcc"] = inc(i, "G")
        blk.gpsimd(_)

    def emit_gamma_back():
        def _(sync):
            wt(sync, "G", tok["gcc"])
            bc = bass.AP(tensor=gbout_d.tensor, offset=gbout_d.offset,
                         ap=[[0, 128], [1, 4]])
            i = sync.dma_start(out=gthr_sb[:], in_=bc)
            tok["gback"] = inc(i, "Lmsk", 16)
        blk.sync(_)

    def emit_gamma_finalize(which, nelem, tslot, ntslot):
        wi = {"w1": 0, "w2": 1, "w3": 2}[which]

        def _(v):
            wt(v, "Lmsk", tok["gback"])
            v.tensor_scalar(vs(tslot), gthr_sb[:, wi:wi + 1],
                            0.5 / nelem, 0.5 * FFN_EPS, ALU.mult, ALU.add)
            v.drain()
            i = v.tensor_scalar(vs(ntslot), vs(tslot), -1.0, None, ALU.mult)
            tok[f"gfin{which}"] = inc(i, "V")
        blk.vector(_)

    def emit_wk_dma(ofc):
        slot = ofc % 2

        def _(g):
            if ofc == 0:
                wt(g, "T", tok["hstat3"])
            if ofc >= 2:
                wt(g, "T", tok[f"kmmO2_{ofc-2}"])
            i = g.dma_start(out=wkbf_sb[slot].rearrange("p a b -> p (a b)"), in_=wk_d[ofc])
            tok[f"wkd{ofc}"] = inc(i, f"Lkq{slot}", 16)
        blk.gpsimd(_)

    emit_wk_dma(0)
    emit_wk_dma(1)

    def _(g):
        wt(g, "T", tok["hstat2"])
        for dc in range(DC):
            i = g.dma_start(out=wvbf_sb[:, dc, :], in_=wv_d[dc])
            tok[f"wvd{dc}"] = inc(i, "Lwv0", 16)
    blk.gpsimd(_)

    def emit_kmmH(ofc):
        slot = ofc % 2

        def _(t):
            wt(t, f"Lkq{slot}", tok[f"wkd{ofc}"])
            if ofc < 2:
                wt(t, "V", tok["hnorm3"])
            if ofc >= 2:
                wt(t, "V", tok[f"kevH{ofc-2}"])
            for dc in range(DC):
                i = t.matmul(pkH[slot][:], wkbf_sb[slot][:, dc, :],
                             xnh_sb[:, dc, :], start=(dc == 0), stop=(dc == DC - 1))
            tok[f"kmmH{ofc}"] = inc(i, "T")
        blk.tensor(_)

        def _(v):
            wt(v, "T", tok[f"kmmH{ofc}"])
            i = v.tensor_copy(k_sb[:, ofc, 0:512], pkH[slot][:])
            tok[f"kevH{ofc}"] = inc(i, "V")
        blk.vector(_)

    esKV = ExitStack()
    pkO = [esKV.enter_context(nc.psum_tensor(f"pkO{i}", [128, 1024], F32)) for i in range(2)]

    def emit_kmmO1(ofc):
        slot = ofc % 2

        def _(t):
            if ofc == 0:
                wt(t, "V", tok["ln1nrm0"])
            if ofc >= 2:
                wt(t, "V", tok[f"kevO{ofc-2}"])
            for dc in range(DC):
                i = t.matmul(pkO[slot][:, 0:512], wkbf_sb[slot][:, dc, :],
                             xn_sb[:, dc, 0:512],
                             start=(dc == 0), stop=(dc == DC - 1))
            tok[f"kmmO1_{ofc}"] = inc(i, "T")
        blk.tensor(_)

    def emit_kmmO2(ofc):
        slot = ofc % 2

        def _(t):
            if ofc == 0:
                wt(t, "V", tok["ln1nrm1"])
            for dc in range(DC):
                i = t.matmul(pkO[slot][:, 512:1024], wkbf_sb[slot][:, dc, :],
                             xn_sb[:, dc, 512:1024],
                             start=(dc == 0), stop=(dc == DC - 1))
            tok[f"kmmO2_{ofc}"] = inc(i, "T")
        blk.tensor(_)

        def _(v):
            wt(v, "T", tok[f"kmmO2_{ofc}"])
            i = v.tensor_copy(k_sb[:, ofc, 512:1536], pkO[slot][:])
            tok[f"kevO{ofc}"] = inc(i, "V")
        blk.vector(_)

    emit_kmmH(0)
    emit_kmmH(1)
    emit_kmmO1(0)
    emit_kmmO1(1)
    emit_kmmO2(0)
    emit_kmmO2(1)
    emit_wk_dma(2)
    emit_wk_dma(3)
    emit_kmmH(2)
    emit_kmmH(3)
    emit_kmmO1(2)
    emit_kmmO1(3)
    emit_kmmO2(2)
    emit_kmmO2(3)

    pv = [esKV.enter_context(nc.psum_tensor(f"pv{i}", [128, 512], F32)) for i in range(2)]
    for tc in range(12):
        slot = tc % 2

        def _(t):
            if tc == 0:
                wt(t, "Lwv0", tok[f"wvd{DC-1}"])
            if tc < 4:
                wt(t, "V", tok[f"hnorm{tc}"])
            if tc >= 2:
                wt(t, "V", tok[f"vev{tc-2}"])
            for dc in range(DC):
                if tc < 4:
                    lhs = xnh_sb[:, dc, tc * 128:(tc + 1) * 128]
                else:
                    lhs = xn_sb[:, dc, (tc - 4) * 128:(tc - 3) * 128]
                i = t.matmul(pv[slot][:], lhs, wvbf_sb[:, dc, :],
                             start=(dc == 0), stop=(dc == DC - 1))
            tok[f"vmm{tc}"] = inc(i, "T")
        blk.tensor(_)

        def _(v):
            wt(v, "T", tok[f"vmm{tc}"])
            i = v.tensor_copy(v_sb[:, tc, :], pv[slot][:])
            tok[f"vev{tc}"] = inc(i, "V")
        blk.vector(_)
    esKV.close()
    esKVH.close()

    esB = ExitStack()
    pS = [esB.enter_context(nc.psum_tensor(f"pS{i}", [128, 640], F32)) for i in range(2)]
    pT = [esB.enter_context(nc.psum_tensor(f"pT{i}", [128, 640], BF16)) for i in range(2)]
    pAV = [esB.enter_context(nc.psum_tensor("pAV0", [128, 512], F32))]
    pQ = esB.enter_context(nc.psum_tensor("pQ", [128, 512], F32))

    def _(sync):
        wt(sync, "V", tok["ln1nrm1"])
        i = sync.dma_start(out=hmsk_sb[:], in_=hmsk_d)
        tok["hmskd"] = inc(i, "Lhmsk", 16)
    blk.sync(_)

    exp_par = {0: 0, 1: 0}
    exp_ops = []
    S_PLAN = [CNT["S"]]
    oev_par = {0: 0, 1: 0}
    av_par = {0: 0, 1: 0}
    atev_par = {0: 0, 1: 0}
    last_av = 0
    pending_av = [None]

    def emit_av(t):
        (ph, pb, pavpar, pkvh, pbb) = pending_av[0]
        at = at_sb2[pbb]
        wt(t, "V", tok[f"atev{ph * NBLK + pb}_3"])
        wt(t, "V", oev_par[0])
        i = None
        for c in range(8):
            vt = v_sb[:, 4 * pb + c, pkvh * 128:(pkvh + 1) * 128]
            if c == 0:
                i = t.matmul(pAV[0][:, 0:128], vt, at[:, 0, 0:128],
                             start=True, stop=False)
            elif c < 4:
                i = t.matmul(pAV[0][:, 0:128 * c], vt, at[:, c, 0:128 * c],
                             start=False, stop=False)
                i = t.matmul(pAV[0][:, 128 * c:128 * (c + 1)], vt,
                             at[:, c, 128 * c:128 * (c + 1)],
                             start=False, stop=False)
            else:
                cp = c - 4
                i = t.matmul(pAV[0][:, 128 * cp:512], vt,
                             at[:, c, 128 * cp:512],
                             start=False, stop=(c == 7))
        tok[f"av{ph}_{pb}"] = inc(i, "T")
        av_par[pbb] = tok[f"av{ph}_{pb}"]

    def emit_oev(v):
        (ph, pb, pavpar, pkvh, pbb) = pending_av.pop(0)
        wt(v, "T", tok[f"av{ph}_{pb}"])
        i = v.tensor_copy(o_sb[:, ph, 512 * pb:512 * (pb + 1)], pAV[0][:])
        tok[f"oev{ph}_{pb}"] = inc(i, "V")
        oev_par[0] = tok[f"oev{ph}_{pb}"]

    for h in range(H):
        kvh = h // 4
        hslot = h % 2

        def _(g, h=h, hslot=hslot):
            if h < 2:
                wt(g, "T", tok[f"kmmH{2+h}"])
            else:
                wt(g, "T", tok[f"q1mm{h-2}"])
            i = g.dma_start(out=wqbf_sb[hslot].rearrange("p a b -> p (a b)"),
                            in_=wq_d[h])
            tok[f"wqd{h}"] = inc(i, f"Lkq{hslot}", 16)
        blk.gpsimd(_)

        def emit_qmm(t, half):
            wt(t, f"Lkq{hslot}", tok[f"wqd{h}"])
            if half == 0 and h == 0:
                wt(t, "V", tok["vev11"])
            if half == 0 and h >= 1:
                wt(t, "V", tok[f"qev1_{h-1}"])
            if half == 1:
                wt(t, "V", tok[f"qev0_{h}"])
            for dc in range(DC):
                lhs = wqbf_sb[hslot][:, dc, :]
                i = t.matmul(pQ[:], lhs, xn_sb[:, dc, 512 * half:512 * (half + 1)],
                             start=(dc == 0), stop=(dc == DC - 1))
            tok[f"q{half}mm{h}"] = inc(i, "T")

        def emit_qev(v, half):
            wt(v, "T", tok[f"q{half}mm{h}"])
            if h >= 1:
                wt(v, "T", tok[f"smm{h-1}_{half}_3"])
            i = v.tensor_scalar(qh_sb[half][:], pQ[:], SCALE, None, ALU.mult)
            tok[f"qev{half}_{h}"] = inc(i, "V")

        def _(t, h=h):
            emit_qmm(t, 0)
        blk.tensor(_)

        def _(v, h=h):
            emit_qev(v, 0)
        blk.vector(_)

        for b in range(NBLK):
            avpar = (h * NBLK + b) % 2
            bb = (h * NBLK + b) % 2
            at_sb = at_sb2[bb]

            def emit_smm(t, j):
                par = j % 2
                k0 = 512 * b + 128 * j
                wt(t, "S", exp_par[par], future=True)
                wt(t, "V", tok[f"qev{b}_{h}"])
                if h == 0 and b == 0 and j == 0:
                    wt(t, "V", tok["kevO3"])
                    wt(t, "V", tok["vev11"])
                    wt(t, "Lidn", tok["idnd"])
                    wt(t, "Lhmsk", tok["hmskd"])
                    wt(t, "Lmsk", tok["mskd"])
                lhs = qh_sb[b][:, 128 * j: 128 * (j + 1)]
                t.matmul(pS[par][:, 0:512], lhs, k_sb[:, kvh, k0:k0 + 512],
                         start=True, stop=False)
                t.matmul(pS[par][:, 512:640], lhs, k_sb[:, kvh, k0 + 512:k0 + 640],
                         start=True, stop=False)
                t.matmul(pS[par][:, 0:128], idn_sb[:], msk_sb[:, 0, :],
                         start=False, stop=(b != 0))
                i = t.matmul(pS[par][:, 512:640], idn_sb[:], msk_sb[:, 1, :],
                             start=False, stop=True)
                if b == 0:
                    i = t.matmul(pS[par][:, 0:512], idn_sb[:], hmsk_sb[:, j, :],
                                 start=False, stop=True)
                tok[f"smm{h}_{b}_{j}"] = inc(i, "T")

            def emit_red(v, j):
                par = j % 2
                wt(v, "T", tok[f"smm{h}_{b}_{j}"])
                i = v.tensor_reduce(vs(V_NEGMAX + par), pS[par][:], AXX, ALU.max,
                                    negate=True)
                tok[f"msk{h}_{b}_{j}"] = inc(i, "V")

            def emit_exp(sc_unused, j):
                par = j % 2
                wtok = tok[f"msk{h}_{b}_{j}"]
                S_PLAN[0] += 1
                tok[f"exp{h}_{b}_{j}"] = S_PLAN[0]
                exp_par[par] = S_PLAN[0]

                def op(sc, wtok=wtok, par=par):
                    wt(sc, "V", wtok)
                    i = sc.activation(pexp_sb[:, par, :], pS[par][:], AF.Exp,
                                      bias=vs(V_NEGMAX + par),
                                      accum_out=vs(V_SUMEXP + par))
                    inc(i, "S")
                exp_ops.append(op)

            def emit_recip(v, j):
                par = j % 2
                wt(v, "S", tok[f"exp{h}_{b}_{j}"], future=True)
                i = v.reciprocal(vs(V_RECIP + par), vs(V_SUMEXP + par))
                tok[f"rcp{h}_{b}_{j}"] = inc(i, "V")

            def emit_nrm(v, j):
                par = j % 2
                jj = (h * NBLK + b) * 4 + j
                par4 = jj % 4
                if jj >= 4:
                    wt(v, "T", tok[f"tr{jj-4}"])
                i = v.tensor_scalar(abf_sb[:, par4, :], pexp_sb[:, par, :],
                                    vs(V_RECIP + par), None, ALU.mult)
                tok[f"nrm{jj}"] = inc(i, "V")

            def emit_tr(t, jj):
                par = jj % 2
                wt(t, "V", tok[f"nrm{jj}"])
                wt(t, "V", atev_par[par])
                for c in range(5):
                    i = t.transpose(pT[par][:, c * 128:(c + 1) * 128],
                                    abf_sb[:, jj % 4, c * 128:(c + 1) * 128], idn_sb[:])
                tok[f"tr{jj}"] = inc(i, "T")

            def emit_atev(v, jj):
                par = jj % 2
                j = jj % 4
                blki = jj // 4
                wt(v, "T", tok[f"tr{jj}"])
                if j == 0:
                    wt(v, "T", av_par[blki % 2])
                i = v.tensor_copy(at_sb2[blki % 2][:, j:j + 5, 128 * j:128 * (j + 1)],
                                  pT[par].rearrange("p (c m) -> p c m", c=5))
                tok[f"atev{blki}_{j}"] = inc(i, "V")
                atev_par[par] = tok[f"atev{blki}_{j}"]

            blkid = h * NBLK + b

            def _(t):
                emit_smm(t, 0); emit_smm(t, 1)
                if blkid >= 1:
                    emit_tr(t, (blkid - 1) * 4 + 2)
                    emit_tr(t, (blkid - 1) * 4 + 3)
            blk.tensor(_)

            def _(v):
                emit_red(v, 0); emit_red(v, 1)
                if blkid >= 1:
                    emit_atev(v, (blkid - 1) * 4 + 2)
                    emit_atev(v, (blkid - 1) * 4 + 3)
            blk.vector(_)

            emit_exp(None, 0); emit_exp(None, 1)

            def _(v):
                emit_recip(v, 0); emit_recip(v, 1)
                v.drain()
                emit_nrm(v, 0); emit_nrm(v, 1)
            blk.vector(_)

            def _(t):
                emit_smm(t, 2); emit_smm(t, 3)
                emit_tr(t, blkid * 4 + 0); emit_tr(t, blkid * 4 + 1)
                if b == 0:
                    emit_qmm(t, 1)
                if pending_av[0] is not None:
                    emit_av(t)
            blk.tensor(_)

            def _(v):
                emit_red(v, 2); emit_red(v, 3)
                emit_atev(v, blkid * 4 + 0); emit_atev(v, blkid * 4 + 1)
                if b == 0:
                    emit_qev(v, 1)
                if pending_av[0] is not None:
                    emit_oev(v)
            blk.vector(_)

            emit_exp(None, 2); emit_exp(None, 3)

            def _(v):
                emit_recip(v, 2); emit_recip(v, 3)
                v.drain()
                emit_nrm(v, 2); emit_nrm(v, 3)
            blk.vector(_)

            pending_av.insert(0, (h, b, avpar, kvh, bb))
            gamma_pump(1)

    def _(sc):
        for op in exp_ops:
            op(sc)
    blk.scalar(_)
    assert CNT["S"] == S_PLAN[0], (CNT["S"], S_PLAN[0])

    lastblk = H * NBLK - 1

    for jj in (lastblk * 4 + 2, lastblk * 4 + 3):
        def _(t, jj=jj):
            par = jj % 2
            t.wait_ge(SEMS["V"], tok[f"nrm{jj}"])
            t.wait_ge(SEMS["V"], atev_par[par])
            for c in range(5):
                i = t.transpose(pT[par][:, c * 128:(c + 1) * 128],
                                abf_sb[:, jj % 4, c * 128:(c + 1) * 128], idn_sb[:])
            tok[f"tr{jj}"] = inc(i, "T")
        blk.tensor(_)

        def _(v, jj=jj):
            par = jj % 2
            j = jj % 4
            wt(v, "T", tok[f"tr{jj}"])
            i = v.tensor_copy(at_sb2[lastblk % 2][:, j:j + 5, 128 * j:128 * (j + 1)],
                              pT[par].rearrange("p (c m) -> p c m", c=5))
            tok[f"atev{lastblk}_{j}"] = inc(i, "V")
            atev_par[par] = tok[f"atev{lastblk}_{j}"]
        blk.vector(_)

    def _(t):
        emit_av(t)
    blk.tensor(_)

    def _(v):
        emit_oev(v)
    blk.vector(_)
    last_av = tok[f"av{H-1}_{NBLK-1}"]

    esB.close()

    emit_gamma_reduce_head()
    emit_gamma_allreduce()

    esC2 = ExitStack()
    pstat2_m = esC2.enter_context(nc.psum_tensor("pstat2_m", [128, TOWN], F32))
    pstat2_q = esC2.enter_context(nc.psum_tensor("pstat2_q", [128, TOWN], F32))
    esC = ExitStack()
    pwo = [esC.enter_context(nc.psum_tensor(f"pwo{i}", [128, 1024], F32)) for i in range(2)]

    def c_front(doc):
        slot = doc % 2

        def _(g):
            if doc == 0:
                wt(g, "T", last_av)
            if doc >= 2:
                wt(g, "T", tok[f"womm{doc-2}"])
            i = g.dma_start(out=wobf_sb[slot].rearrange("p a b -> p (a b)"), in_=wo_d[doc])
            tok[f"wod{doc}"] = inc(i, f"Lwo{slot}", 16)
        blk.gpsimd(_)

    def c_body(doc):
        slot = doc % 2

        def _(t):
            wt(t, f"Lwo{slot}", tok[f"wod{doc}"])
            if doc == 0:
                wt(t, "V", tok[f"oev{H-1}_{NBLK-1}"])
            if doc >= 2:
                wt(t, "V", tok[f"res{doc-2}"])
            for fc in range(DC):
                lhs = wobf_sb[slot][:, fc, :]
                t.matmul(pwo[slot][:, 0:512], lhs, o_sb[:, fc, 0:512],
                         start=(fc == 0), stop=(fc == DC - 1))
                i = t.matmul(pwo[slot][:, 512:1024], lhs, o_sb[:, fc, 512:1024],
                             start=(fc == 0), stop=(fc == DC - 1))
            tok[f"womm{doc}"] = inc(i, "T")
        blk.tensor(_)

        def _(v):
            wt(v, "T", tok[f"womm{doc}"])
            i = v.tensor_tensor(x_sb[:, doc, :], x_sb[:, doc, :], pwo[slot][:], ALU.add)
            tok[f"res{doc}"] = inc(i, "V")
        blk.vector(_)

        def _(sc):
            wt(sc, "V", tok[f"res{doc}"])
            i = sc.dma_start(out=xsp_d[:, doc, :], in_=x_sb[:, doc, :])
            tok[f"spill{doc}"] = inc(i, "Lsp", 16)
        blk.scalar(_)

        def _(v):
            if doc >= 2:
                wt(v, "T", tok[f"ln2st{doc-2}"])
            v.drain()
            v.tensor_copy(xbscr2_sb[slot][:], x_sb[:, doc, :])
            v.drain()
            i = v.tensor_tensor(xsqscr2_sb[slot][:], x_sb[:, doc, :],
                                x_sb[:, doc, :], ALU.mult)
            tok[f"ln2xbq{doc}"] = inc(i, "V")
        blk.vector(_)

    def c_stats(doc):
        slot = doc % 2

        def _(t):
            wt(t, "V", tok[f"ln2xbq{doc}"])
            for b2 in range(2):
                sl = slice(b2 * 512, (b2 + 1) * 512)
                t.matmul(pstat2_m[:, sl], ones2d_sb[:], xbscr2_sb[slot][:, sl],
                         start=(doc == 0), stop=(doc == DC - 1))
                i = t.matmul(pstat2_q[:, sl], ones2d_sb[:], xsqscr2_sb[slot][:, sl],
                             start=(doc == 0), stop=(doc == DC - 1))
            tok[f"ln2st{doc}"] = inc(i, "T")
        blk.tensor(_)

    def d1_dma(hc):
        slot = hc % 2

        def _(sync):
            if hc < 2:
                wt(sync, "T", last_av)
            else:
                wt(sync, "V", tok[f"qz1_{hc-2}"])
            i = sync.dma_start(out=w1raw_sb[slot].rearrange("p a b -> p (a b)"), in_=w1_d[hc])
            tok[f"w1d{hc}"] = inc(i, f"Lw1{slot}", 16)
            if hc >= 2:
                wt(sync, "V", tok[f"qz2_{hc-2}"])
            i = sync.dma_start(out=w2raw_sb[slot].rearrange("p a b -> p (a b)"), in_=w2_d[hc])
            tok[f"w2d{hc}"] = inc(i, f"Lw2{slot}", 16)
        blk.sync(_)

    def d1_qz1(hc):
        slot = hc % 2

        def _(v):
            wt(v, f"Lw1{slot}", tok[f"w1d{hc}"])
            if hc == 0:
                wt(v, "V", tok["gfinw1"])
                wt(v, "T", last_av)
            if hc >= 2:
                wt(v, "T", tok[f"x1mm{hc-2}"])
            w1r = w1raw_sb[slot].rearrange("p a b -> p (a b)")
            q1r = qh1_sb[slot].rearrange("p a b -> p (a b)")
            v.drain()
            v.tensor_scalar(ascr_sb[:], w1r, vs(V_T1), None, ALU.is_ge)
            v.drain()
            i = v.scalar_tensor_tensor(q1r, w1r, vs(V_NT1), ascr_sb[:], ALU.is_le,
                                       ALU.subtract)
            tok[f"qz1_{hc}"] = inc(i, "V")
        blk.vector(_)

    def d1_qz2(hc):
        slot = hc % 2

        def _(v):
            wt(v, f"Lw2{slot}", tok[f"w2d{hc}"])
            if hc == 0:
                wt(v, "V", tok["gfinw2"])
            if hc < 2:
                wt(v, "T", tok[f"womm{DC-1}"])
            else:
                wt(v, "T", tok[f"x2mm{hc-2}"])
            w2r = w2raw_sb[slot].rearrange("p a b -> p (a b)")
            q2r = qh2_sb[slot].rearrange("p a b -> p (a b)")
            v.drain()
            v.tensor_scalar(ascr2_sb[:], w2r, vs(V_T2), None, ALU.is_ge)
            v.drain()
            i = v.scalar_tensor_tensor(q2r, w2r, vs(V_NT2), ascr2_sb[:], ALU.is_le,
                                       ALU.subtract)
            tok[f"qz2_{hc}"] = inc(i, "V")
        blk.vector(_)

    c_front(0)
    for doc in range(DC):
        if doc + 1 < DC:
            c_front(doc + 1)
        if doc >= 2:
            c_stats(doc - 2)
        c_body(doc)
        if doc == 8:
            d1_dma(0)
        if doc == 9:
            d1_dma(1)
        if doc == 12:
            emit_gamma_back()
        if doc == 14:
            emit_gamma_finalize("w1", NW12, V_T1, V_NT1)
            emit_gamma_finalize("w2", NW12, V_T2, V_NT2)
            emit_gamma_finalize("w3", NW3, V_T3, V_NT3)
            d1_qz1(0)
            d1_qz1(1)
    c_stats(DC - 2)
    c_stats(DC - 1)

    esC.close()
    emit_ln_finalize(pstat2_m, pstat2_q, TOWN, "ln2")
    emit_ln_normalize(pstat2_q, x_sb, xn_sb, TOWN, "ln2", wide=True,
                      between=lambda b2: d1_qz2(b2))
    esC2.close()

    def _(sync):
        wt(sync, "V", tok[f"gv{len(gsched)-1}"])
        i = sync.dma_start(out=w3pre_sb.rearrange("p a b -> p (a b)"), in_=w3_d[0])
        tok["w3pre"] = inc(i, "Lw30", 16)
    blk.sync(_)

    esD1 = ExitStack()
    px1 = [esD1.enter_context(nc.psum_tensor(f"px1{i}", [128, 1024], F32)) for i in range(2)]
    px2 = [esD1.enter_context(nc.psum_tensor(f"px2{i}", [128, 1024], F32)) for i in range(2)]

    def d1_body(hc):
        slot = hc % 2

        def _(t):
            wt(t, "V", tok[f"qz1_{hc}"])
            if hc >= 2:
                wt(t, "S", tok[f"sig{hc-2}"], future=True)
            for dc in range(DC):
                lhs = qh1_sb[slot][:, dc, :]
                t.matmul(px1[slot][:, 0:512], lhs, xn_sb[:, dc, 0:512],
                         start=(dc == 0), stop=(dc == DC - 1))
                i = t.matmul(px1[slot][:, 512:1024], lhs, xn_sb[:, dc, 512:1024],
                             start=(dc == 0), stop=(dc == DC - 1))
            tok[f"x1mm{hc}"] = inc(i, "T")
        blk.tensor(_)

        def _(t):
            wt(t, "V", tok[f"qz2_{hc}"])
            if hc >= 2:
                wt(t, "V", tok[f"hid{hc-2}"])
            for dc in range(DC):
                lhs = qh2_sb[slot][:, dc, :]
                t.matmul(px2[slot][:, 0:512], lhs, xn_sb[:, dc, 0:512],
                         start=(dc == 0), stop=(dc == DC - 1))
                i = t.matmul(px2[slot][:, 512:1024], lhs, xn_sb[:, dc, 512:1024],
                             start=(dc == 0), stop=(dc == DC - 1))
            tok[f"x2mm{hc}"] = inc(i, "T")
        blk.tensor(_)

        waits = [("T", tok[f"x1mm{hc}"])]
        if hc >= 2:
            waits.append(("V", tok[f"hid{hc-2}"]))
        collect_sig(hc, waits)

        def _(v):
            wt(v, "T", tok[f"x2mm{hc}"])
            wt(v, "S", tok[f"sig{hc}"], future=True)
            if hc == 0:
                wt(v, "Lsp", tok[f"spill{DC-1}"])
            i = v.scalar_tensor_tensor(hid_sb[:, hc, :], px2[slot][:], -1.0,
                                       sig_sb[slot][:], ALU.mult, ALU.mult)
            tok[f"hid{hc}"] = inc(i, "V")
        blk.vector(_)

    def _(t):
        wt(t, "V", tok["qz1_0"])
        wt(t, "V", tok["ln2nrm0"])
        for dc in range(DC):
            i = t.matmul(px1[0][:, 0:512], qh1_sb[0][:, dc, :],
                         xn_sb[:, dc, 0:512], start=(dc == 0), stop=(dc == DC - 1))
        tok["x1mm0a"] = inc(i, "T")
    blk.tensor(_)

    def _(t):
        wt(t, "V", tok["qz1_1"])
        for dc in range(DC):
            i = t.matmul(px1[1][:, 0:512], qh1_sb[1][:, dc, :],
                         xn_sb[:, dc, 0:512], start=(dc == 0), stop=(dc == DC - 1))
        tok["x1mm1a"] = inc(i, "T")
    blk.tensor(_)

    def _(t):
        wt(t, "V", tok["ln2nrm1"])
        for dc in range(DC):
            i = t.matmul(px1[0][:, 512:1024], qh1_sb[0][:, dc, :],
                         xn_sb[:, dc, 512:1024], start=(dc == 0), stop=(dc == DC - 1))
        tok["x1mm0"] = inc(i, "T")
    blk.tensor(_)

    def _(t):
        for dc in range(DC):
            i = t.matmul(px1[1][:, 512:1024], qh1_sb[1][:, dc, :],
                         xn_sb[:, dc, 512:1024], start=(dc == 0), stop=(dc == DC - 1))
        tok["x1mm1"] = inc(i, "T")
    blk.tensor(_)

    sig_ops = []
    SIG_PLAN = [CNT["S"]]

    def collect_sig(hc, waits):
        SIG_PLAN[0] += 1
        tok[f"sig{hc}"] = SIG_PLAN[0]
        slot = hc % 2

        def op(sc, hc=hc, slot=slot, waits=tuple(waits)):
            for semn, tv in waits:
                wt(sc, semn, tv, future=True)
            i = sc.activation(sig_sb[slot][:], px1[slot][:], AF.Sigmoid, scale=-1.0)
            inc(i, "S")
        sig_ops.append(op)

    collect_sig(0, [("T", tok["x1mm0"])])

    def _(t):
        wt(t, "V", tok["qz2_0"])
        for dc in range(DC):
            lhs = qh2_sb[0][:, dc, :]
            t.matmul(px2[0][:, 0:512], lhs, xn_sb[:, dc, 0:512],
                     start=(dc == 0), stop=(dc == DC - 1))
            i = t.matmul(px2[0][:, 512:1024], lhs, xn_sb[:, dc, 512:1024],
                         start=(dc == 0), stop=(dc == DC - 1))
        tok["x2mm0"] = inc(i, "T")
    blk.tensor(_)

    collect_sig(1, [("T", tok["x1mm1"])])

    def _(t):
        wt(t, "V", tok["qz2_1"])
        for dc in range(DC):
            lhs = qh2_sb[1][:, dc, :]
            t.matmul(px2[1][:, 0:512], lhs, xn_sb[:, dc, 0:512],
                     start=(dc == 0), stop=(dc == DC - 1))
            i = t.matmul(px2[1][:, 512:1024], lhs, xn_sb[:, dc, 512:1024],
                         start=(dc == 0), stop=(dc == DC - 1))
        tok["x2mm1"] = inc(i, "T")
    blk.tensor(_)

    for hc in range(2):
        def _(v, hc=hc):
            wt(v, "T", tok[f"x2mm{hc}"])
            wt(v, "S", tok[f"sig{hc}"], future=True)
            if hc == 0:
                wt(v, "Lsp", tok[f"spill{DC-1}"])
            i = v.scalar_tensor_tensor(hid_sb[:, hc, :], px2[hc][:], -1.0,
                                       sig_sb[hc][:], ALU.mult, ALU.mult)
            tok[f"hid{hc}"] = inc(i, "V")
        blk.vector(_)

    d1_dma(2)
    d1_dma(3)
    d1_qz1(2)
    d1_qz2(2)
    d1_qz1(3)
    d1_qz2(3)
    for hc in range(2, FHC):
        if hc + 2 < FHC:
            d1_dma(hc + 2)
        d1_body(hc)
        if hc + 2 < FHC:
            d1_qz1(hc + 2)
            d1_qz2(hc + 2)

    def _(sc):
        for op in sig_ops:
            op(sc)
    blk.scalar(_)
    assert CNT["S"] == SIG_PLAN[0], (CNT["S"], SIG_PLAN[0])
    esD1.close()


    esD2 = ExitStack()
    pout = [esD2.enter_context(nc.psum_tensor(f"pout{i}", [128, 1024], F32)) for i in range(2)]
    pending_stores = []

    def d2_front(doc):
        slot = doc % 2
        pstore = pending_stores.pop(0) if pending_stores else None

        def _(sync):
            if doc < 2:
                wt(sync, "T", tok[f"x2mm{FHC-1}"])
                wt(sync, "V", tok[f"qz1_{FHC-1}"])
            else:
                wt(sync, "V", tok[f"qz3b_{doc-2}"])
            if doc != 0:
                i = sync.dma_start(out=w3raw_sb[slot].rearrange("p a b -> p (a b)"),
                                   in_=w3_d[doc])
                tok[f"w3d{doc}"] = inc(i, f"Lw3{slot}", 16)
            if pstore is not None:
                dprev, sprev = pstore
                wt(sync, "V", tok[f"fin{dprev}"])
                i = sync.dma_start(out=y_d[:, dprev, :], in_=outst_sb[sprev][:])
                tok[f"store{dprev}"] = inc(i, f"Lst{sprev}", 16)
        blk.sync(_)

        def _(sc):
            if doc < 2:
                wt(sc, "T", tok[f"x2mm{FHC-1}"])
            if doc >= 2:
                wt(sc, "V", tok[f"fin{doc-2}"])
            i = sc.dma_start(out=xrld_sb[slot][:], in_=xsp_d[:, doc, :])
            tok[f"xr{doc}"] = inc(i, f"Lxr{slot}", 16)
        blk.scalar(_)

        HB = FHC // 2 * 128
        w3src = w3pre_sb if doc == 0 else w3raw_sb[slot]
        w3r = w3src.rearrange("p a b -> p (a b)")
        q3r = qh3_sb[slot].rearrange("p a b -> p (a b)")

        def _(v):
            if doc == 0:
                wt(v, "Lw30", tok["w3pre"])
                wt(v, "V", tok["gfinw3"])
            else:
                wt(v, f"Lw3{slot}", tok[f"w3d{doc}"])
            if doc < 2:
                wt(v, "T", tok[f"x2mm{FHC-1}"])
            else:
                wt(v, "T", tok[f"omma_{doc-2}"])
            v.drain()
            v.tensor_scalar(a3scr_sb[:, 0:HB], w3r[:, 0:HB], vs(V_T3), None, ALU.is_ge)
            v.drain()
            i = v.scalar_tensor_tensor(q3r[:, 0:HB], w3r[:, 0:HB], vs(V_NT3),
                                       a3scr_sb[:, 0:HB], ALU.is_le, ALU.subtract)
            tok[f"qz3a_{doc}"] = inc(i, "V")
        blk.vector(_)

        def _(v):
            if doc >= 2:
                wt(v, "T", tok[f"omm{doc-2}"])
            v.drain()
            v.tensor_scalar(a3scr_sb[:, HB:], w3r[:, HB:], vs(V_T3), None, ALU.is_ge)
            v.drain()
            i = v.scalar_tensor_tensor(q3r[:, HB:], w3r[:, HB:], vs(V_NT3),
                                       a3scr_sb[:, HB:], ALU.is_le, ALU.subtract)
            tok[f"qz3b_{doc}"] = inc(i, "V")
        blk.vector(_)

    def d2_body(doc):
        slot = doc % 2

        def _(t):
            wt(t, "V", tok[f"qz3a_{doc}"])
            if doc == 0:
                wt(t, "V", tok[f"hid{FHC-1}"])
            if doc >= 2:
                wt(t, "V", tok[f"fin{doc-2}"])
            for hc in range(FHC // 2):
                lhs = qh3_sb[slot][:, hc, :]
                t.matmul(pout[slot][:, 0:512], lhs, hid_sb[:, hc, 0:512],
                         start=(hc == 0), stop=False)
                i = t.matmul(pout[slot][:, 512:1024], lhs, hid_sb[:, hc, 512:1024],
                             start=(hc == 0), stop=False)
            tok[f"omma_{doc}"] = inc(i, "T")
        blk.tensor(_)

        def _(t):
            wt(t, "V", tok[f"qz3b_{doc}"])
            for hc in range(FHC // 2, FHC):
                lhs = qh3_sb[slot][:, hc, :]
                t.matmul(pout[slot][:, 0:512], lhs, hid_sb[:, hc, 0:512],
                         start=False, stop=(hc == FHC - 1))
                i = t.matmul(pout[slot][:, 512:1024], lhs, hid_sb[:, hc, 512:1024],
                             start=False, stop=(hc == FHC - 1))
            tok[f"omm{doc}"] = inc(i, "T")
        blk.tensor(_)

        def _(v):
            wt(v, "T", tok[f"omm{doc}"])
            wt(v, f"Lxr{slot}", tok[f"xr{doc}"])
            if doc >= 2:
                wt(v, f"Lst{slot}", tok[f"store{doc-2}"])
            i = v.tensor_tensor(outst_sb[slot][:], xrld_sb[slot][:], pout[slot][:],
                                ALU.subtract)
            tok[f"fin{doc}"] = inc(i, "V")
        blk.vector(_)
        pending_stores.append((doc, slot))

    d2_front(0)
    for doc in range(DC):
        if doc + 1 < DC:
            d2_front(doc + 1)
        d2_body(doc)

    def _(sync):
        for dprev, sprev in pending_stores:
            wt(sync, "V", tok[f"fin{dprev}"])
            i = sync.dma_start(out=y_d[:, dprev, :], in_=outst_sb[sprev][:])
            tok[f"store{dprev}"] = inc(i, f"Lst{sprev}", 16)
        sync.wait_ge(SEMS["Lst0"], CNT["Lst0"])
        sync.wait_ge(SEMS["Lst1"], CNT["Lst1"])
    blk.sync(_)
    esD2.close()

    assert gstate["pos"] == len(gsched), gstate["pos"]
    es_top.close()
    return nc



def _stage_inputs(x, Wq, Wk, Wv, Wo, w1, w2, w3):
    x = np.ascontiguousarray(np.asarray(x, dtype=np.float32))
    xf = np.ascontiguousarray(x.reshape(8192, D).T)

    wq_t = np.ascontiguousarray(
        np.asarray(Wq, np.float32).reshape(DC, 128, H, 128).transpose(2, 1, 0, 3).reshape(H, 128, D))
    wk_t = np.ascontiguousarray(
        np.asarray(Wk, np.float32).reshape(DC, 128, 4, 128).transpose(2, 1, 0, 3).reshape(4, 128, D))
    wv_t = np.ascontiguousarray(np.asarray(Wv, np.float32).reshape(DC, 128, 512))
    wo_t = np.ascontiguousarray(
        np.asarray(Wo, np.float32).reshape(DC, 128, DC, 128).transpose(2, 1, 0, 3).reshape(DC, 128, D))
    w1_t = np.ascontiguousarray(
        np.asarray(w1, np.float32).reshape(FHC, 128, DC, 128).transpose(0, 3, 2, 1).reshape(FHC, 128, D))
    w2_t = np.ascontiguousarray(
        np.asarray(w2, np.float32).reshape(FHC, 128, DC, 128).transpose(0, 3, 2, 1).reshape(FHC, 128, D))
    w3_t = np.ascontiguousarray(
        np.asarray(w3, np.float32).reshape(DC, 128, FHC, 128).transpose(0, 3, 2, 1).reshape(DC, 128, FH))

    r = np.arange(128)[:, None]
    c = np.arange(128)[None, :]
    msk = np.zeros((128, 2, 128), np.float32)
    msk[:, 0, :] = np.where(c > r, 0.0, NEG)
    msk[:, 1, :] = np.where(c <= r, 0.0, NEG)
    msk = msk.astype(BF)
    idn = np.eye(128, dtype=BF)

    w12v = np.concatenate([
        w1_t.reshape(FHC, 128, 2, 1024).transpose(0, 2, 1, 3).reshape(2 * FHC, 128, 1024),
        w2_t.reshape(FHC, 128, 2, 1024).transpose(0, 2, 1, 3).reshape(2 * FHC, 128, 1024)])
    w3v = w3_t.reshape(DC, 128, 4, 1408).transpose(0, 2, 1, 3).reshape(4 * DC, 128, 1408)

    hm0 = np.zeros((128, 4, 512), np.float32)
    for j in range(4):
        kk = np.arange(512)[None, :]
        hm0[:, j, :] = np.where(kk < 512 - 128 * j, NEG, 0.0)
    hm0 = hm0.astype(BF)
    hmz = np.zeros((128, 4, 512), BF)

    in_maps = []
    for core in range(8):
        t0 = core * TOWN
        x_own = np.ascontiguousarray(
            xf[:, t0:t0 + TOWN].reshape(DC, 128, TOWN).transpose(1, 0, 2))
        if core % 4 == 0:
            halo = np.zeros((D, THALO), np.float32)
        else:
            halo = xf[:, t0 - THALO:t0]
        x_halo = np.ascontiguousarray(
            halo.reshape(DC, 128, 4, 128).transpose(2, 1, 0, 3).reshape(4, 128, D))
        gsh12 = np.ascontiguousarray(np.concatenate(
            [w12v[0:88][core::8], w12v[88:176][core::8]]))
        gsh3 = np.ascontiguousarray(w3v[core::8])
        in_maps.append(dict(
            x_own=x_own, x_halo=x_halo,
            wq=wq_t, wk=wk_t, wv=wv_t, wo=wo_t,
            w1=w1_t, w2=w2_t, w3=w3_t,
            gsh12=gsh12, gsh3=gsh3,
            msk=msk, hmsk=(hm0 if core % 4 == 0 else hmz), idn=idn,
        ))
    return in_maps


_PROG_CACHE = {}


def kernel(L0_attn_x, L0_attn_Wq, L0_attn_Wk, L0_attn_Wv, L0_attn_Wo,
           L0_ffn_w1, L0_ffn_w2, L0_ffn_w3):
    in_maps = _stage_inputs(L0_attn_x, L0_attn_Wq, L0_attn_Wk, L0_attn_Wv,
                            L0_attn_Wo, L0_ffn_w1, L0_ffn_w2, L0_ffn_w3)
    if "nc" not in _PROG_CACHE:
        _PROG_CACHE["nc"] = build_program()
    nc = _PROG_CACHE["nc"]
    res = run_bass_kernel_spmd(nc, in_maps, core_ids=list(range(8)))

    out = np.empty((D, 8192), np.float32)
    for core in range(8):
        shard = np.asarray(res.results[core]["y"])
        out[:, core * TOWN:(core + 1) * TOWN] = shard.transpose(1, 0, 2).reshape(D, TOWN)
    return np.ascontiguousarray(out.T.reshape(2, 4096, D))


# version 23
# speedup vs baseline: 1.0149x; 1.0039x over previous
import numpy as np
import ml_dtypes
import concourse.bass as bass
import concourse.mybir as mybir
from concourse.bass_utils import run_bass_kernel_spmd
from concourse.alu_op_type import AluOpType as ALU

from contextlib import ExitStack

F32 = mybir.dt.float32
BF16 = mybir.dt.bfloat16
AF = mybir.ActivationFunctionType
AXX = mybir.AxisListType.X
BF = ml_dtypes.bfloat16

D = 2048
DC = 16
H = 16
HD = 128
FH = 5632
FHC = 44
TOWN = 1024
THALO = 512
TTOT = 1536
NBLK = 2
SCALE = 1.0 / np.sqrt(128.0)
NEG = -1e30
LN_EPS = 1e-5
FFN_EPS = 1e-8
NW12 = FH * D
NW3 = D * FH

MBASE = 16512
SBUF_LIMIT = 212832


def build_program():
    nc = bass.Bass("TRN2", target_bir_lowering=False)

    def din(name, shape, dt=F32):
        return nc.dram_tensor(name, shape, dt, kind="ExternalInput").ap()

    x_own_d = din("x_own", [128, DC, TOWN])
    x_halo_d = din("x_halo", [4, 128, D])
    wq_d = din("wq", [H, 128, D])
    wk_d = din("wk", [4, 128, D])
    wv_d = din("wv", [DC, 128, 512])
    wo_d = din("wo", [DC, 128, D])
    w1_d = din("w1", [FHC, 128, D])
    w2_d = din("w2", [FHC, 128, D])
    w3_d = din("w3", [DC, 128, FH])
    msk_d = din("msk", [128, 2, 128], BF16)
    hmsk_d = din("hmsk", [128, 4, 512], BF16)
    idn_d = din("idn", [128, 128], BF16)

    y_d = nc.dram_tensor("y", [128, DC, TOWN], F32, kind="ExternalOutput").ap()
    xsp_d = nc.dram_tensor("xsp", [128, DC, TOWN], F32, kind="Internal").ap()

    gsh12_d = din("gsh12", [22, 128, 1024])
    gsh3_d = din("gsh3", [8, 128, 1408])
    gbin_d = nc.dram_tensor("gbin", [1, 4], F32, kind="Internal").ap()
    gbout_d = nc.dram_tensor("gbout", [1, 4], F32, kind="Internal").ap()

    def sb(name, shape, dt, off):
        elts = 1
        for s in shape[1:]:
            elts *= s
        nbytes = elts * (4 if dt == F32 else 2)
        assert off + nbytes <= SBUF_LIMIT, (name, off, nbytes, off + nbytes)
        return nc.alloc_sbuf_tensor_at(name, shape, dt, offset=MBASE + off).ap()

    x_sb = sb("x", [128, DC, TOWN], F32, 0)
    hid_sb = sb("hid", [128, FHC, TOWN], BF16, 0)
    haloraw_sb = [sb(f"hraw{i}", [128, DC, 128], F32, 65536 + i * 8192) for i in range(2)]
    xnh_sb = sb("xnh", [128, DC, THALO], BF16, 81920)
    wqbf_sb = [sb(f"wqbf{i}", [128, DC, 128], BF16, 188416 + i * 4096) for i in range(2)]
    o_sb = sb("o", [128, DC, TOWN], BF16, 65536)
    outst_sb = [sb(f"outst{i}", [128, TOWN], F32, 90112 + i * 4096) for i in range(2)]
    xn_sb = sb("xn", [128, DC, TOWN], BF16, 98304)
    at_sb2 = [sb(f"at{i}", [128, 8, 512], BF16, 131072 + i * 8192) for i in range(2)]
    pexp_sb = sb("pexp", [128, 2, 640], F32, 147456)
    abf_sb = sb("abf", [128, 4, 640], BF16, 152576)
    hmsk_sb = sb("hmsk", [128, 4, 512], BF16, 157696)
    xrld_sb = [sb(f"xrld{i}", [128, TOWN], F32, 98304 + i * 4096) for i in range(2)]
    xbscr2_sb = [sb(f"xbscr2_{i}", [128, TOWN], BF16, 106496 + i * 2048) for i in range(2)]
    xsqscr2_sb = [sb(f"xsqscr2_{i}", [128, TOWN], BF16, 110592 + i * 2048) for i in range(2)]
    hxb_sb = [sb(f"hxb{i}", [128, D], BF16, (131072, 147456)[i]) for i in range(2)]
    hxq_sb = [sb(f"hxq{i}", [128, D], BF16, (135168, 151552)[i]) for i in range(2)]
    wvbf_sb = sb("wvbf", [128, DC, 512], BF16, 131072)
    wkbf_sb = [sb(f"wkbf{i}", [128, DC, 128], BF16, 147456 + i * 4096) for i in range(2)]
    qh_sb = [sb(f"qh_{i}", [128, 512], BF16, 161792 + i * 1024) for i in range(2)]
    w1raw_sb = [sb(f"w1raw{i}", [128, DC, 128], F32, 131072 + i * 8192) for i in range(2)]
    qh1_sb = [sb(f"qh1{i}", [128, DC, 128], BF16, 147456 + i * 4096) for i in range(2)]
    ascr_sb = sb("ascr", [128, D], BF16, 155648)
    ascr2_sb = sb("ascr2", [128, D], BF16, 90112)
    sig_sb = [sb(f"sig{i}", [128, TOWN], BF16, 159744 + i * 2048) for i in range(2)]
    k_sb = sb("k", [128, 4, TTOT], BF16, 163840)
    wobf_sb = [sb(f"wobf{i}", [128, DC, 128], BF16, 180224 + i * 4096) for i in range(2)]
    v_sb = sb("v", [128, 12, 512], BF16, 176128)
    w2raw_sb = [sb(f"w2raw{i}", [128, DC, 128], F32, 163840 + i * 8192) for i in range(2)]
    qh2_sb = [sb(f"qh2{i}", [128, DC, 128], BF16, 180224 + i * 4096) for i in range(2)]
    gsh_sb = [sb(f"gsh{i}", [128, 1408], F32, (196608, 205312)[i]) for i in range(2)]
    w3raw_sb = [sb(f"w3raw{i}", [128, FHC, 128], F32, 106496 + i * 22528) for i in range(2)]
    qh3_sb = [sb(f"qh3{i}", [128, FHC, 128], BF16, 151552 + i * 11264) for i in range(2)]
    a3scr_sb = sb("a3scr", [128, FH], BF16, 174080)
    msk_sb = sb("msk", [128, 2, 128], BF16, 204800)
    mrow_sb = sb("mrow", [128, TOWN], F32, 155648)
    sqrow_sb = sb("sqrow", [128, TOWN], F32, 159744)
    idn_sb = sb("idn", [128, 128], BF16, 211488)
    ones_sb = sb("ones", [128, 1], BF16, 211744)
    onesf_sb = sb("onesf", [128, 1], F32, 211776)
    onesrow_sb = sb("onesrow", [1, 128], F32, 211808)
    vecs_sb = sb("vecs", [128, 24], F32, 212320)
    gsum_sb = sb("gsum", [1, 4], F32, 212416)
    gsuma_sb = sb("gsuma", [1, 4], F32, 212448)
    ones2d_sb = sb("ones2d", [128, 128], BF16, 212480)
    gthr_sb = sb("gthr", [128, 4], F32, 212736)
    w3pre_sb = sb("w3pre", [128, FHC, 128], F32, 188416)

    def vs(i):
        return vecs_sb[:, i:i + 1]
    V_MAX, V_NEGMAX, V_SUMEXP, V_RECIP = 0, 2, 4, 6
    V_T1, V_NT1, V_T2, V_NT2, V_T3, V_NT3 = 8, 9, 10, 11, 12, 13
    V_GACC = {"w1": 14, "w2": 15, "w3": 16}
    V_GTMP = 17
    V_EPS = 18
    V_TG = 19

    es_top = ExitStack()
    SEMS = {n: es_top.enter_context(nc.semaphore(n))
            for n in ("T", "V", "S", "Lx0", "Lx1", "Lx2", "Lx3", "Lmsk", "Lidn", "Lhmsk",
                      "Lh0", "Lh1", "Lwv0", "Lwv1", "Lkq0", "Lkq1",
                      "Lwo0", "Lwo1", "Lw10", "Lw11", "Lw20", "Lw21",
                      "Lw30", "Lw31", "Lg0", "Lg1", "Lsp",
                      "Lxr0", "Lxr1", "Lst0", "Lst1", "G")}
    CNT = {k: 0 for k in SEMS}

    def inc(instr, name, k=1):
        instr.then_inc(SEMS[name], k)
        CNT[name] += k
        return CNT[name]

    def wt(eng, name, tokv, future=False):
        assert future or tokv <= CNT[name], (name, tokv, CNT[name])
        if tokv > 0:
            eng.wait_ge(SEMS[name], tokv)

    blk = es_top.enter_context(nc.Block())
    tok = {}

    def bcast(ap, n):
        return bass.AP(tensor=ap.tensor, offset=ap.offset,
                       ap=[list(ap.ap[0]), [0, n], list(ap.ap[1])])

    def _(sync):
        for s in range(2):
            i = sync.dma_start(out=haloraw_sb[s].rearrange("p a b -> p (a b)"),
                               in_=x_halo_d[s])
            tok[f"hraw{s}"] = inc(i, f"Lh{s}", 16)
        for g in range(4):
            i = sync.dma_start(out=x_sb[:, 4 * g:4 * (g + 1), :], in_=x_own_d[:, 4 * g:4 * (g + 1), :])
            tok[f"xg{g}"] = inc(i, f"Lx{g}", 16)
        i = sync.dma_start(out=msk_sb[:], in_=msk_d)
        tok["mskd"] = inc(i, "Lmsk", 16)
        i = sync.dma_start(out=idn_sb[:], in_=idn_d)
        tok["idnd"] = inc(i, "Lidn", 16)
    blk.sync(_)

    def _(v):
        v.memset(ones_sb[:], 1.0)
        v.memset(onesf_sb[:], 1.0)
        v.memset(onesrow_sb[:], 1.0)
        v.memset(ones2d_sb[:], 1.0)
        v.memset(vecs_sb[:, 0:V_EPS], 0.0)
        v.memset(vecs_sb[:, V_EPS + 1:], 0.0)
        v.memset(vecs_sb[:, V_EPS:V_EPS + 1], LN_EPS)
        i = v.memset(gsum_sb[0:1, 3:4], 0.0)
        tok["vinit"] = inc(i, "V")
    blk.vector(_)


    def emit_ln_finalize(pst_m, pst_q, ntok, name, extra=()):
        def _(v):
            wt(v, "T", tok[f"{name}st{DC-1}"])
            for semn, tv in extra:
                wt(v, semn, tv)
            v.drain()
            v.tensor_scalar(mrow_sb[:, 0:ntok], pst_m[:], 1.0 / D, None, ALU.mult)
            v.drain()
            v.tensor_tensor(sqrow_sb[:, 0:ntok], mrow_sb[:, 0:ntok],
                            mrow_sb[:, 0:ntok], ALU.mult)
            v.drain()
            i = v.scalar_tensor_tensor(pst_q[:], pst_q[:], 1.0 / D,
                                       sqrow_sb[:, 0:ntok], ALU.mult, ALU.subtract)
            tok[f"{name}var"] = inc(i, "V")
        blk.vector(_)

    def emit_ln_normalize(pst_q, xsrc, xdst, ntok, name, wide=False, between=None):
        esn = ExitStack()
        pw = ntok if wide else 512
        pa = esn.enter_context(nc.psum_tensor(f"pa_{name}", [128, pw], F32))
        pb = esn.enter_context(nc.psum_tensor(f"pb_{name}", [128, pw], F32))
        if wide:
            def _(sc):
                wt(sc, "V", tok[f"{name}var"])
                i = None
                for b2 in range(ntok // 512):
                    sl = slice(b2 * 512, (b2 + 1) * 512)
                    i = sc.activation(pa[:, sl], pst_q[:, sl], AF.Sqrt,
                                      bias=vecs_sb[:, V_EPS:V_EPS + 1])
                for b2 in range(ntok // 512):
                    tok[f"{name}sd{b2}"] = CNT["S"] + 1
                inc(i, "S")
            blk.scalar(_)
        for b2 in range(ntok // 512):
            sl = slice(b2 * 512, (b2 + 1) * 512)
            pl = sl if wide else slice(0, 512)

            if not wide:
                def _(sc):
                    if b2 == 0:
                        wt(sc, "V", tok[f"{name}var"])
                    else:
                        wt(sc, "V", tok[f"{name}nrm{b2-1}"])
                    i = sc.activation(pa[:], pst_q[:, sl], AF.Sqrt,
                                      bias=vecs_sb[:, V_EPS:V_EPS + 1])
                    tok[f"{name}sd{b2}"] = inc(i, "S")
                blk.scalar(_)

            def _(v):
                wt(v, "S", tok[f"{name}sd{b2}"])
                v.reciprocal(pa[:, pl], pa[:, pl])
                v.drain()
                v.scalar_tensor_tensor(pb[:, pl], mrow_sb[:, sl], -1.0, pa[:, pl],
                                       ALU.mult, ALU.mult)
                v.drain()
                dst = xdst[:, :, sl]
                v.tensor_tensor(dst, xsrc[:, :, sl], bcast(pa[:, pl], DC), ALU.mult)
                v.drain()
                i = v.tensor_tensor(dst, dst, bcast(pb[:, pl], DC), ALU.add)
                v.drain()
                tok[f"{name}nrm{b2}"] = inc(i, "V")
            blk.vector(_)
            if between is not None:
                between(b2)
        esn.close()

    esKVH = ExitStack()
    pkH = [esKVH.enter_context(nc.psum_tensor(f"pkH{i}", [128, 512], F32)) for i in range(2)]
    esA2 = ExitStack()
    pstat_m = esA2.enter_context(nc.psum_tensor("pstat_m", [128, TOWN], F32))
    pstat_q = esA2.enter_context(nc.psum_tensor("pstat_q", [128, TOWN], F32))
    esA1 = ExitStack()
    pstH_m = esA1.enter_context(nc.psum_tensor("pstH_m", [128, 128], F32))
    pstH_q = esA1.enter_context(nc.psum_tensor("pstH_q", [128, 128], F32))

    def h_front(s):
        slot = s % 2
        if s >= 2:
            def _(sync):
                wt(sync, "V", tok[f"hnorm{s-2}"])
                i = sync.dma_start(out=haloraw_sb[slot].rearrange("p a b -> p (a b)"),
                                   in_=x_halo_d[s])
                tok[f"hraw{s}"] = inc(i, f"Lh{slot}", 16)
            blk.sync(_)

        def _(v):
            wt(v, f"Lh{slot}", tok[f"hraw{s}"])
            if s >= 2:
                wt(v, "T", tok[f"hstat{s-2}"])
            raw = haloraw_sb[slot].rearrange("p a b -> p (a b)")
            v.tensor_copy(hxb_sb[slot][:], raw)
            i = v.tensor_tensor(hxq_sb[slot][:], raw, raw, ALU.mult)
            tok[f"hxbq{s}"] = inc(i, "V")
        blk.vector(_)

    def h_stats(s):
        slot = s % 2

        def _(t):
            wt(t, "V", tok[f"hxbq{s}"])
            if s >= 1:
                wt(t, "V", tok[f"hnorm{s-1}"])
            for dc in range(DC):
                t.matmul(pstH_m[:], ones2d_sb[:], hxb_sb[slot][:, dc * 128:(dc + 1) * 128],
                         start=(dc == 0), stop=(dc == DC - 1))
            for dc in range(DC):
                i = t.matmul(pstH_q[:], ones2d_sb[:], hxq_sb[slot][:, dc * 128:(dc + 1) * 128],
                             start=(dc == 0), stop=(dc == DC - 1))
            tok[f"hstat{s}"] = inc(i, "T")
        blk.tensor(_)

    def h_fin(s):
        def _(v):
            wt(v, "T", tok[f"hstat{s}"])
            i = v.tensor_scalar(mrow_sb[:, 0:128], pstH_m[:], 1.0 / D, None, ALU.mult)
            tok[f"hm{s}"] = inc(i, "V")
        blk.vector(_)

        def _(sc):
            wt(sc, "V", tok[f"hm{s}"])
            i = sc.activation(sqrow_sb[:, 0:128], mrow_sb[:, 0:128], AF.Square)
            tok[f"hsq{s}"] = inc(i, "S")
        blk.scalar(_)

        def _(v):
            wt(v, "S", tok[f"hsq{s}"])
            i = v.scalar_tensor_tensor(pstH_q[:], pstH_q[:], 1.0 / D,
                                       sqrow_sb[:, 0:128], ALU.mult, ALU.subtract)
            tok[f"hvar{s}"] = inc(i, "V")
        blk.vector(_)

        def _(sc):
            wt(sc, "V", tok[f"hvar{s}"])
            i = sc.activation(pstH_m[:], pstH_q[:], AF.Sqrt,
                              bias=vecs_sb[:, V_EPS:V_EPS + 1])
            tok[f"hsd{s}"] = inc(i, "S")
        blk.scalar(_)

    def h_norm(s):
        slot = s % 2

        def _(v):
            wt(v, "S", tok[f"hsd{s}"])
            v.reciprocal(pstH_m[:], pstH_m[:])
            v.drain()
            v.scalar_tensor_tensor(pstH_q[:], mrow_sb[:, 0:128], -1.0,
                                   pstH_m[:], ALU.mult, ALU.mult)
            v.drain()
            dst = xnh_sb[:, :, s * 128:(s + 1) * 128]
            v.tensor_tensor(dst, haloraw_sb[slot][:], bcast(pstH_m[:], DC), ALU.mult)
            v.drain()
            i = v.tensor_tensor(dst, dst, bcast(pstH_q[:], DC), ALU.add)
            v.drain()
            tok[f"hnorm{s}"] = inc(i, "V")
        blk.vector(_)

    def a2_cast(dc):
        slot = dc % 2

        def _(v):
            wt(v, f"Lx{dc // 4}", tok[f"xg{dc // 4}"])
            if dc >= 2:
                wt(v, "T", tok[f"ln1st{dc-2}"])
            v.tensor_copy(xbscr2_sb[slot][:], x_sb[:, dc, :])
            i = v.tensor_tensor(xsqscr2_sb[slot][:], x_sb[:, dc, :], x_sb[:, dc, :],
                                ALU.mult)
            tok[f"ln1xbq{dc}"] = inc(i, "V")
        blk.vector(_)

    def a2_stats(dc):
        slot = dc % 2

        def _(t):
            wt(t, "V", tok[f"ln1xbq{dc}"])
            for b2 in range(2):
                sl = slice(b2 * 512, (b2 + 1) * 512)
                t.matmul(pstat_m[:, sl], ones2d_sb[:], xbscr2_sb[slot][:, sl],
                         start=(dc == 0), stop=(dc == DC - 1))
                i = t.matmul(pstat_q[:, sl], ones2d_sb[:], xsqscr2_sb[slot][:, sl],
                             start=(dc == 0), stop=(dc == DC - 1))
            tok[f"ln1st{dc}"] = inc(i, "T")
        blk.tensor(_)

    h_front(0); h_front(1)
    halo_stages = []
    for s in range(4):
        halo_stages.append(lambda s=s: h_stats(s))
        halo_stages.append(lambda s=s: h_fin(s))
        if s + 2 < 4:
            halo_stages.append(lambda s=s: (h_norm(s), h_front(s + 2)))
        else:
            halo_stages.append(lambda s=s: h_norm(s))
    for dc in range(DC):
        a2_cast(dc)
        a2_stats(dc)
        if halo_stages:
            halo_stages.pop(0)()
    while halo_stages:
        halo_stages.pop(0)()
    esA1.close()
    emit_ln_finalize(pstat_m, pstat_q, TOWN, "ln1", extra=[("S", tok["hsd3"])])
    emit_ln_normalize(pstat_q, x_sb, xn_sb, TOWN, "ln1")
    esA2.close()

    gsched = ([("w1", i, 1024) for i in range(11)]
              + [("w2", i, 1024) for i in range(11)]
              + [("w3", i, 1408) for i in range(8)])
    gstate = {"pos": 0, "guards": [], "guardn": 0}

    def gamma_pump(n):
        for _n in range(n):
            gi = gstate["pos"]
            if gi >= len(gsched):
                return
            gstate["pos"] += 1
            which, ich, width = gsched[gi]
            slot = gi % 2
            src = gsh3_d[ich] if which == "w3" else gsh12_d[gi]

            def _(sync):
                wt(sync, "V", tok.get(f"gv{gi-2}", 0))
                i = sync.dma_start(out=gsh_sb[slot][:, 0:width], in_=src)
                tok[f"gd{gi}"] = inc(i, f"Lg{slot}", 16)
            blk.sync(_)

            def _(v):
                wt(v, f"Lg{slot}", tok[f"gd{gi}"])
                v.drain()
                v.tensor_reduce(vs(V_GTMP), gsh_sb[slot][:, 0:width], AXX, ALU.add,
                                apply_absolute_value=True)
                v.drain()
                acc = V_GACC[which]
                i = v.tensor_tensor(vs(acc), vs(acc), vs(V_GTMP), ALU.add)
                tok[f"gv{gi}"] = inc(i, "V")
            blk.vector(_)

    def emit_gamma_reduce_head():
        def _(sync):
            wt(sync, "V", tok[f"gv{len(gsched)-1}"])
            wt(sync, "T", last_av)
            i = sync.dma_start(out=mrow_sb[0:1, 0:384], in_=vecs_sb[:, V_GACC["w1"]:V_GACC["w1"] + 3])
            tok["grow"] = inc(i, "Lmsk", 16)
        blk.sync(_)

        def _(v):
            wt(v, "Lmsk", tok["grow"])
            i = None
            for wi in range(3):
                src = bass.AP(tensor=mrow_sb.tensor, offset=mrow_sb.offset + wi,
                              ap=[[mrow_sb.ap[0][0], 1], [3, 128]])
                i = v.tensor_reduce(gsum_sb[0:1, wi:wi + 1], src, AXX, ALU.add)
            tok["gsum"] = inc(i, "V")
        blk.vector(_)

        def _(sync):
            wt(sync, "V", tok["gsum"])
            i = sync.dma_start(out=gbin_d, in_=gsum_sb[0:1, :])
            tok["gbin"] = inc(i, "Lmsk", 16)
        blk.sync(_)

    def emit_gamma_allreduce():
        def _(g):
            wt(g, "Lmsk", tok["gbin"])
            i = g.collective_compute(
                "AllReduce", ALU.add,
                replica_groups=[list(range(8))],
                ins=[gbin_d.opt()], outs=[gbout_d.opt()])
            tok["gcc"] = inc(i, "G")
        blk.gpsimd(_)

    def emit_gamma_back():
        def _(sync):
            wt(sync, "G", tok["gcc"])
            bc = bass.AP(tensor=gbout_d.tensor, offset=gbout_d.offset,
                         ap=[[0, 128], [1, 4]])
            i = sync.dma_start(out=gthr_sb[:], in_=bc)
            tok["gback"] = inc(i, "Lmsk", 16)
        blk.sync(_)

    def emit_gamma_finalize(which, nelem, tslot, ntslot):
        wi = {"w1": 0, "w2": 1, "w3": 2}[which]

        def _(v):
            wt(v, "Lmsk", tok["gback"])
            v.tensor_scalar(vs(tslot), gthr_sb[:, wi:wi + 1],
                            0.5 / nelem, 0.5 * FFN_EPS, ALU.mult, ALU.add)
            v.drain()
            i = v.tensor_scalar(vs(ntslot), vs(tslot), -1.0, None, ALU.mult)
            tok[f"gfin{which}"] = inc(i, "V")
        blk.vector(_)

    def emit_wk_dma(ofc):
        slot = ofc % 2

        def _(g):
            if ofc == 0:
                wt(g, "T", tok["hstat3"])
            if ofc >= 2:
                wt(g, "T", tok[f"kmmO2_{ofc-2}"])
            i = g.dma_start(out=wkbf_sb[slot].rearrange("p a b -> p (a b)"), in_=wk_d[ofc])
            tok[f"wkd{ofc}"] = inc(i, f"Lkq{slot}", 16)
        blk.gpsimd(_)

    emit_wk_dma(0)
    emit_wk_dma(1)

    def _(g):
        wt(g, "T", tok["hstat2"])
        for dc in range(DC):
            i = g.dma_start(out=wvbf_sb[:, dc, :], in_=wv_d[dc])
            tok[f"wvd{dc}"] = inc(i, "Lwv0", 16)
    blk.gpsimd(_)

    def emit_kmmH(ofc):
        slot = ofc % 2

        def _(t):
            wt(t, f"Lkq{slot}", tok[f"wkd{ofc}"])
            if ofc < 2:
                wt(t, "V", tok["hnorm3"])
            if ofc >= 2:
                wt(t, "V", tok[f"kevH{ofc-2}"])
            for dc in range(DC):
                i = t.matmul(pkH[slot][:], wkbf_sb[slot][:, dc, :],
                             xnh_sb[:, dc, :], start=(dc == 0), stop=(dc == DC - 1))
            tok[f"kmmH{ofc}"] = inc(i, "T")
        blk.tensor(_)

        def _(v):
            wt(v, "T", tok[f"kmmH{ofc}"])
            i = v.tensor_copy(k_sb[:, ofc, 0:512], pkH[slot][:])
            tok[f"kevH{ofc}"] = inc(i, "V")
        blk.vector(_)

    esKV = ExitStack()
    pkO = [esKV.enter_context(nc.psum_tensor(f"pkO{i}", [128, 1024], F32)) for i in range(2)]

    def emit_kmmO1(ofc):
        slot = ofc % 2

        def _(t):
            if ofc == 0:
                wt(t, "V", tok["ln1nrm0"])
            if ofc >= 2:
                wt(t, "V", tok[f"kevO{ofc-2}"])
            for dc in range(DC):
                i = t.matmul(pkO[slot][:, 0:512], wkbf_sb[slot][:, dc, :],
                             xn_sb[:, dc, 0:512],
                             start=(dc == 0), stop=(dc == DC - 1))
            tok[f"kmmO1_{ofc}"] = inc(i, "T")
        blk.tensor(_)

    def emit_kmmO2(ofc):
        slot = ofc % 2

        def _(t):
            if ofc == 0:
                wt(t, "V", tok["ln1nrm1"])
            for dc in range(DC):
                i = t.matmul(pkO[slot][:, 512:1024], wkbf_sb[slot][:, dc, :],
                             xn_sb[:, dc, 512:1024],
                             start=(dc == 0), stop=(dc == DC - 1))
            tok[f"kmmO2_{ofc}"] = inc(i, "T")
        blk.tensor(_)

        def _(v):
            wt(v, "T", tok[f"kmmO2_{ofc}"])
            i = v.tensor_copy(k_sb[:, ofc, 512:1536], pkO[slot][:])
            tok[f"kevO{ofc}"] = inc(i, "V")
        blk.vector(_)

    emit_kmmH(0)
    emit_kmmH(1)
    emit_kmmO1(0)
    emit_kmmO1(1)
    emit_kmmO2(0)
    emit_kmmO2(1)
    emit_wk_dma(2)
    emit_wk_dma(3)
    emit_kmmH(2)
    emit_kmmH(3)
    emit_kmmO1(2)
    emit_kmmO1(3)
    emit_kmmO2(2)
    emit_kmmO2(3)

    pv = [esKV.enter_context(nc.psum_tensor(f"pv{i}", [128, 512], F32)) for i in range(2)]
    for tc in range(12):
        slot = tc % 2

        def _(t):
            if tc == 0:
                wt(t, "Lwv0", tok[f"wvd{DC-1}"])
            if tc < 4:
                wt(t, "V", tok[f"hnorm{tc}"])
            if tc >= 2:
                wt(t, "V", tok[f"vev{tc-2}"])
            for dc in range(DC):
                if tc < 4:
                    lhs = xnh_sb[:, dc, tc * 128:(tc + 1) * 128]
                else:
                    lhs = xn_sb[:, dc, (tc - 4) * 128:(tc - 3) * 128]
                i = t.matmul(pv[slot][:], lhs, wvbf_sb[:, dc, :],
                             start=(dc == 0), stop=(dc == DC - 1))
            tok[f"vmm{tc}"] = inc(i, "T")
        blk.tensor(_)

        def _(v):
            wt(v, "T", tok[f"vmm{tc}"])
            i = v.tensor_copy(v_sb[:, tc, :], pv[slot][:])
            tok[f"vev{tc}"] = inc(i, "V")
        blk.vector(_)
    esKV.close()
    esKVH.close()

    esB = ExitStack()
    pS = [esB.enter_context(nc.psum_tensor(f"pS{i}", [128, 640], F32)) for i in range(2)]
    pT = [esB.enter_context(nc.psum_tensor(f"pT{i}", [128, 640], BF16)) for i in range(2)]
    pAV = [esB.enter_context(nc.psum_tensor("pAV0", [128, 512], F32))]
    pQ = esB.enter_context(nc.psum_tensor("pQ", [128, 512], F32))

    def _(sync):
        wt(sync, "V", tok["ln1nrm1"])
        i = sync.dma_start(out=hmsk_sb[:], in_=hmsk_d)
        tok["hmskd"] = inc(i, "Lhmsk", 16)
    blk.sync(_)

    exp_par = {0: 0, 1: 0}
    exp_ops = []
    S_PLAN = [CNT["S"]]
    oev_par = {0: 0, 1: 0}
    av_par = {0: 0, 1: 0}
    atev_par = {0: 0, 1: 0}
    last_av = 0
    pending_av = [None]

    def emit_av(t):
        (ph, pb, pavpar, pkvh, pbb) = pending_av[0]
        at = at_sb2[pbb]
        wt(t, "V", tok[f"atev{ph * NBLK + pb}_3"])
        wt(t, "V", oev_par[0])
        i = None
        for c in range(8):
            vt = v_sb[:, 4 * pb + c, pkvh * 128:(pkvh + 1) * 128]
            if c == 0:
                i = t.matmul(pAV[0][:, 0:128], vt, at[:, 0, 0:128],
                             start=True, stop=False)
            elif c < 4:
                i = t.matmul(pAV[0][:, 0:128 * c], vt, at[:, c, 0:128 * c],
                             start=False, stop=False)
                i = t.matmul(pAV[0][:, 128 * c:128 * (c + 1)], vt,
                             at[:, c, 128 * c:128 * (c + 1)],
                             start=False, stop=False)
            else:
                cp = c - 4
                i = t.matmul(pAV[0][:, 128 * cp:512], vt,
                             at[:, c, 128 * cp:512],
                             start=False, stop=(c == 7))
        tok[f"av{ph}_{pb}"] = inc(i, "T")
        av_par[pbb] = tok[f"av{ph}_{pb}"]

    def emit_oev(v):
        (ph, pb, pavpar, pkvh, pbb) = pending_av.pop(0)
        wt(v, "T", tok[f"av{ph}_{pb}"])
        i = v.tensor_copy(o_sb[:, ph, 512 * pb:512 * (pb + 1)], pAV[0][:])
        tok[f"oev{ph}_{pb}"] = inc(i, "V")
        oev_par[0] = tok[f"oev{ph}_{pb}"]

    for h in range(H):
        kvh = h // 4
        hslot = h % 2

        def _(g, h=h, hslot=hslot):
            if h < 2:
                wt(g, "T", tok[f"kmmH{2+h}"])
            else:
                wt(g, "T", tok[f"q1mm{h-2}"])
            i = g.dma_start(out=wqbf_sb[hslot].rearrange("p a b -> p (a b)"),
                            in_=wq_d[h])
            tok[f"wqd{h}"] = inc(i, f"Lkq{hslot}", 16)
        blk.gpsimd(_)

        def emit_qmm(t, half):
            wt(t, f"Lkq{hslot}", tok[f"wqd{h}"])
            if half == 0 and h == 0:
                wt(t, "V", tok["vev11"])
            if half == 0 and h >= 1:
                wt(t, "V", tok[f"qev1_{h-1}"])
            if half == 1:
                wt(t, "V", tok[f"qev0_{h}"])
            for dc in range(DC):
                lhs = wqbf_sb[hslot][:, dc, :]
                i = t.matmul(pQ[:], lhs, xn_sb[:, dc, 512 * half:512 * (half + 1)],
                             start=(dc == 0), stop=(dc == DC - 1))
            tok[f"q{half}mm{h}"] = inc(i, "T")

        def emit_qev(v, half):
            wt(v, "T", tok[f"q{half}mm{h}"])
            if h >= 1:
                wt(v, "T", tok[f"smm{h-1}_{half}_3"])
            i = v.tensor_scalar(qh_sb[half][:], pQ[:], SCALE, None, ALU.mult)
            tok[f"qev{half}_{h}"] = inc(i, "V")

        def _(t, h=h):
            emit_qmm(t, 0)
        blk.tensor(_)

        def _(v, h=h):
            emit_qev(v, 0)
        blk.vector(_)

        for b in range(NBLK):
            avpar = (h * NBLK + b) % 2
            bb = (h * NBLK + b) % 2
            at_sb = at_sb2[bb]

            def emit_smm(t, j):
                par = j % 2
                k0 = 512 * b + 128 * j
                wt(t, "S", exp_par[par], future=True)
                wt(t, "V", tok[f"qev{b}_{h}"])
                if h == 0 and b == 0 and j == 0:
                    wt(t, "V", tok["kevO3"])
                    wt(t, "V", tok["vev11"])
                    wt(t, "Lidn", tok["idnd"])
                    wt(t, "Lhmsk", tok["hmskd"])
                    wt(t, "Lmsk", tok["mskd"])
                lhs = qh_sb[b][:, 128 * j: 128 * (j + 1)]
                t.matmul(pS[par][:, 0:512], lhs, k_sb[:, kvh, k0:k0 + 512],
                         start=True, stop=False)
                t.matmul(pS[par][:, 512:640], lhs, k_sb[:, kvh, k0 + 512:k0 + 640],
                         start=True, stop=False)
                t.matmul(pS[par][:, 0:128], idn_sb[:], msk_sb[:, 0, :],
                         start=False, stop=(b != 0))
                i = t.matmul(pS[par][:, 512:640], idn_sb[:], msk_sb[:, 1, :],
                             start=False, stop=True)
                if b == 0:
                    i = t.matmul(pS[par][:, 0:512], idn_sb[:], hmsk_sb[:, j, :],
                                 start=False, stop=True)
                tok[f"smm{h}_{b}_{j}"] = inc(i, "T")

            def emit_red(v, j):
                par = j % 2
                wt(v, "T", tok[f"smm{h}_{b}_{j}"])
                i = v.tensor_reduce(vs(V_NEGMAX + par), pS[par][:], AXX, ALU.max,
                                    negate=True)
                tok[f"msk{h}_{b}_{j}"] = inc(i, "V")

            def emit_exp(sc_unused, j):
                par = j % 2
                wtok = tok[f"msk{h}_{b}_{j}"]
                S_PLAN[0] += 1
                tok[f"exp{h}_{b}_{j}"] = S_PLAN[0]
                exp_par[par] = S_PLAN[0]

                def op(sc, wtok=wtok, par=par):
                    wt(sc, "V", wtok)
                    i = sc.activation(pexp_sb[:, par, :], pS[par][:], AF.Exp,
                                      bias=vs(V_NEGMAX + par),
                                      accum_out=vs(V_SUMEXP + par))
                    inc(i, "S")
                exp_ops.append(op)

            def emit_recip(v, j):
                par = j % 2
                wt(v, "S", tok[f"exp{h}_{b}_{j}"], future=True)
                i = v.reciprocal(vs(V_RECIP + par), vs(V_SUMEXP + par))
                tok[f"rcp{h}_{b}_{j}"] = inc(i, "V")

            def emit_nrm(v, j):
                par = j % 2
                jj = (h * NBLK + b) * 4 + j
                par4 = jj % 4
                if jj >= 4:
                    wt(v, "T", tok[f"tr{jj-4}"])
                i = v.tensor_scalar(abf_sb[:, par4, :], pexp_sb[:, par, :],
                                    vs(V_RECIP + par), None, ALU.mult)
                tok[f"nrm{jj}"] = inc(i, "V")

            def emit_tr(t, jj):
                par = jj % 2
                wt(t, "V", tok[f"nrm{jj}"])
                wt(t, "V", atev_par[par])
                for c in range(5):
                    i = t.transpose(pT[par][:, c * 128:(c + 1) * 128],
                                    abf_sb[:, jj % 4, c * 128:(c + 1) * 128], idn_sb[:])
                tok[f"tr{jj}"] = inc(i, "T")

            def emit_atev(v, jj):
                par = jj % 2
                j = jj % 4
                blki = jj // 4
                wt(v, "T", tok[f"tr{jj}"])
                if j == 0:
                    wt(v, "T", av_par[blki % 2])
                i = v.tensor_copy(at_sb2[blki % 2][:, j:j + 5, 128 * j:128 * (j + 1)],
                                  pT[par].rearrange("p (c m) -> p c m", c=5))
                tok[f"atev{blki}_{j}"] = inc(i, "V")
                atev_par[par] = tok[f"atev{blki}_{j}"]

            blkid = h * NBLK + b

            def _(t):
                emit_smm(t, 0); emit_smm(t, 1)
                if blkid >= 1:
                    emit_tr(t, (blkid - 1) * 4 + 2)
                    emit_tr(t, (blkid - 1) * 4 + 3)
            blk.tensor(_)

            def _(v):
                emit_red(v, 0); emit_red(v, 1)
                if blkid >= 1:
                    emit_atev(v, (blkid - 1) * 4 + 2)
                    emit_atev(v, (blkid - 1) * 4 + 3)
            blk.vector(_)

            emit_exp(None, 0); emit_exp(None, 1)

            def _(v):
                emit_recip(v, 0); emit_recip(v, 1)
                v.drain()
                emit_nrm(v, 0); emit_nrm(v, 1)
            blk.vector(_)
            gamma_pump(1)

            def _(t):
                emit_smm(t, 2); emit_smm(t, 3)
                emit_tr(t, blkid * 4 + 0); emit_tr(t, blkid * 4 + 1)
                if b == 0:
                    emit_qmm(t, 1)
                if pending_av[0] is not None:
                    emit_av(t)
            blk.tensor(_)

            def _(v):
                emit_red(v, 2); emit_red(v, 3)
                emit_atev(v, blkid * 4 + 0); emit_atev(v, blkid * 4 + 1)
                if b == 0:
                    emit_qev(v, 1)
                if pending_av[0] is not None:
                    emit_oev(v)
            blk.vector(_)

            emit_exp(None, 2); emit_exp(None, 3)

            def _(v):
                emit_recip(v, 2); emit_recip(v, 3)
                v.drain()
                emit_nrm(v, 2); emit_nrm(v, 3)
            blk.vector(_)

            pending_av.insert(0, (h, b, avpar, kvh, bb))

    def _(sc):
        for op in exp_ops:
            op(sc)
    blk.scalar(_)
    assert CNT["S"] == S_PLAN[0], (CNT["S"], S_PLAN[0])

    lastblk = H * NBLK - 1

    for jj in (lastblk * 4 + 2, lastblk * 4 + 3):
        def _(t, jj=jj):
            par = jj % 2
            t.wait_ge(SEMS["V"], tok[f"nrm{jj}"])
            t.wait_ge(SEMS["V"], atev_par[par])
            for c in range(5):
                i = t.transpose(pT[par][:, c * 128:(c + 1) * 128],
                                abf_sb[:, jj % 4, c * 128:(c + 1) * 128], idn_sb[:])
            tok[f"tr{jj}"] = inc(i, "T")
        blk.tensor(_)

        def _(v, jj=jj):
            par = jj % 2
            j = jj % 4
            wt(v, "T", tok[f"tr{jj}"])
            i = v.tensor_copy(at_sb2[lastblk % 2][:, j:j + 5, 128 * j:128 * (j + 1)],
                              pT[par].rearrange("p (c m) -> p c m", c=5))
            tok[f"atev{lastblk}_{j}"] = inc(i, "V")
            atev_par[par] = tok[f"atev{lastblk}_{j}"]
        blk.vector(_)

    def _(t):
        emit_av(t)
    blk.tensor(_)

    def _(v):
        emit_oev(v)
    blk.vector(_)
    last_av = tok[f"av{H-1}_{NBLK-1}"]

    esB.close()

    emit_gamma_reduce_head()
    emit_gamma_allreduce()

    esC2 = ExitStack()
    pstat2_m = esC2.enter_context(nc.psum_tensor("pstat2_m", [128, TOWN], F32))
    pstat2_q = esC2.enter_context(nc.psum_tensor("pstat2_q", [128, TOWN], F32))
    esC = ExitStack()
    pwo = [esC.enter_context(nc.psum_tensor(f"pwo{i}", [128, 1024], F32)) for i in range(2)]

    def c_front(doc):
        slot = doc % 2

        def _(g):
            if doc == 0:
                wt(g, "T", last_av)
            if doc >= 2:
                wt(g, "T", tok[f"womm{doc-2}"])
            i = g.dma_start(out=wobf_sb[slot].rearrange("p a b -> p (a b)"), in_=wo_d[doc])
            tok[f"wod{doc}"] = inc(i, f"Lwo{slot}", 16)
        blk.gpsimd(_)

    def c_body(doc):
        slot = doc % 2

        def _(t):
            wt(t, f"Lwo{slot}", tok[f"wod{doc}"])
            if doc == 0:
                wt(t, "V", tok[f"oev{H-1}_{NBLK-1}"])
            if doc >= 2:
                wt(t, "V", tok[f"res{doc-2}"])
            for fc in range(DC):
                lhs = wobf_sb[slot][:, fc, :]
                t.matmul(pwo[slot][:, 0:512], lhs, o_sb[:, fc, 0:512],
                         start=(fc == 0), stop=(fc == DC - 1))
                i = t.matmul(pwo[slot][:, 512:1024], lhs, o_sb[:, fc, 512:1024],
                             start=(fc == 0), stop=(fc == DC - 1))
            tok[f"womm{doc}"] = inc(i, "T")
        blk.tensor(_)

        def _(v):
            wt(v, "T", tok[f"womm{doc}"])
            i = v.tensor_tensor(x_sb[:, doc, :], x_sb[:, doc, :], pwo[slot][:], ALU.add)
            tok[f"res{doc}"] = inc(i, "V")
        blk.vector(_)

        def _(sc):
            wt(sc, "V", tok[f"res{doc}"])
            i = sc.dma_start(out=xsp_d[:, doc, :], in_=x_sb[:, doc, :])
            tok[f"spill{doc}"] = inc(i, "Lsp", 16)
        blk.scalar(_)

        def _(v):
            if doc >= 2:
                wt(v, "T", tok[f"ln2st{doc-2}"])
            v.drain()
            v.tensor_copy(xbscr2_sb[slot][:], x_sb[:, doc, :])
            v.drain()
            i = v.tensor_tensor(xsqscr2_sb[slot][:], x_sb[:, doc, :],
                                x_sb[:, doc, :], ALU.mult)
            tok[f"ln2xbq{doc}"] = inc(i, "V")
        blk.vector(_)

    def c_stats(doc):
        slot = doc % 2

        def _(t):
            wt(t, "V", tok[f"ln2xbq{doc}"])
            for b2 in range(2):
                sl = slice(b2 * 512, (b2 + 1) * 512)
                t.matmul(pstat2_m[:, sl], ones2d_sb[:], xbscr2_sb[slot][:, sl],
                         start=(doc == 0), stop=(doc == DC - 1))
                i = t.matmul(pstat2_q[:, sl], ones2d_sb[:], xsqscr2_sb[slot][:, sl],
                             start=(doc == 0), stop=(doc == DC - 1))
            tok[f"ln2st{doc}"] = inc(i, "T")
        blk.tensor(_)

    def d1_dma(hc):
        slot = hc % 2

        def _(sync):
            if hc < 2:
                wt(sync, "T", last_av)
            else:
                wt(sync, "V", tok[f"qz1_{hc-2}"])
            i = sync.dma_start(out=w1raw_sb[slot].rearrange("p a b -> p (a b)"), in_=w1_d[hc])
            tok[f"w1d{hc}"] = inc(i, f"Lw1{slot}", 16)
            if hc >= 2:
                wt(sync, "V", tok[f"qz2_{hc-2}"])
            i = sync.dma_start(out=w2raw_sb[slot].rearrange("p a b -> p (a b)"), in_=w2_d[hc])
            tok[f"w2d{hc}"] = inc(i, f"Lw2{slot}", 16)
        blk.sync(_)

    def d1_qz1(hc):
        slot = hc % 2

        def _(v):
            wt(v, f"Lw1{slot}", tok[f"w1d{hc}"])
            if hc == 0:
                wt(v, "V", tok["gfinw1"])
                wt(v, "T", last_av)
            if hc >= 2:
                wt(v, "T", tok[f"x1mm{hc-2}"])
            w1r = w1raw_sb[slot].rearrange("p a b -> p (a b)")
            q1r = qh1_sb[slot].rearrange("p a b -> p (a b)")
            v.drain()
            v.tensor_scalar(ascr_sb[:], w1r, vs(V_T1), None, ALU.is_ge)
            v.drain()
            i = v.scalar_tensor_tensor(q1r, w1r, vs(V_NT1), ascr_sb[:], ALU.is_le,
                                       ALU.subtract)
            tok[f"qz1_{hc}"] = inc(i, "V")
        blk.vector(_)

    def d1_qz2(hc):
        slot = hc % 2

        def _(v):
            wt(v, f"Lw2{slot}", tok[f"w2d{hc}"])
            if hc == 0:
                wt(v, "V", tok["gfinw2"])
            if hc < 2:
                wt(v, "T", tok[f"womm{DC-1}"])
            else:
                wt(v, "T", tok[f"x2mm{hc-2}"])
            w2r = w2raw_sb[slot].rearrange("p a b -> p (a b)")
            q2r = qh2_sb[slot].rearrange("p a b -> p (a b)")
            v.drain()
            v.tensor_scalar(ascr2_sb[:], w2r, vs(V_T2), None, ALU.is_ge)
            v.drain()
            i = v.scalar_tensor_tensor(q2r, w2r, vs(V_NT2), ascr2_sb[:], ALU.is_le,
                                       ALU.subtract)
            tok[f"qz2_{hc}"] = inc(i, "V")
        blk.vector(_)

    c_front(0)
    for doc in range(DC):
        if doc + 1 < DC:
            c_front(doc + 1)
        if doc >= 2:
            c_stats(doc - 2)
        c_body(doc)
        if doc == 8:
            d1_dma(0)
        if doc == 9:
            d1_dma(1)
        if doc == 12:
            emit_gamma_back()
        if doc == 14:
            emit_gamma_finalize("w1", NW12, V_T1, V_NT1)
            emit_gamma_finalize("w2", NW12, V_T2, V_NT2)
            emit_gamma_finalize("w3", NW3, V_T3, V_NT3)
            d1_qz1(0)
            d1_qz1(1)
    c_stats(DC - 2)
    c_stats(DC - 1)

    esC.close()
    emit_ln_finalize(pstat2_m, pstat2_q, TOWN, "ln2")
    emit_ln_normalize(pstat2_q, x_sb, xn_sb, TOWN, "ln2", wide=True,
                      between=lambda b2: d1_qz2(b2))
    esC2.close()

    def _(sync):
        wt(sync, "V", tok[f"gv{len(gsched)-1}"])
        i = sync.dma_start(out=w3pre_sb.rearrange("p a b -> p (a b)"), in_=w3_d[0])
        tok["w3pre"] = inc(i, "Lw30", 16)
    blk.sync(_)

    esD1 = ExitStack()
    px1 = [esD1.enter_context(nc.psum_tensor(f"px1{i}", [128, 1024], F32)) for i in range(2)]
    px2 = [esD1.enter_context(nc.psum_tensor(f"px2{i}", [128, 1024], F32)) for i in range(2)]

    def d1_body(hc):
        slot = hc % 2

        def _(t):
            wt(t, "V", tok[f"qz1_{hc}"])
            if hc >= 2:
                wt(t, "S", tok[f"sig{hc-2}"], future=True)
            for dc in range(DC):
                lhs = qh1_sb[slot][:, dc, :]
                t.matmul(px1[slot][:, 0:512], lhs, xn_sb[:, dc, 0:512],
                         start=(dc == 0), stop=(dc == DC - 1))
                i = t.matmul(px1[slot][:, 512:1024], lhs, xn_sb[:, dc, 512:1024],
                             start=(dc == 0), stop=(dc == DC - 1))
            tok[f"x1mm{hc}"] = inc(i, "T")
        blk.tensor(_)

        def _(t):
            wt(t, "V", tok[f"qz2_{hc}"])
            if hc >= 2:
                wt(t, "V", tok[f"hid{hc-2}"])
            for dc in range(DC):
                lhs = qh2_sb[slot][:, dc, :]
                t.matmul(px2[slot][:, 0:512], lhs, xn_sb[:, dc, 0:512],
                         start=(dc == 0), stop=(dc == DC - 1))
                i = t.matmul(px2[slot][:, 512:1024], lhs, xn_sb[:, dc, 512:1024],
                             start=(dc == 0), stop=(dc == DC - 1))
            tok[f"x2mm{hc}"] = inc(i, "T")
        blk.tensor(_)

        waits = [("T", tok[f"x1mm{hc}"])]
        if hc >= 2:
            waits.append(("V", tok[f"hid{hc-2}"]))
        collect_sig(hc, waits)

        def _(v):
            wt(v, "T", tok[f"x2mm{hc}"])
            wt(v, "S", tok[f"sig{hc}"], future=True)
            if hc == 0:
                wt(v, "Lsp", tok[f"spill{DC-1}"])
            i = v.scalar_tensor_tensor(hid_sb[:, hc, :], px2[slot][:], -1.0,
                                       sig_sb[slot][:], ALU.mult, ALU.mult)
            tok[f"hid{hc}"] = inc(i, "V")
        blk.vector(_)

    def _(t):
        wt(t, "V", tok["qz1_0"])
        wt(t, "V", tok["ln2nrm0"])
        for dc in range(DC):
            i = t.matmul(px1[0][:, 0:512], qh1_sb[0][:, dc, :],
                         xn_sb[:, dc, 0:512], start=(dc == 0), stop=(dc == DC - 1))
        tok["x1mm0a"] = inc(i, "T")
    blk.tensor(_)

    def _(t):
        wt(t, "V", tok["qz1_1"])
        for dc in range(DC):
            i = t.matmul(px1[1][:, 0:512], qh1_sb[1][:, dc, :],
                         xn_sb[:, dc, 0:512], start=(dc == 0), stop=(dc == DC - 1))
        tok["x1mm1a"] = inc(i, "T")
    blk.tensor(_)

    def _(t):
        wt(t, "V", tok["ln2nrm1"])
        for dc in range(DC):
            i = t.matmul(px1[0][:, 512:1024], qh1_sb[0][:, dc, :],
                         xn_sb[:, dc, 512:1024], start=(dc == 0), stop=(dc == DC - 1))
        tok["x1mm0"] = inc(i, "T")
    blk.tensor(_)

    def _(t):
        for dc in range(DC):
            i = t.matmul(px1[1][:, 512:1024], qh1_sb[1][:, dc, :],
                         xn_sb[:, dc, 512:1024], start=(dc == 0), stop=(dc == DC - 1))
        tok["x1mm1"] = inc(i, "T")
    blk.tensor(_)

    sig_ops = []
    SIG_PLAN = [CNT["S"]]

    def collect_sig(hc, waits):
        SIG_PLAN[0] += 1
        tok[f"sig{hc}"] = SIG_PLAN[0]
        slot = hc % 2

        def op(sc, hc=hc, slot=slot, waits=tuple(waits)):
            for semn, tv in waits:
                wt(sc, semn, tv, future=True)
            i = sc.activation(sig_sb[slot][:], px1[slot][:], AF.Sigmoid, scale=-1.0)
            inc(i, "S")
        sig_ops.append(op)

    collect_sig(0, [("T", tok["x1mm0"])])

    def _(t):
        wt(t, "V", tok["qz2_0"])
        for dc in range(DC):
            lhs = qh2_sb[0][:, dc, :]
            t.matmul(px2[0][:, 0:512], lhs, xn_sb[:, dc, 0:512],
                     start=(dc == 0), stop=(dc == DC - 1))
            i = t.matmul(px2[0][:, 512:1024], lhs, xn_sb[:, dc, 512:1024],
                         start=(dc == 0), stop=(dc == DC - 1))
        tok["x2mm0"] = inc(i, "T")
    blk.tensor(_)

    collect_sig(1, [("T", tok["x1mm1"])])

    def _(t):
        wt(t, "V", tok["qz2_1"])
        for dc in range(DC):
            lhs = qh2_sb[1][:, dc, :]
            t.matmul(px2[1][:, 0:512], lhs, xn_sb[:, dc, 0:512],
                     start=(dc == 0), stop=(dc == DC - 1))
            i = t.matmul(px2[1][:, 512:1024], lhs, xn_sb[:, dc, 512:1024],
                         start=(dc == 0), stop=(dc == DC - 1))
        tok["x2mm1"] = inc(i, "T")
    blk.tensor(_)

    for hc in range(2):
        def _(v, hc=hc):
            wt(v, "T", tok[f"x2mm{hc}"])
            wt(v, "S", tok[f"sig{hc}"], future=True)
            if hc == 0:
                wt(v, "Lsp", tok[f"spill{DC-1}"])
            i = v.scalar_tensor_tensor(hid_sb[:, hc, :], px2[hc][:], -1.0,
                                       sig_sb[hc][:], ALU.mult, ALU.mult)
            tok[f"hid{hc}"] = inc(i, "V")
        blk.vector(_)

    d1_dma(2)
    d1_dma(3)
    d1_qz1(2)
    d1_qz2(2)
    d1_qz1(3)
    d1_qz2(3)
    for hc in range(2, FHC):
        if hc + 2 < FHC:
            d1_dma(hc + 2)
        d1_body(hc)
        if hc + 2 < FHC:
            d1_qz1(hc + 2)
            d1_qz2(hc + 2)

    def _(sc):
        for op in sig_ops:
            op(sc)
    blk.scalar(_)
    assert CNT["S"] == SIG_PLAN[0], (CNT["S"], SIG_PLAN[0])
    esD1.close()


    esD2 = ExitStack()
    pout = [esD2.enter_context(nc.psum_tensor(f"pout{i}", [128, 1024], F32)) for i in range(2)]
    pending_stores = []

    def d2_front(doc):
        slot = doc % 2
        pstore = pending_stores.pop(0) if pending_stores else None

        def _(sync):
            if doc < 2:
                wt(sync, "T", tok[f"x2mm{FHC-1}"])
                wt(sync, "V", tok[f"qz1_{FHC-1}"])
            else:
                wt(sync, "V", tok[f"qz3b_{doc-2}"])
            if doc != 0:
                i = sync.dma_start(out=w3raw_sb[slot].rearrange("p a b -> p (a b)"),
                                   in_=w3_d[doc])
                tok[f"w3d{doc}"] = inc(i, f"Lw3{slot}", 16)
            if pstore is not None:
                dprev, sprev = pstore
                wt(sync, "V", tok[f"fin{dprev}"])
                i = sync.dma_start(out=y_d[:, dprev, :], in_=outst_sb[sprev][:])
                tok[f"store{dprev}"] = inc(i, f"Lst{sprev}", 16)
        blk.sync(_)

        def _(sc):
            if doc < 2:
                wt(sc, "T", tok[f"x2mm{FHC-1}"])
            if doc >= 2:
                wt(sc, "V", tok[f"fin{doc-2}"])
            i = sc.dma_start(out=xrld_sb[slot][:], in_=xsp_d[:, doc, :])
            tok[f"xr{doc}"] = inc(i, f"Lxr{slot}", 16)
        blk.scalar(_)

        HB = FHC // 2 * 128
        w3src = w3pre_sb if doc == 0 else w3raw_sb[slot]
        w3r = w3src.rearrange("p a b -> p (a b)")
        q3r = qh3_sb[slot].rearrange("p a b -> p (a b)")

        def _(v):
            if doc == 0:
                wt(v, "Lw30", tok["w3pre"])
                wt(v, "V", tok["gfinw3"])
            else:
                wt(v, f"Lw3{slot}", tok[f"w3d{doc}"])
            if doc < 2:
                wt(v, "T", tok[f"x2mm{FHC-1}"])
            else:
                wt(v, "T", tok[f"omma_{doc-2}"])
            v.drain()
            v.tensor_scalar(a3scr_sb[:, 0:HB], w3r[:, 0:HB], vs(V_T3), None, ALU.is_ge)
            v.drain()
            i = v.scalar_tensor_tensor(q3r[:, 0:HB], w3r[:, 0:HB], vs(V_NT3),
                                       a3scr_sb[:, 0:HB], ALU.is_le, ALU.subtract)
            tok[f"qz3a_{doc}"] = inc(i, "V")
        blk.vector(_)

        def _(v):
            if doc >= 2:
                wt(v, "T", tok[f"omm{doc-2}"])
            v.drain()
            v.tensor_scalar(a3scr_sb[:, HB:], w3r[:, HB:], vs(V_T3), None, ALU.is_ge)
            v.drain()
            i = v.scalar_tensor_tensor(q3r[:, HB:], w3r[:, HB:], vs(V_NT3),
                                       a3scr_sb[:, HB:], ALU.is_le, ALU.subtract)
            tok[f"qz3b_{doc}"] = inc(i, "V")
        blk.vector(_)

    def d2_body(doc):
        slot = doc % 2

        def _(t):
            wt(t, "V", tok[f"qz3a_{doc}"])
            if doc == 0:
                wt(t, "V", tok[f"hid{FHC-1}"])
            if doc >= 2:
                wt(t, "V", tok[f"fin{doc-2}"])
            for hc in range(FHC // 2):
                lhs = qh3_sb[slot][:, hc, :]
                t.matmul(pout[slot][:, 0:512], lhs, hid_sb[:, hc, 0:512],
                         start=(hc == 0), stop=False)
                i = t.matmul(pout[slot][:, 512:1024], lhs, hid_sb[:, hc, 512:1024],
                             start=(hc == 0), stop=False)
            tok[f"omma_{doc}"] = inc(i, "T")
        blk.tensor(_)

        def _(t):
            wt(t, "V", tok[f"qz3b_{doc}"])
            for hc in range(FHC // 2, FHC):
                lhs = qh3_sb[slot][:, hc, :]
                t.matmul(pout[slot][:, 0:512], lhs, hid_sb[:, hc, 0:512],
                         start=False, stop=(hc == FHC - 1))
                i = t.matmul(pout[slot][:, 512:1024], lhs, hid_sb[:, hc, 512:1024],
                             start=False, stop=(hc == FHC - 1))
            tok[f"omm{doc}"] = inc(i, "T")
        blk.tensor(_)

        def _(v):
            wt(v, "T", tok[f"omm{doc}"])
            wt(v, f"Lxr{slot}", tok[f"xr{doc}"])
            if doc >= 2:
                wt(v, f"Lst{slot}", tok[f"store{doc-2}"])
            i = v.tensor_tensor(outst_sb[slot][:], xrld_sb[slot][:], pout[slot][:],
                                ALU.subtract)
            tok[f"fin{doc}"] = inc(i, "V")
        blk.vector(_)
        pending_stores.append((doc, slot))

    d2_front(0)
    for doc in range(DC):
        if doc + 1 < DC:
            d2_front(doc + 1)
        d2_body(doc)

    def _(sync):
        for dprev, sprev in pending_stores:
            wt(sync, "V", tok[f"fin{dprev}"])
            i = sync.dma_start(out=y_d[:, dprev, :], in_=outst_sb[sprev][:])
            tok[f"store{dprev}"] = inc(i, f"Lst{sprev}", 16)
        sync.wait_ge(SEMS["Lst0"], CNT["Lst0"])
        sync.wait_ge(SEMS["Lst1"], CNT["Lst1"])
    blk.sync(_)
    esD2.close()

    assert gstate["pos"] == len(gsched), gstate["pos"]
    es_top.close()
    return nc



def _stage_inputs(x, Wq, Wk, Wv, Wo, w1, w2, w3):
    x = np.ascontiguousarray(np.asarray(x, dtype=np.float32))
    xf = np.ascontiguousarray(x.reshape(8192, D).T)

    wq_t = np.ascontiguousarray(
        np.asarray(Wq, np.float32).reshape(DC, 128, H, 128).transpose(2, 1, 0, 3).reshape(H, 128, D))
    wk_t = np.ascontiguousarray(
        np.asarray(Wk, np.float32).reshape(DC, 128, 4, 128).transpose(2, 1, 0, 3).reshape(4, 128, D))
    wv_t = np.ascontiguousarray(np.asarray(Wv, np.float32).reshape(DC, 128, 512))
    wo_t = np.ascontiguousarray(
        np.asarray(Wo, np.float32).reshape(DC, 128, DC, 128).transpose(2, 1, 0, 3).reshape(DC, 128, D))
    w1_t = np.ascontiguousarray(
        np.asarray(w1, np.float32).reshape(FHC, 128, DC, 128).transpose(0, 3, 2, 1).reshape(FHC, 128, D))
    w2_t = np.ascontiguousarray(
        np.asarray(w2, np.float32).reshape(FHC, 128, DC, 128).transpose(0, 3, 2, 1).reshape(FHC, 128, D))
    w3_t = np.ascontiguousarray(
        np.asarray(w3, np.float32).reshape(DC, 128, FHC, 128).transpose(0, 3, 2, 1).reshape(DC, 128, FH))

    r = np.arange(128)[:, None]
    c = np.arange(128)[None, :]
    msk = np.zeros((128, 2, 128), np.float32)
    msk[:, 0, :] = np.where(c > r, 0.0, NEG)
    msk[:, 1, :] = np.where(c <= r, 0.0, NEG)
    msk = msk.astype(BF)
    idn = np.eye(128, dtype=BF)

    w12v = np.concatenate([
        w1_t.reshape(FHC, 128, 2, 1024).transpose(0, 2, 1, 3).reshape(2 * FHC, 128, 1024),
        w2_t.reshape(FHC, 128, 2, 1024).transpose(0, 2, 1, 3).reshape(2 * FHC, 128, 1024)])
    w3v = w3_t.reshape(DC, 128, 4, 1408).transpose(0, 2, 1, 3).reshape(4 * DC, 128, 1408)

    hm0 = np.zeros((128, 4, 512), np.float32)
    for j in range(4):
        kk = np.arange(512)[None, :]
        hm0[:, j, :] = np.where(kk < 512 - 128 * j, NEG, 0.0)
    hm0 = hm0.astype(BF)
    hmz = np.zeros((128, 4, 512), BF)

    in_maps = []
    for core in range(8):
        t0 = core * TOWN
        x_own = np.ascontiguousarray(
            xf[:, t0:t0 + TOWN].reshape(DC, 128, TOWN).transpose(1, 0, 2))
        if core % 4 == 0:
            halo = np.zeros((D, THALO), np.float32)
        else:
            halo = xf[:, t0 - THALO:t0]
        x_halo = np.ascontiguousarray(
            halo.reshape(DC, 128, 4, 128).transpose(2, 1, 0, 3).reshape(4, 128, D))
        gsh12 = np.ascontiguousarray(np.concatenate(
            [w12v[0:88][core::8], w12v[88:176][core::8]]))
        gsh3 = np.ascontiguousarray(w3v[core::8])
        in_maps.append(dict(
            x_own=x_own, x_halo=x_halo,
            wq=wq_t, wk=wk_t, wv=wv_t, wo=wo_t,
            w1=w1_t, w2=w2_t, w3=w3_t,
            gsh12=gsh12, gsh3=gsh3,
            msk=msk, hmsk=(hm0 if core % 4 == 0 else hmz), idn=idn,
        ))
    return in_maps


_PROG_CACHE = {}


def kernel(L0_attn_x, L0_attn_Wq, L0_attn_Wk, L0_attn_Wv, L0_attn_Wo,
           L0_ffn_w1, L0_ffn_w2, L0_ffn_w3):
    in_maps = _stage_inputs(L0_attn_x, L0_attn_Wq, L0_attn_Wk, L0_attn_Wv,
                            L0_attn_Wo, L0_ffn_w1, L0_ffn_w2, L0_ffn_w3)
    if "nc" not in _PROG_CACHE:
        _PROG_CACHE["nc"] = build_program()
    nc = _PROG_CACHE["nc"]
    res = run_bass_kernel_spmd(nc, in_maps, core_ids=list(range(8)))

    out = np.empty((D, 8192), np.float32)
    for core in range(8):
        shard = np.asarray(res.results[core]["y"])
        out[:, core * TOWN:(core + 1) * TOWN] = shard.transpose(1, 0, 2).reshape(D, TOWN)
    return np.ascontiguousarray(out.T.reshape(2, 4096, D))


# version 24 (final)
# speedup vs baseline: 1.0209x; 1.0059x over previous
import numpy as np
import ml_dtypes
import concourse.bass as bass
import concourse.mybir as mybir
from concourse.bass_utils import run_bass_kernel_spmd
from concourse.alu_op_type import AluOpType as ALU

from contextlib import ExitStack

F32 = mybir.dt.float32
BF16 = mybir.dt.bfloat16
AF = mybir.ActivationFunctionType
AXX = mybir.AxisListType.X
BF = ml_dtypes.bfloat16

D = 2048
DC = 16
H = 16
HD = 128
FH = 5632
FHC = 44
TOWN = 1024
THALO = 512
TTOT = 1536
NBLK = 2
SCALE = 1.0 / np.sqrt(128.0)
NEG = -1e30
LN_EPS = 1e-5
FFN_EPS = 1e-8
NW12 = FH * D
NW3 = D * FH

MBASE = 16512
SBUF_LIMIT = 212832


def build_program():
    nc = bass.Bass("TRN2", target_bir_lowering=False)

    def din(name, shape, dt=F32):
        return nc.dram_tensor(name, shape, dt, kind="ExternalInput").ap()

    x_own_d = din("x_own", [128, DC, TOWN])
    x_halo_d = din("x_halo", [4, 128, D])
    wq_d = din("wq", [H, 128, D])
    wk_d = din("wk", [4, 128, D])
    wv_d = din("wv", [DC, 128, 512])
    wo_d = din("wo", [DC, 128, D])
    w1_d = din("w1", [FHC, 128, D])
    w2_d = din("w2", [FHC, 128, D])
    w3_d = din("w3", [DC, 128, FH])
    msk_d = din("msk", [128, 2, 128], BF16)
    hmsk_d = din("hmsk", [128, 4, 512], BF16)
    idn_d = din("idn", [128, 128], BF16)

    y_d = nc.dram_tensor("y", [128, DC, TOWN], F32, kind="ExternalOutput").ap()
    xsp_d = nc.dram_tensor("xsp", [128, DC, TOWN], F32, kind="Internal").ap()

    gsh12_d = din("gsh12", [22, 128, 1024])
    gsh3_d = din("gsh3", [8, 128, 1408])
    gbin_d = nc.dram_tensor("gbin", [1, 4], F32, kind="Internal").ap()
    gbout_d = nc.dram_tensor("gbout", [1, 4], F32, kind="Internal").ap()

    def sb(name, shape, dt, off):
        elts = 1
        for s in shape[1:]:
            elts *= s
        nbytes = elts * (4 if dt == F32 else 2)
        assert off + nbytes <= SBUF_LIMIT, (name, off, nbytes, off + nbytes)
        return nc.alloc_sbuf_tensor_at(name, shape, dt, offset=MBASE + off).ap()

    x_sb = sb("x", [128, DC, TOWN], F32, 0)
    hid_sb = sb("hid", [128, FHC, TOWN], BF16, 0)
    haloraw_sb = [sb(f"hraw{i}", [128, DC, 128], F32, 65536 + i * 8192) for i in range(2)]
    xnh_sb = sb("xnh", [128, DC, THALO], BF16, 81920)
    wqbf_sb = [sb(f"wqbf{i}", [128, DC, 128], BF16, 188416 + i * 4096) for i in range(2)]
    o_sb = sb("o", [128, DC, TOWN], BF16, 65536)
    outst_sb = [sb(f"outst{i}", [128, TOWN], F32, 90112 + i * 4096) for i in range(2)]
    xn_sb = sb("xn", [128, DC, TOWN], BF16, 98304)
    at_sb2 = [sb(f"at{i}", [128, 8, 512], BF16, 131072 + i * 8192) for i in range(2)]
    pexp_sb = sb("pexp", [128, 2, 640], F32, 147456)
    abf_sb = sb("abf", [128, 4, 640], BF16, 152576)
    hmsk_sb = sb("hmsk", [128, 4, 512], BF16, 157696)
    xrld_sb = [sb(f"xrld{i}", [128, TOWN], F32, 98304 + i * 4096) for i in range(2)]
    xbscr2_sb = [sb(f"xbscr2_{i}", [128, TOWN], BF16, 106496 + i * 2048) for i in range(2)]
    xsqscr2_sb = [sb(f"xsqscr2_{i}", [128, TOWN], BF16, 110592 + i * 2048) for i in range(2)]
    hxb_sb = [sb(f"hxb{i}", [128, D], BF16, (131072, 147456)[i]) for i in range(2)]
    hxq_sb = [sb(f"hxq{i}", [128, D], BF16, (135168, 151552)[i]) for i in range(2)]
    wvbf_sb = sb("wvbf", [128, DC, 512], BF16, 131072)
    wkbf_sb = [sb(f"wkbf{i}", [128, DC, 128], BF16, 147456 + i * 4096) for i in range(2)]
    qh_sb = [sb(f"qh_{i}", [128, 512], BF16, 161792 + i * 1024) for i in range(2)]
    w1raw_sb = [sb(f"w1raw{i}", [128, DC, 128], F32, 131072 + i * 8192) for i in range(2)]
    qh1_sb = [sb(f"qh1{i}", [128, DC, 128], BF16, 147456 + i * 4096) for i in range(2)]
    ascr_sb = sb("ascr", [128, D], BF16, 155648)
    ascr2_sb = sb("ascr2", [128, D], BF16, 90112)
    sig_sb = [sb(f"sig{i}", [128, TOWN], BF16, 159744 + i * 2048) for i in range(2)]
    k_sb = sb("k", [128, 4, TTOT], BF16, 163840)
    wobf_sb = [sb(f"wobf{i}", [128, DC, 128], BF16, 180224 + i * 4096) for i in range(2)]
    v_sb = sb("v", [128, 12, 512], BF16, 176128)
    w2raw_sb = [sb(f"w2raw{i}", [128, DC, 128], F32, 163840 + i * 8192) for i in range(2)]
    qh2_sb = [sb(f"qh2{i}", [128, DC, 128], BF16, 180224 + i * 4096) for i in range(2)]
    gsh_sb = [sb(f"gsh{i}", [128, 1408], F32, (196608, 205312)[i]) for i in range(2)]
    w3raw_sb = [sb(f"w3raw{i}", [128, FHC, 128], F32, 106496 + i * 22528) for i in range(2)]
    qh3_sb = [sb(f"qh3{i}", [128, FHC, 128], BF16, 151552 + i * 11264) for i in range(2)]
    a3scr_sb = sb("a3scr", [128, FH], BF16, 174080)
    msk_sb = sb("msk", [128, 2, 128], BF16, 204800)
    mrow_sb = sb("mrow", [128, TOWN], F32, 155648)
    sqrow_sb = sb("sqrow", [128, TOWN], F32, 159744)
    idn_sb = sb("idn", [128, 128], BF16, 211488)
    ones_sb = sb("ones", [128, 1], BF16, 211744)
    onesf_sb = sb("onesf", [128, 1], F32, 211776)
    onesrow_sb = sb("onesrow", [1, 128], F32, 211808)
    vecs_sb = sb("vecs", [128, 24], F32, 212320)
    gsum_sb = sb("gsum", [1, 4], F32, 212416)
    gsuma_sb = sb("gsuma", [1, 4], F32, 212448)
    ones2d_sb = sb("ones2d", [128, 128], BF16, 212480)
    gthr_sb = sb("gthr", [128, 4], F32, 212736)
    w3pre_sb = sb("w3pre", [128, FHC, 128], F32, 188416)

    def vs(i):
        return vecs_sb[:, i:i + 1]
    V_MAX, V_NEGMAX, V_SUMEXP, V_RECIP = 0, 2, 4, 6
    V_T1, V_NT1, V_T2, V_NT2, V_T3, V_NT3 = 8, 9, 10, 11, 12, 13
    V_GACC = {"w1": 14, "w2": 15, "w3": 16}
    V_GTMP = 17
    V_EPS = 18
    V_TG = 19

    es_top = ExitStack()
    SEMS = {n: es_top.enter_context(nc.semaphore(n))
            for n in ("T", "V", "S", "Lx0", "Lx1", "Lx2", "Lx3", "Lmsk", "Lidn", "Lhmsk",
                      "Lh0", "Lh1", "Lwv0", "Lwv1", "Lkq0", "Lkq1",
                      "Lwo0", "Lwo1", "Lw10", "Lw11", "Lw20", "Lw21",
                      "Lw30", "Lw31", "Lg0", "Lg1", "Lsp",
                      "Lxr0", "Lxr1", "Lst0", "Lst1", "G")}
    CNT = {k: 0 for k in SEMS}

    def inc(instr, name, k=1):
        instr.then_inc(SEMS[name], k)
        CNT[name] += k
        return CNT[name]

    def wt(eng, name, tokv, future=False):
        assert future or tokv <= CNT[name], (name, tokv, CNT[name])
        if tokv > 0:
            eng.wait_ge(SEMS[name], tokv)

    blk = es_top.enter_context(nc.Block())
    tok = {}

    def bcast(ap, n):
        return bass.AP(tensor=ap.tensor, offset=ap.offset,
                       ap=[list(ap.ap[0]), [0, n], list(ap.ap[1])])

    def _(sync):
        for s in range(2):
            i = sync.dma_start(out=haloraw_sb[s].rearrange("p a b -> p (a b)"),
                               in_=x_halo_d[s])
            tok[f"hraw{s}"] = inc(i, f"Lh{s}", 16)
        for g in range(4):
            i = sync.dma_start(out=x_sb[:, 4 * g:4 * (g + 1), :], in_=x_own_d[:, 4 * g:4 * (g + 1), :])
            tok[f"xg{g}"] = inc(i, f"Lx{g}", 16)
        i = sync.dma_start(out=msk_sb[:], in_=msk_d)
        tok["mskd"] = inc(i, "Lmsk", 16)
        i = sync.dma_start(out=idn_sb[:], in_=idn_d)
        tok["idnd"] = inc(i, "Lidn", 16)
    blk.sync(_)

    def _(v):
        v.memset(ones_sb[:], 1.0)
        v.memset(onesf_sb[:], 1.0)
        v.memset(onesrow_sb[:], 1.0)
        v.memset(ones2d_sb[:], 1.0)
        v.memset(vecs_sb[:, 0:V_EPS], 0.0)
        v.memset(vecs_sb[:, V_EPS + 1:], 0.0)
        v.memset(vecs_sb[:, V_EPS:V_EPS + 1], LN_EPS)
        i = v.memset(gsum_sb[0:1, 3:4], 0.0)
        tok["vinit"] = inc(i, "V")
    blk.vector(_)


    def emit_ln_finalize(pst_m, pst_q, ntok, name, extra=()):
        def _(v):
            wt(v, "T", tok[f"{name}st{DC-1}"])
            for semn, tv in extra:
                wt(v, semn, tv)
            v.drain()
            v.tensor_scalar(mrow_sb[:, 0:ntok], pst_m[:], 1.0 / D, None, ALU.mult)
            v.drain()
            v.tensor_tensor(sqrow_sb[:, 0:ntok], mrow_sb[:, 0:ntok],
                            mrow_sb[:, 0:ntok], ALU.mult)
            v.drain()
            i = v.scalar_tensor_tensor(pst_q[:], pst_q[:], 1.0 / D,
                                       sqrow_sb[:, 0:ntok], ALU.mult, ALU.subtract)
            tok[f"{name}var"] = inc(i, "V")
        blk.vector(_)

    def emit_ln_normalize(pst_q, xsrc, xdst, ntok, name, wide=False, between=None):
        esn = ExitStack()
        pw = ntok if wide else 512
        pa = esn.enter_context(nc.psum_tensor(f"pa_{name}", [128, pw], F32))
        pb = esn.enter_context(nc.psum_tensor(f"pb_{name}", [128, pw], F32))
        if wide:
            def _(sc):
                wt(sc, "V", tok[f"{name}var"])
                i = None
                for b2 in range(ntok // 512):
                    sl = slice(b2 * 512, (b2 + 1) * 512)
                    i = sc.activation(pa[:, sl], pst_q[:, sl], AF.Sqrt,
                                      bias=vecs_sb[:, V_EPS:V_EPS + 1])
                for b2 in range(ntok // 512):
                    tok[f"{name}sd{b2}"] = CNT["S"] + 1
                inc(i, "S")
            blk.scalar(_)
        for b2 in range(ntok // 512):
            sl = slice(b2 * 512, (b2 + 1) * 512)
            pl = sl if wide else slice(0, 512)

            if not wide:
                def _(sc):
                    if b2 == 0:
                        wt(sc, "V", tok[f"{name}var"])
                    else:
                        wt(sc, "V", tok[f"{name}nrm{b2-1}"])
                    i = sc.activation(pa[:], pst_q[:, sl], AF.Sqrt,
                                      bias=vecs_sb[:, V_EPS:V_EPS + 1])
                    tok[f"{name}sd{b2}"] = inc(i, "S")
                blk.scalar(_)

            def _(v):
                wt(v, "S", tok[f"{name}sd{b2}"])
                v.reciprocal(pa[:, pl], pa[:, pl])
                v.drain()
                v.scalar_tensor_tensor(pb[:, pl], mrow_sb[:, sl], -1.0, pa[:, pl],
                                       ALU.mult, ALU.mult)
                v.drain()
                dst = xdst[:, :, sl]
                v.tensor_tensor(dst, xsrc[:, :, sl], bcast(pa[:, pl], DC), ALU.mult)
                v.drain()
                i = v.tensor_tensor(dst, dst, bcast(pb[:, pl], DC), ALU.add)
                v.drain()
                tok[f"{name}nrm{b2}"] = inc(i, "V")
            blk.vector(_)
            if between is not None:
                between(b2)
        esn.close()

    esKVH = ExitStack()
    pkH = [esKVH.enter_context(nc.psum_tensor(f"pkH{i}", [128, 512], F32)) for i in range(2)]
    esA2 = ExitStack()
    pstat_m = esA2.enter_context(nc.psum_tensor("pstat_m", [128, TOWN], F32))
    pstat_q = esA2.enter_context(nc.psum_tensor("pstat_q", [128, TOWN], F32))
    esA1 = ExitStack()
    pstH_m = esA1.enter_context(nc.psum_tensor("pstH_m", [128, 128], F32))
    pstH_q = esA1.enter_context(nc.psum_tensor("pstH_q", [128, 128], F32))

    def h_front(s):
        slot = s % 2
        if s >= 2:
            def _(sync):
                wt(sync, "V", tok[f"hnorm{s-2}"])
                i = sync.dma_start(out=haloraw_sb[slot].rearrange("p a b -> p (a b)"),
                                   in_=x_halo_d[s])
                tok[f"hraw{s}"] = inc(i, f"Lh{slot}", 16)
            blk.sync(_)

        def _(v):
            wt(v, f"Lh{slot}", tok[f"hraw{s}"])
            if s >= 2:
                wt(v, "T", tok[f"hstat{s-2}"])
            raw = haloraw_sb[slot].rearrange("p a b -> p (a b)")
            v.tensor_copy(hxb_sb[slot][:], raw)
            i = v.tensor_tensor(hxq_sb[slot][:], raw, raw, ALU.mult)
            tok[f"hxbq{s}"] = inc(i, "V")
        blk.vector(_)

    def h_stats(s):
        slot = s % 2

        def _(t):
            wt(t, "V", tok[f"hxbq{s}"])
            if s >= 1:
                wt(t, "V", tok[f"hnorm{s-1}"])
            for dc in range(DC):
                t.matmul(pstH_m[:], ones2d_sb[:], hxb_sb[slot][:, dc * 128:(dc + 1) * 128],
                         start=(dc == 0), stop=(dc == DC - 1))
            for dc in range(DC):
                i = t.matmul(pstH_q[:], ones2d_sb[:], hxq_sb[slot][:, dc * 128:(dc + 1) * 128],
                             start=(dc == 0), stop=(dc == DC - 1))
            tok[f"hstat{s}"] = inc(i, "T")
        blk.tensor(_)

    def h_fin(s):
        def _(v):
            wt(v, "T", tok[f"hstat{s}"])
            v.drain()
            v.tensor_scalar(mrow_sb[:, 0:128], pstH_m[:], 1.0 / D, None, ALU.mult)
            v.drain()
            v.tensor_tensor(sqrow_sb[:, 0:128], mrow_sb[:, 0:128],
                            mrow_sb[:, 0:128], ALU.mult)
            v.drain()
            i = v.scalar_tensor_tensor(pstH_q[:], pstH_q[:], 1.0 / D,
                                       sqrow_sb[:, 0:128], ALU.mult, ALU.subtract)
            tok[f"hvar{s}"] = inc(i, "V")
        blk.vector(_)

        def _(sc):
            wt(sc, "V", tok[f"hvar{s}"])
            i = sc.activation(pstH_m[:], pstH_q[:], AF.Sqrt,
                              bias=vecs_sb[:, V_EPS:V_EPS + 1])
            tok[f"hsd{s}"] = inc(i, "S")
        blk.scalar(_)

    def h_norm(s):
        slot = s % 2

        def _(v):
            wt(v, "S", tok[f"hsd{s}"])
            v.reciprocal(pstH_m[:], pstH_m[:])
            v.drain()
            v.scalar_tensor_tensor(pstH_q[:], mrow_sb[:, 0:128], -1.0,
                                   pstH_m[:], ALU.mult, ALU.mult)
            v.drain()
            dst = xnh_sb[:, :, s * 128:(s + 1) * 128]
            v.tensor_tensor(dst, haloraw_sb[slot][:], bcast(pstH_m[:], DC), ALU.mult)
            v.drain()
            i = v.tensor_tensor(dst, dst, bcast(pstH_q[:], DC), ALU.add)
            v.drain()
            tok[f"hnorm{s}"] = inc(i, "V")
        blk.vector(_)

    def a2_cast(dc):
        slot = dc % 2

        def _(v):
            wt(v, f"Lx{dc // 4}", tok[f"xg{dc // 4}"])
            if dc >= 2:
                wt(v, "T", tok[f"ln1st{dc-2}"])
            v.tensor_copy(xbscr2_sb[slot][:], x_sb[:, dc, :])
            i = v.tensor_tensor(xsqscr2_sb[slot][:], x_sb[:, dc, :], x_sb[:, dc, :],
                                ALU.mult)
            tok[f"ln1xbq{dc}"] = inc(i, "V")
        blk.vector(_)

    def a2_stats(dc):
        slot = dc % 2

        def _(t):
            wt(t, "V", tok[f"ln1xbq{dc}"])
            for b2 in range(2):
                sl = slice(b2 * 512, (b2 + 1) * 512)
                t.matmul(pstat_m[:, sl], ones2d_sb[:], xbscr2_sb[slot][:, sl],
                         start=(dc == 0), stop=(dc == DC - 1))
                i = t.matmul(pstat_q[:, sl], ones2d_sb[:], xsqscr2_sb[slot][:, sl],
                             start=(dc == 0), stop=(dc == DC - 1))
            tok[f"ln1st{dc}"] = inc(i, "T")
        blk.tensor(_)

    h_front(0); h_front(1)
    halo_stages = []
    for s in range(4):
        halo_stages.append(lambda s=s: h_stats(s))
        halo_stages.append(lambda s=s: h_fin(s))
        if s + 2 < 4:
            halo_stages.append(lambda s=s: (h_norm(s), h_front(s + 2)))
        else:
            halo_stages.append(lambda s=s: h_norm(s))
    for dc in range(DC):
        a2_cast(dc)
        a2_stats(dc)
        if halo_stages:
            halo_stages.pop(0)()
    while halo_stages:
        halo_stages.pop(0)()
    esA1.close()
    emit_ln_finalize(pstat_m, pstat_q, TOWN, "ln1", extra=[("S", tok["hsd3"])])
    emit_ln_normalize(pstat_q, x_sb, xn_sb, TOWN, "ln1")
    esA2.close()

    gsched = ([("w1", i, 1024) for i in range(11)]
              + [("w2", i, 1024) for i in range(11)]
              + [("w3", i, 1408) for i in range(8)])
    gstate = {"pos": 0, "guards": [], "guardn": 0}

    def gamma_pump(n):
        for _n in range(n):
            gi = gstate["pos"]
            if gi >= len(gsched):
                return
            gstate["pos"] += 1
            which, ich, width = gsched[gi]
            slot = gi % 2
            src = gsh3_d[ich] if which == "w3" else gsh12_d[gi]

            def _(sync):
                wt(sync, "V", tok.get(f"gv{gi-2}", 0))
                i = sync.dma_start(out=gsh_sb[slot][:, 0:width], in_=src)
                tok[f"gd{gi}"] = inc(i, f"Lg{slot}", 16)
            blk.sync(_)

            def _(v):
                wt(v, f"Lg{slot}", tok[f"gd{gi}"])
                v.drain()
                v.tensor_reduce(vs(V_GTMP), gsh_sb[slot][:, 0:width], AXX, ALU.add,
                                apply_absolute_value=True)
                v.drain()
                acc = V_GACC[which]
                i = v.tensor_tensor(vs(acc), vs(acc), vs(V_GTMP), ALU.add)
                tok[f"gv{gi}"] = inc(i, "V")
            blk.vector(_)

    def emit_gamma_reduce_head():
        def _(sync):
            wt(sync, "V", tok[f"gv{len(gsched)-1}"])
            wt(sync, "T", last_av)
            i = sync.dma_start(out=mrow_sb[0:1, 0:384], in_=vecs_sb[:, V_GACC["w1"]:V_GACC["w1"] + 3])
            tok["grow"] = inc(i, "Lmsk", 16)
        blk.sync(_)

        def _(v):
            wt(v, "Lmsk", tok["grow"])
            i = None
            for wi in range(3):
                src = bass.AP(tensor=mrow_sb.tensor, offset=mrow_sb.offset + wi,
                              ap=[[mrow_sb.ap[0][0], 1], [3, 128]])
                i = v.tensor_reduce(gsum_sb[0:1, wi:wi + 1], src, AXX, ALU.add)
            tok["gsum"] = inc(i, "V")
        blk.vector(_)

        def _(sync):
            wt(sync, "V", tok["gsum"])
            i = sync.dma_start(out=gbin_d, in_=gsum_sb[0:1, :])
            tok["gbin"] = inc(i, "Lmsk", 16)
        blk.sync(_)

    def emit_gamma_allreduce():
        def _(g):
            wt(g, "Lmsk", tok["gbin"])
            i = g.collective_compute(
                "AllReduce", ALU.add,
                replica_groups=[list(range(8))],
                ins=[gbin_d.opt()], outs=[gbout_d.opt()])
            tok["gcc"] = inc(i, "G")
        blk.gpsimd(_)

    def emit_gamma_back():
        def _(sync):
            wt(sync, "G", tok["gcc"])
            bc = bass.AP(tensor=gbout_d.tensor, offset=gbout_d.offset,
                         ap=[[0, 128], [1, 4]])
            i = sync.dma_start(out=gthr_sb[:], in_=bc)
            tok["gback"] = inc(i, "Lmsk", 16)
        blk.sync(_)

    def emit_gamma_finalize(which, nelem, tslot, ntslot):
        wi = {"w1": 0, "w2": 1, "w3": 2}[which]

        def _(v):
            wt(v, "Lmsk", tok["gback"])
            v.tensor_scalar(vs(tslot), gthr_sb[:, wi:wi + 1],
                            0.5 / nelem, 0.5 * FFN_EPS, ALU.mult, ALU.add)
            v.drain()
            i = v.tensor_scalar(vs(ntslot), vs(tslot), -1.0, None, ALU.mult)
            tok[f"gfin{which}"] = inc(i, "V")
        blk.vector(_)

    def emit_wk_dma(ofc):
        slot = ofc % 2

        def _(g):
            if ofc == 0:
                wt(g, "T", tok["hstat3"])
            if ofc >= 2:
                wt(g, "T", tok[f"kmmO2_{ofc-2}"])
            i = g.dma_start(out=wkbf_sb[slot].rearrange("p a b -> p (a b)"), in_=wk_d[ofc])
            tok[f"wkd{ofc}"] = inc(i, f"Lkq{slot}", 16)
        blk.gpsimd(_)

    emit_wk_dma(0)
    emit_wk_dma(1)

    def _(g):
        wt(g, "T", tok["hstat2"])
        for dc in range(DC):
            i = g.dma_start(out=wvbf_sb[:, dc, :], in_=wv_d[dc])
            tok[f"wvd{dc}"] = inc(i, "Lwv0", 16)
    blk.gpsimd(_)

    def emit_kmmH(ofc):
        slot = ofc % 2

        def _(t):
            wt(t, f"Lkq{slot}", tok[f"wkd{ofc}"])
            if ofc < 2:
                wt(t, "V", tok["hnorm3"])
            if ofc >= 2:
                wt(t, "V", tok[f"kevH{ofc-2}"])
            for dc in range(DC):
                i = t.matmul(pkH[slot][:], wkbf_sb[slot][:, dc, :],
                             xnh_sb[:, dc, :], start=(dc == 0), stop=(dc == DC - 1))
            tok[f"kmmH{ofc}"] = inc(i, "T")
        blk.tensor(_)

        def _(v):
            wt(v, "T", tok[f"kmmH{ofc}"])
            i = v.tensor_copy(k_sb[:, ofc, 0:512], pkH[slot][:])
            tok[f"kevH{ofc}"] = inc(i, "V")
        blk.vector(_)

    esKV = ExitStack()
    pkO = [esKV.enter_context(nc.psum_tensor(f"pkO{i}", [128, 1024], F32)) for i in range(2)]

    def emit_kmmO1(ofc):
        slot = ofc % 2

        def _(t):
            if ofc == 0:
                wt(t, "V", tok["ln1nrm0"])
            if ofc >= 2:
                wt(t, "V", tok[f"kevO{ofc-2}"])
            for dc in range(DC):
                i = t.matmul(pkO[slot][:, 0:512], wkbf_sb[slot][:, dc, :],
                             xn_sb[:, dc, 0:512],
                             start=(dc == 0), stop=(dc == DC - 1))
            tok[f"kmmO1_{ofc}"] = inc(i, "T")
        blk.tensor(_)

    def emit_kmmO2(ofc):
        slot = ofc % 2

        def _(t):
            if ofc == 0:
                wt(t, "V", tok["ln1nrm1"])
            for dc in range(DC):
                i = t.matmul(pkO[slot][:, 512:1024], wkbf_sb[slot][:, dc, :],
                             xn_sb[:, dc, 512:1024],
                             start=(dc == 0), stop=(dc == DC - 1))
            tok[f"kmmO2_{ofc}"] = inc(i, "T")
        blk.tensor(_)

        def _(v):
            wt(v, "T", tok[f"kmmO2_{ofc}"])
            i = v.tensor_copy(k_sb[:, ofc, 512:1536], pkO[slot][:])
            tok[f"kevO{ofc}"] = inc(i, "V")
        blk.vector(_)

    emit_kmmH(0)
    emit_kmmH(1)
    emit_kmmO1(0)
    emit_kmmO1(1)
    emit_kmmO2(0)
    emit_kmmO2(1)
    emit_wk_dma(2)
    emit_wk_dma(3)
    emit_kmmH(2)
    emit_kmmH(3)
    emit_kmmO1(2)
    emit_kmmO1(3)
    emit_kmmO2(2)
    emit_kmmO2(3)

    pv = [esKV.enter_context(nc.psum_tensor(f"pv{i}", [128, 512], F32)) for i in range(2)]
    for tc in range(12):
        slot = tc % 2

        def _(t):
            if tc == 0:
                wt(t, "Lwv0", tok[f"wvd{DC-1}"])
            if tc < 4:
                wt(t, "V", tok[f"hnorm{tc}"])
            if tc >= 2:
                wt(t, "V", tok[f"vev{tc-2}"])
            for dc in range(DC):
                if tc < 4:
                    lhs = xnh_sb[:, dc, tc * 128:(tc + 1) * 128]
                else:
                    lhs = xn_sb[:, dc, (tc - 4) * 128:(tc - 3) * 128]
                i = t.matmul(pv[slot][:], lhs, wvbf_sb[:, dc, :],
                             start=(dc == 0), stop=(dc == DC - 1))
            tok[f"vmm{tc}"] = inc(i, "T")
        blk.tensor(_)

        def _(v):
            wt(v, "T", tok[f"vmm{tc}"])
            i = v.tensor_copy(v_sb[:, tc, :], pv[slot][:])
            tok[f"vev{tc}"] = inc(i, "V")
        blk.vector(_)
    esKV.close()
    esKVH.close()

    esB = ExitStack()
    pS = [esB.enter_context(nc.psum_tensor(f"pS{i}", [128, 640], F32)) for i in range(2)]
    pT = [esB.enter_context(nc.psum_tensor(f"pT{i}", [128, 640], BF16)) for i in range(2)]
    pAV = [esB.enter_context(nc.psum_tensor("pAV0", [128, 512], F32))]
    pQ = esB.enter_context(nc.psum_tensor("pQ", [128, 512], F32))

    def _(sync):
        wt(sync, "V", tok["ln1nrm1"])
        i = sync.dma_start(out=hmsk_sb[:], in_=hmsk_d)
        tok["hmskd"] = inc(i, "Lhmsk", 16)
    blk.sync(_)

    exp_par = {0: 0, 1: 0}
    exp_ops = []
    S_PLAN = [CNT["S"]]
    oev_par = {0: 0, 1: 0}
    av_par = {0: 0, 1: 0}
    atev_par = {0: 0, 1: 0}
    last_av = 0
    pending_av = [None]

    def emit_av(t):
        (ph, pb, pavpar, pkvh, pbb) = pending_av[0]
        at = at_sb2[pbb]
        wt(t, "V", tok[f"atev{ph * NBLK + pb}_3"])
        wt(t, "V", oev_par[0])
        i = None
        for c in range(8):
            vt = v_sb[:, 4 * pb + c, pkvh * 128:(pkvh + 1) * 128]
            if c == 0:
                i = t.matmul(pAV[0][:, 0:128], vt, at[:, 0, 0:128],
                             start=True, stop=False)
            elif c < 4:
                i = t.matmul(pAV[0][:, 0:128 * c], vt, at[:, c, 0:128 * c],
                             start=False, stop=False)
                i = t.matmul(pAV[0][:, 128 * c:128 * (c + 1)], vt,
                             at[:, c, 128 * c:128 * (c + 1)],
                             start=False, stop=False)
            else:
                cp = c - 4
                i = t.matmul(pAV[0][:, 128 * cp:512], vt,
                             at[:, c, 128 * cp:512],
                             start=False, stop=(c == 7))
        tok[f"av{ph}_{pb}"] = inc(i, "T")
        av_par[pbb] = tok[f"av{ph}_{pb}"]

    def emit_oev(v):
        (ph, pb, pavpar, pkvh, pbb) = pending_av.pop(0)
        wt(v, "T", tok[f"av{ph}_{pb}"])
        i = v.tensor_copy(o_sb[:, ph, 512 * pb:512 * (pb + 1)], pAV[0][:])
        tok[f"oev{ph}_{pb}"] = inc(i, "V")
        oev_par[0] = tok[f"oev{ph}_{pb}"]

    for h in range(H):
        kvh = h // 4
        hslot = h % 2

        def _(g, h=h, hslot=hslot):
            if h < 2:
                wt(g, "T", tok[f"kmmH{2+h}"])
            else:
                wt(g, "T", tok[f"q1mm{h-2}"])
            i = g.dma_start(out=wqbf_sb[hslot].rearrange("p a b -> p (a b)"),
                            in_=wq_d[h])
            tok[f"wqd{h}"] = inc(i, f"Lkq{hslot}", 16)
        blk.gpsimd(_)

        def emit_qmm(t, half):
            wt(t, f"Lkq{hslot}", tok[f"wqd{h}"])
            if half == 0 and h == 0:
                wt(t, "V", tok["vev11"])
            if half == 0 and h >= 1:
                wt(t, "V", tok[f"qev1_{h-1}"])
            if half == 1:
                wt(t, "V", tok[f"qev0_{h}"])
            for dc in range(DC):
                lhs = wqbf_sb[hslot][:, dc, :]
                i = t.matmul(pQ[:], lhs, xn_sb[:, dc, 512 * half:512 * (half + 1)],
                             start=(dc == 0), stop=(dc == DC - 1))
            tok[f"q{half}mm{h}"] = inc(i, "T")

        def emit_qev(v, half):
            wt(v, "T", tok[f"q{half}mm{h}"])
            if h >= 1:
                wt(v, "T", tok[f"smm{h-1}_{half}_3"])
            i = v.tensor_scalar(qh_sb[half][:], pQ[:], SCALE, None, ALU.mult)
            tok[f"qev{half}_{h}"] = inc(i, "V")

        def _(t, h=h):
            emit_qmm(t, 0)
        blk.tensor(_)

        def _(v, h=h):
            emit_qev(v, 0)
        blk.vector(_)

        for b in range(NBLK):
            avpar = (h * NBLK + b) % 2
            bb = (h * NBLK + b) % 2
            at_sb = at_sb2[bb]

            def emit_smm(t, j):
                par = j % 2
                k0 = 512 * b + 128 * j
                wt(t, "S", exp_par[par], future=True)
                wt(t, "V", tok[f"qev{b}_{h}"])
                if h == 0 and b == 0 and j == 0:
                    wt(t, "V", tok["kevO3"])
                    wt(t, "V", tok["vev11"])
                    wt(t, "Lidn", tok["idnd"])
                    wt(t, "Lhmsk", tok["hmskd"])
                    wt(t, "Lmsk", tok["mskd"])
                lhs = qh_sb[b][:, 128 * j: 128 * (j + 1)]
                t.matmul(pS[par][:, 0:512], lhs, k_sb[:, kvh, k0:k0 + 512],
                         start=True, stop=False)
                t.matmul(pS[par][:, 512:640], lhs, k_sb[:, kvh, k0 + 512:k0 + 640],
                         start=True, stop=False)
                t.matmul(pS[par][:, 0:128], idn_sb[:], msk_sb[:, 0, :],
                         start=False, stop=(b != 0))
                i = t.matmul(pS[par][:, 512:640], idn_sb[:], msk_sb[:, 1, :],
                             start=False, stop=True)
                if b == 0:
                    i = t.matmul(pS[par][:, 0:512], idn_sb[:], hmsk_sb[:, j, :],
                                 start=False, stop=True)
                tok[f"smm{h}_{b}_{j}"] = inc(i, "T")

            def emit_red(v, j):
                par = j % 2
                wt(v, "T", tok[f"smm{h}_{b}_{j}"])
                i = v.tensor_reduce(vs(V_NEGMAX + par), pS[par][:], AXX, ALU.max,
                                    negate=True)
                tok[f"msk{h}_{b}_{j}"] = inc(i, "V")

            def emit_exp(sc_unused, j):
                par = j % 2
                wtok = tok[f"msk{h}_{b}_{j}"]
                S_PLAN[0] += 1
                tok[f"exp{h}_{b}_{j}"] = S_PLAN[0]
                exp_par[par] = S_PLAN[0]

                def op(sc, wtok=wtok, par=par):
                    wt(sc, "V", wtok)
                    i = sc.activation(pexp_sb[:, par, :], pS[par][:], AF.Exp,
                                      bias=vs(V_NEGMAX + par),
                                      accum_out=vs(V_SUMEXP + par))
                    inc(i, "S")
                exp_ops.append(op)

            def emit_recip(v, j):
                par = j % 2
                wt(v, "S", tok[f"exp{h}_{b}_{j}"], future=True)
                i = v.reciprocal(vs(V_RECIP + par), vs(V_SUMEXP + par))
                tok[f"rcp{h}_{b}_{j}"] = inc(i, "V")

            def emit_nrm(v, j):
                par = j % 2
                jj = (h * NBLK + b) * 4 + j
                par4 = jj % 4
                if jj >= 4:
                    wt(v, "T", tok[f"tr{jj-4}"])
                i = v.tensor_scalar(abf_sb[:, par4, :], pexp_sb[:, par, :],
                                    vs(V_RECIP + par), None, ALU.mult)
                tok[f"nrm{jj}"] = inc(i, "V")

            def emit_tr(t, jj):
                par = jj % 2
                wt(t, "V", tok[f"nrm{jj}"])
                wt(t, "V", atev_par[par])
                for c in range(5):
                    i = t.transpose(pT[par][:, c * 128:(c + 1) * 128],
                                    abf_sb[:, jj % 4, c * 128:(c + 1) * 128], idn_sb[:])
                tok[f"tr{jj}"] = inc(i, "T")

            def emit_atev(v, jj):
                par = jj % 2
                j = jj % 4
                blki = jj // 4
                wt(v, "T", tok[f"tr{jj}"])
                if j == 0:
                    wt(v, "T", av_par[blki % 2])
                i = v.tensor_copy(at_sb2[blki % 2][:, j:j + 5, 128 * j:128 * (j + 1)],
                                  pT[par].rearrange("p (c m) -> p c m", c=5))
                tok[f"atev{blki}_{j}"] = inc(i, "V")
                atev_par[par] = tok[f"atev{blki}_{j}"]

            blkid = h * NBLK + b

            def _(t):
                emit_smm(t, 0); emit_smm(t, 1)
                if blkid >= 1:
                    emit_tr(t, (blkid - 1) * 4 + 2)
                    emit_tr(t, (blkid - 1) * 4 + 3)
            blk.tensor(_)

            def _(v):
                emit_red(v, 0); emit_red(v, 1)
                if blkid >= 1:
                    emit_atev(v, (blkid - 1) * 4 + 2)
                    emit_atev(v, (blkid - 1) * 4 + 3)
            blk.vector(_)

            emit_exp(None, 0); emit_exp(None, 1)

            def _(v):
                emit_recip(v, 0); emit_recip(v, 1)
                v.drain()
                emit_nrm(v, 0); emit_nrm(v, 1)
            blk.vector(_)
            gamma_pump(1)

            def _(t):
                emit_smm(t, 2); emit_smm(t, 3)
                emit_tr(t, blkid * 4 + 0); emit_tr(t, blkid * 4 + 1)
                if b == 0:
                    emit_qmm(t, 1)
                if pending_av[0] is not None:
                    emit_av(t)
            blk.tensor(_)

            def _(v):
                emit_red(v, 2); emit_red(v, 3)
                emit_atev(v, blkid * 4 + 0); emit_atev(v, blkid * 4 + 1)
                if b == 0:
                    emit_qev(v, 1)
                if pending_av[0] is not None:
                    emit_oev(v)
            blk.vector(_)

            emit_exp(None, 2); emit_exp(None, 3)

            def _(v):
                emit_recip(v, 2); emit_recip(v, 3)
                v.drain()
                emit_nrm(v, 2); emit_nrm(v, 3)
            blk.vector(_)

            pending_av.insert(0, (h, b, avpar, kvh, bb))

    def _(sc):
        for op in exp_ops:
            op(sc)
    blk.scalar(_)
    assert CNT["S"] == S_PLAN[0], (CNT["S"], S_PLAN[0])

    lastblk = H * NBLK - 1

    for jj in (lastblk * 4 + 2, lastblk * 4 + 3):
        def _(t, jj=jj):
            par = jj % 2
            t.wait_ge(SEMS["V"], tok[f"nrm{jj}"])
            t.wait_ge(SEMS["V"], atev_par[par])
            for c in range(5):
                i = t.transpose(pT[par][:, c * 128:(c + 1) * 128],
                                abf_sb[:, jj % 4, c * 128:(c + 1) * 128], idn_sb[:])
            tok[f"tr{jj}"] = inc(i, "T")
        blk.tensor(_)

        def _(v, jj=jj):
            par = jj % 2
            j = jj % 4
            wt(v, "T", tok[f"tr{jj}"])
            i = v.tensor_copy(at_sb2[lastblk % 2][:, j:j + 5, 128 * j:128 * (j + 1)],
                              pT[par].rearrange("p (c m) -> p c m", c=5))
            tok[f"atev{lastblk}_{j}"] = inc(i, "V")
            atev_par[par] = tok[f"atev{lastblk}_{j}"]
        blk.vector(_)

    def _(t):
        emit_av(t)
    blk.tensor(_)

    def _(v):
        emit_oev(v)
    blk.vector(_)
    last_av = tok[f"av{H-1}_{NBLK-1}"]

    esB.close()

    emit_gamma_reduce_head()
    emit_gamma_allreduce()

    esC2 = ExitStack()
    pstat2_m = esC2.enter_context(nc.psum_tensor("pstat2_m", [128, TOWN], F32))
    pstat2_q = esC2.enter_context(nc.psum_tensor("pstat2_q", [128, TOWN], F32))
    esC = ExitStack()
    pwo = [esC.enter_context(nc.psum_tensor(f"pwo{i}", [128, 1024], F32)) for i in range(2)]

    def c_front(doc):
        slot = doc % 2

        def _(g):
            if doc == 0:
                wt(g, "T", last_av)
            if doc >= 2:
                wt(g, "T", tok[f"womm{doc-2}"])
            i = g.dma_start(out=wobf_sb[slot].rearrange("p a b -> p (a b)"), in_=wo_d[doc])
            tok[f"wod{doc}"] = inc(i, f"Lwo{slot}", 16)
        blk.gpsimd(_)

    def c_body(doc):
        slot = doc % 2

        def _(t):
            wt(t, f"Lwo{slot}", tok[f"wod{doc}"])
            if doc == 0:
                wt(t, "V", tok[f"oev{H-1}_{NBLK-1}"])
            if doc >= 2:
                wt(t, "V", tok[f"res{doc-2}"])
            for fc in range(DC):
                lhs = wobf_sb[slot][:, fc, :]
                t.matmul(pwo[slot][:, 0:512], lhs, o_sb[:, fc, 0:512],
                         start=(fc == 0), stop=(fc == DC - 1))
                i = t.matmul(pwo[slot][:, 512:1024], lhs, o_sb[:, fc, 512:1024],
                             start=(fc == 0), stop=(fc == DC - 1))
            tok[f"womm{doc}"] = inc(i, "T")
        blk.tensor(_)

        def _(v):
            wt(v, "T", tok[f"womm{doc}"])
            i = v.tensor_tensor(x_sb[:, doc, :], x_sb[:, doc, :], pwo[slot][:], ALU.add)
            tok[f"res{doc}"] = inc(i, "V")
        blk.vector(_)

        def _(sc):
            wt(sc, "V", tok[f"res{doc}"])
            i = sc.dma_start(out=xsp_d[:, doc, :], in_=x_sb[:, doc, :])
            tok[f"spill{doc}"] = inc(i, "Lsp", 16)
        blk.scalar(_)

        def _(v):
            if doc >= 2:
                wt(v, "T", tok[f"ln2st{doc-2}"])
            v.drain()
            v.tensor_copy(xbscr2_sb[slot][:], x_sb[:, doc, :])
            v.drain()
            i = v.tensor_tensor(xsqscr2_sb[slot][:], x_sb[:, doc, :],
                                x_sb[:, doc, :], ALU.mult)
            tok[f"ln2xbq{doc}"] = inc(i, "V")
        blk.vector(_)

    def c_stats(doc):
        slot = doc % 2

        def _(t):
            wt(t, "V", tok[f"ln2xbq{doc}"])
            for b2 in range(2):
                sl = slice(b2 * 512, (b2 + 1) * 512)
                t.matmul(pstat2_m[:, sl], ones2d_sb[:], xbscr2_sb[slot][:, sl],
                         start=(doc == 0), stop=(doc == DC - 1))
                i = t.matmul(pstat2_q[:, sl], ones2d_sb[:], xsqscr2_sb[slot][:, sl],
                             start=(doc == 0), stop=(doc == DC - 1))
            tok[f"ln2st{doc}"] = inc(i, "T")
        blk.tensor(_)

    def d1_dma(hc):
        slot = hc % 2

        def _(sync):
            if hc < 2:
                wt(sync, "T", last_av)
            else:
                wt(sync, "V", tok[f"qz1_{hc-2}"])
            i = sync.dma_start(out=w1raw_sb[slot].rearrange("p a b -> p (a b)"), in_=w1_d[hc])
            tok[f"w1d{hc}"] = inc(i, f"Lw1{slot}", 16)
            if hc >= 2:
                wt(sync, "V", tok[f"qz2_{hc-2}"])
            i = sync.dma_start(out=w2raw_sb[slot].rearrange("p a b -> p (a b)"), in_=w2_d[hc])
            tok[f"w2d{hc}"] = inc(i, f"Lw2{slot}", 16)
        blk.sync(_)

    def d1_qz1(hc):
        slot = hc % 2

        def _(v):
            wt(v, f"Lw1{slot}", tok[f"w1d{hc}"])
            if hc == 0:
                wt(v, "V", tok["gfinw1"])
                wt(v, "T", last_av)
            if hc >= 2:
                wt(v, "T", tok[f"x1mm{hc-2}"])
            w1r = w1raw_sb[slot].rearrange("p a b -> p (a b)")
            q1r = qh1_sb[slot].rearrange("p a b -> p (a b)")
            v.drain()
            v.tensor_scalar(ascr_sb[:], w1r, vs(V_T1), None, ALU.is_ge)
            v.drain()
            i = v.scalar_tensor_tensor(q1r, w1r, vs(V_NT1), ascr_sb[:], ALU.is_le,
                                       ALU.subtract)
            tok[f"qz1_{hc}"] = inc(i, "V")
        blk.vector(_)

    def d1_qz2(hc):
        slot = hc % 2

        def _(v):
            wt(v, f"Lw2{slot}", tok[f"w2d{hc}"])
            if hc == 0:
                wt(v, "V", tok["gfinw2"])
            if hc < 2:
                wt(v, "T", tok[f"womm{DC-1}"])
            else:
                wt(v, "T", tok[f"x2mm{hc-2}"])
            w2r = w2raw_sb[slot].rearrange("p a b -> p (a b)")
            q2r = qh2_sb[slot].rearrange("p a b -> p (a b)")
            v.drain()
            v.tensor_scalar(ascr2_sb[:], w2r, vs(V_T2), None, ALU.is_ge)
            v.drain()
            i = v.scalar_tensor_tensor(q2r, w2r, vs(V_NT2), ascr2_sb[:], ALU.is_le,
                                       ALU.subtract)
            tok[f"qz2_{hc}"] = inc(i, "V")
        blk.vector(_)

    c_front(0)
    for doc in range(DC):
        if doc + 1 < DC:
            c_front(doc + 1)
        if doc >= 2:
            c_stats(doc - 2)
        c_body(doc)
        if doc == 8:
            d1_dma(0)
        if doc == 9:
            d1_dma(1)
        if doc == 12:
            emit_gamma_back()
        if doc == 14:
            emit_gamma_finalize("w1", NW12, V_T1, V_NT1)
            emit_gamma_finalize("w2", NW12, V_T2, V_NT2)
            emit_gamma_finalize("w3", NW3, V_T3, V_NT3)
            d1_qz1(0)
            d1_qz1(1)
    c_stats(DC - 2)
    c_stats(DC - 1)

    esC.close()
    emit_ln_finalize(pstat2_m, pstat2_q, TOWN, "ln2")
    emit_ln_normalize(pstat2_q, x_sb, xn_sb, TOWN, "ln2", wide=True,
                      between=lambda b2: d1_qz2(b2))
    esC2.close()

    def _(sync):
        wt(sync, "V", tok[f"gv{len(gsched)-1}"])
        i = sync.dma_start(out=w3pre_sb.rearrange("p a b -> p (a b)"), in_=w3_d[0])
        tok["w3pre"] = inc(i, "Lw30", 16)
    blk.sync(_)

    esD1 = ExitStack()
    px1 = [esD1.enter_context(nc.psum_tensor(f"px1{i}", [128, 1024], F32)) for i in range(2)]
    px2 = [esD1.enter_context(nc.psum_tensor(f"px2{i}", [128, 1024], F32)) for i in range(2)]

    def d1_body(hc):
        slot = hc % 2

        def _(t):
            wt(t, "V", tok[f"qz1_{hc}"])
            if hc >= 2:
                wt(t, "S", tok[f"sig{hc-2}"], future=True)
            for dc in range(DC):
                lhs = qh1_sb[slot][:, dc, :]
                t.matmul(px1[slot][:, 0:512], lhs, xn_sb[:, dc, 0:512],
                         start=(dc == 0), stop=(dc == DC - 1))
                i = t.matmul(px1[slot][:, 512:1024], lhs, xn_sb[:, dc, 512:1024],
                             start=(dc == 0), stop=(dc == DC - 1))
            tok[f"x1mm{hc}"] = inc(i, "T")
        blk.tensor(_)

        def _(t):
            wt(t, "V", tok[f"qz2_{hc}"])
            if hc >= 2:
                wt(t, "V", tok[f"hid{hc-2}"])
            for dc in range(DC):
                lhs = qh2_sb[slot][:, dc, :]
                t.matmul(px2[slot][:, 0:512], lhs, xn_sb[:, dc, 0:512],
                         start=(dc == 0), stop=(dc == DC - 1))
                i = t.matmul(px2[slot][:, 512:1024], lhs, xn_sb[:, dc, 512:1024],
                             start=(dc == 0), stop=(dc == DC - 1))
            tok[f"x2mm{hc}"] = inc(i, "T")
        blk.tensor(_)

        waits = [("T", tok[f"x1mm{hc}"])]
        if hc >= 2:
            waits.append(("V", tok[f"hid{hc-2}"]))
        collect_sig(hc, waits)

        def _(v):
            wt(v, "T", tok[f"x2mm{hc}"])
            wt(v, "S", tok[f"sig{hc}"], future=True)
            if hc == 0:
                wt(v, "Lsp", tok[f"spill{DC-1}"])
            i = v.scalar_tensor_tensor(hid_sb[:, hc, :], px2[slot][:], -1.0,
                                       sig_sb[slot][:], ALU.mult, ALU.mult)
            tok[f"hid{hc}"] = inc(i, "V")
        blk.vector(_)

    def _(t):
        wt(t, "V", tok["qz1_0"])
        wt(t, "V", tok["ln2nrm0"])
        for dc in range(DC):
            i = t.matmul(px1[0][:, 0:512], qh1_sb[0][:, dc, :],
                         xn_sb[:, dc, 0:512], start=(dc == 0), stop=(dc == DC - 1))
        tok["x1mm0a"] = inc(i, "T")
    blk.tensor(_)

    def _(t):
        wt(t, "V", tok["qz1_1"])
        for dc in range(DC):
            i = t.matmul(px1[1][:, 0:512], qh1_sb[1][:, dc, :],
                         xn_sb[:, dc, 0:512], start=(dc == 0), stop=(dc == DC - 1))
        tok["x1mm1a"] = inc(i, "T")
    blk.tensor(_)

    def _(t):
        wt(t, "V", tok["ln2nrm1"])
        for dc in range(DC):
            i = t.matmul(px1[0][:, 512:1024], qh1_sb[0][:, dc, :],
                         xn_sb[:, dc, 512:1024], start=(dc == 0), stop=(dc == DC - 1))
        tok["x1mm0"] = inc(i, "T")
    blk.tensor(_)

    def _(t):
        for dc in range(DC):
            i = t.matmul(px1[1][:, 512:1024], qh1_sb[1][:, dc, :],
                         xn_sb[:, dc, 512:1024], start=(dc == 0), stop=(dc == DC - 1))
        tok["x1mm1"] = inc(i, "T")
    blk.tensor(_)

    sig_ops = []
    SIG_PLAN = [CNT["S"]]

    def collect_sig(hc, waits):
        SIG_PLAN[0] += 1
        tok[f"sig{hc}"] = SIG_PLAN[0]
        slot = hc % 2

        def op(sc, hc=hc, slot=slot, waits=tuple(waits)):
            for semn, tv in waits:
                wt(sc, semn, tv, future=True)
            i = sc.activation(sig_sb[slot][:], px1[slot][:], AF.Sigmoid, scale=-1.0)
            inc(i, "S")
        sig_ops.append(op)

    collect_sig(0, [("T", tok["x1mm0"])])

    def _(t):
        wt(t, "V", tok["qz2_0"])
        for dc in range(DC):
            lhs = qh2_sb[0][:, dc, :]
            t.matmul(px2[0][:, 0:512], lhs, xn_sb[:, dc, 0:512],
                     start=(dc == 0), stop=(dc == DC - 1))
            i = t.matmul(px2[0][:, 512:1024], lhs, xn_sb[:, dc, 512:1024],
                         start=(dc == 0), stop=(dc == DC - 1))
        tok["x2mm0"] = inc(i, "T")
    blk.tensor(_)

    collect_sig(1, [("T", tok["x1mm1"])])

    def _(t):
        wt(t, "V", tok["qz2_1"])
        for dc in range(DC):
            lhs = qh2_sb[1][:, dc, :]
            t.matmul(px2[1][:, 0:512], lhs, xn_sb[:, dc, 0:512],
                     start=(dc == 0), stop=(dc == DC - 1))
            i = t.matmul(px2[1][:, 512:1024], lhs, xn_sb[:, dc, 512:1024],
                         start=(dc == 0), stop=(dc == DC - 1))
        tok["x2mm1"] = inc(i, "T")
    blk.tensor(_)

    for hc in range(2):
        def _(v, hc=hc):
            wt(v, "T", tok[f"x2mm{hc}"])
            wt(v, "S", tok[f"sig{hc}"], future=True)
            if hc == 0:
                wt(v, "Lsp", tok[f"spill{DC-1}"])
            i = v.scalar_tensor_tensor(hid_sb[:, hc, :], px2[hc][:], -1.0,
                                       sig_sb[hc][:], ALU.mult, ALU.mult)
            tok[f"hid{hc}"] = inc(i, "V")
        blk.vector(_)

    d1_dma(2)
    d1_dma(3)
    d1_qz1(2)
    d1_qz2(2)
    d1_qz1(3)
    d1_qz2(3)
    for hc in range(2, FHC):
        if hc + 2 < FHC:
            d1_dma(hc + 2)
        d1_body(hc)
        if hc + 2 < FHC:
            d1_qz1(hc + 2)
            d1_qz2(hc + 2)

    def _(sc):
        for op in sig_ops:
            op(sc)
    blk.scalar(_)
    assert CNT["S"] == SIG_PLAN[0], (CNT["S"], SIG_PLAN[0])
    esD1.close()


    esD2 = ExitStack()
    pout = [esD2.enter_context(nc.psum_tensor(f"pout{i}", [128, 1024], F32)) for i in range(2)]
    pending_stores = []

    def d2_front(doc):
        slot = doc % 2
        pstore = pending_stores.pop(0) if pending_stores else None

        def _(sync):
            if doc < 2:
                wt(sync, "T", tok[f"x2mm{FHC-1}"])
                wt(sync, "V", tok[f"qz1_{FHC-1}"])
            else:
                wt(sync, "V", tok[f"qz3b_{doc-2}"])
            if doc != 0:
                i = sync.dma_start(out=w3raw_sb[slot].rearrange("p a b -> p (a b)"),
                                   in_=w3_d[doc])
                tok[f"w3d{doc}"] = inc(i, f"Lw3{slot}", 16)
            if pstore is not None:
                dprev, sprev = pstore
                wt(sync, "V", tok[f"fin{dprev}"])
                i = sync.dma_start(out=y_d[:, dprev, :], in_=outst_sb[sprev][:])
                tok[f"store{dprev}"] = inc(i, f"Lst{sprev}", 16)
        blk.sync(_)

        def _(sc):
            if doc < 2:
                wt(sc, "T", tok[f"x2mm{FHC-1}"])
            if doc >= 2:
                wt(sc, "V", tok[f"fin{doc-2}"])
            i = sc.dma_start(out=xrld_sb[slot][:], in_=xsp_d[:, doc, :])
            tok[f"xr{doc}"] = inc(i, f"Lxr{slot}", 16)
        blk.scalar(_)

        HB = FHC // 2 * 128
        w3src = w3pre_sb if doc == 0 else w3raw_sb[slot]
        w3r = w3src.rearrange("p a b -> p (a b)")
        q3r = qh3_sb[slot].rearrange("p a b -> p (a b)")

        def _(v):
            if doc == 0:
                wt(v, "Lw30", tok["w3pre"])
                wt(v, "V", tok["gfinw3"])
            else:
                wt(v, f"Lw3{slot}", tok[f"w3d{doc}"])
            if doc < 2:
                wt(v, "T", tok[f"x2mm{FHC-1}"])
            else:
                wt(v, "T", tok[f"omma_{doc-2}"])
            v.drain()
            v.tensor_scalar(a3scr_sb[:, 0:HB], w3r[:, 0:HB], vs(V_T3), None, ALU.is_ge)
            v.drain()
            i = v.scalar_tensor_tensor(q3r[:, 0:HB], w3r[:, 0:HB], vs(V_NT3),
                                       a3scr_sb[:, 0:HB], ALU.is_le, ALU.subtract)
            tok[f"qz3a_{doc}"] = inc(i, "V")
        blk.vector(_)

        def _(v):
            if doc >= 2:
                wt(v, "T", tok[f"omm{doc-2}"])
            v.drain()
            v.tensor_scalar(a3scr_sb[:, HB:], w3r[:, HB:], vs(V_T3), None, ALU.is_ge)
            v.drain()
            i = v.scalar_tensor_tensor(q3r[:, HB:], w3r[:, HB:], vs(V_NT3),
                                       a3scr_sb[:, HB:], ALU.is_le, ALU.subtract)
            tok[f"qz3b_{doc}"] = inc(i, "V")
        blk.vector(_)

    def d2_body(doc):
        slot = doc % 2

        def _(t):
            wt(t, "V", tok[f"qz3a_{doc}"])
            if doc == 0:
                wt(t, "V", tok[f"hid{FHC-1}"])
            if doc >= 2:
                wt(t, "V", tok[f"fin{doc-2}"])
            for hc in range(FHC // 2):
                lhs = qh3_sb[slot][:, hc, :]
                t.matmul(pout[slot][:, 0:512], lhs, hid_sb[:, hc, 0:512],
                         start=(hc == 0), stop=False)
                i = t.matmul(pout[slot][:, 512:1024], lhs, hid_sb[:, hc, 512:1024],
                             start=(hc == 0), stop=False)
            tok[f"omma_{doc}"] = inc(i, "T")
        blk.tensor(_)

        def _(t):
            wt(t, "V", tok[f"qz3b_{doc}"])
            for hc in range(FHC // 2, FHC):
                lhs = qh3_sb[slot][:, hc, :]
                t.matmul(pout[slot][:, 0:512], lhs, hid_sb[:, hc, 0:512],
                         start=False, stop=(hc == FHC - 1))
                i = t.matmul(pout[slot][:, 512:1024], lhs, hid_sb[:, hc, 512:1024],
                             start=False, stop=(hc == FHC - 1))
            tok[f"omm{doc}"] = inc(i, "T")
        blk.tensor(_)

        def _(v):
            wt(v, "T", tok[f"omm{doc}"])
            wt(v, f"Lxr{slot}", tok[f"xr{doc}"])
            if doc >= 2:
                wt(v, f"Lst{slot}", tok[f"store{doc-2}"])
            i = v.tensor_tensor(outst_sb[slot][:], xrld_sb[slot][:], pout[slot][:],
                                ALU.subtract)
            tok[f"fin{doc}"] = inc(i, "V")
        blk.vector(_)
        pending_stores.append((doc, slot))

    d2_front(0)
    for doc in range(DC):
        if doc + 1 < DC:
            d2_front(doc + 1)
        d2_body(doc)

    def _(sync):
        for dprev, sprev in pending_stores:
            wt(sync, "V", tok[f"fin{dprev}"])
            i = sync.dma_start(out=y_d[:, dprev, :], in_=outst_sb[sprev][:])
            tok[f"store{dprev}"] = inc(i, f"Lst{sprev}", 16)
        sync.wait_ge(SEMS["Lst0"], CNT["Lst0"])
        sync.wait_ge(SEMS["Lst1"], CNT["Lst1"])
    blk.sync(_)
    esD2.close()

    assert gstate["pos"] == len(gsched), gstate["pos"]
    es_top.close()
    return nc



def _stage_inputs(x, Wq, Wk, Wv, Wo, w1, w2, w3):
    x = np.ascontiguousarray(np.asarray(x, dtype=np.float32))
    xf = np.ascontiguousarray(x.reshape(8192, D).T)

    wq_t = np.ascontiguousarray(
        np.asarray(Wq, np.float32).reshape(DC, 128, H, 128).transpose(2, 1, 0, 3).reshape(H, 128, D))
    wk_t = np.ascontiguousarray(
        np.asarray(Wk, np.float32).reshape(DC, 128, 4, 128).transpose(2, 1, 0, 3).reshape(4, 128, D))
    wv_t = np.ascontiguousarray(np.asarray(Wv, np.float32).reshape(DC, 128, 512))
    wo_t = np.ascontiguousarray(
        np.asarray(Wo, np.float32).reshape(DC, 128, DC, 128).transpose(2, 1, 0, 3).reshape(DC, 128, D))
    w1_t = np.ascontiguousarray(
        np.asarray(w1, np.float32).reshape(FHC, 128, DC, 128).transpose(0, 3, 2, 1).reshape(FHC, 128, D))
    w2_t = np.ascontiguousarray(
        np.asarray(w2, np.float32).reshape(FHC, 128, DC, 128).transpose(0, 3, 2, 1).reshape(FHC, 128, D))
    w3_t = np.ascontiguousarray(
        np.asarray(w3, np.float32).reshape(DC, 128, FHC, 128).transpose(0, 3, 2, 1).reshape(DC, 128, FH))

    r = np.arange(128)[:, None]
    c = np.arange(128)[None, :]
    msk = np.zeros((128, 2, 128), np.float32)
    msk[:, 0, :] = np.where(c > r, 0.0, NEG)
    msk[:, 1, :] = np.where(c <= r, 0.0, NEG)
    msk = msk.astype(BF)
    idn = np.eye(128, dtype=BF)

    w12v = np.concatenate([
        w1_t.reshape(FHC, 128, 2, 1024).transpose(0, 2, 1, 3).reshape(2 * FHC, 128, 1024),
        w2_t.reshape(FHC, 128, 2, 1024).transpose(0, 2, 1, 3).reshape(2 * FHC, 128, 1024)])
    w3v = w3_t.reshape(DC, 128, 4, 1408).transpose(0, 2, 1, 3).reshape(4 * DC, 128, 1408)

    hm0 = np.zeros((128, 4, 512), np.float32)
    for j in range(4):
        kk = np.arange(512)[None, :]
        hm0[:, j, :] = np.where(kk < 512 - 128 * j, NEG, 0.0)
    hm0 = hm0.astype(BF)
    hmz = np.zeros((128, 4, 512), BF)

    in_maps = []
    for core in range(8):
        t0 = core * TOWN
        x_own = np.ascontiguousarray(
            xf[:, t0:t0 + TOWN].reshape(DC, 128, TOWN).transpose(1, 0, 2))
        if core % 4 == 0:
            halo = np.zeros((D, THALO), np.float32)
        else:
            halo = xf[:, t0 - THALO:t0]
        x_halo = np.ascontiguousarray(
            halo.reshape(DC, 128, 4, 128).transpose(2, 1, 0, 3).reshape(4, 128, D))
        gsh12 = np.ascontiguousarray(np.concatenate(
            [w12v[0:88][core::8], w12v[88:176][core::8]]))
        gsh3 = np.ascontiguousarray(w3v[core::8])
        in_maps.append(dict(
            x_own=x_own, x_halo=x_halo,
            wq=wq_t, wk=wk_t, wv=wv_t, wo=wo_t,
            w1=w1_t, w2=w2_t, w3=w3_t,
            gsh12=gsh12, gsh3=gsh3,
            msk=msk, hmsk=(hm0 if core % 4 == 0 else hmz), idn=idn,
        ))
    return in_maps


_PROG_CACHE = {}


def kernel(L0_attn_x, L0_attn_Wq, L0_attn_Wk, L0_attn_Wv, L0_attn_Wo,
           L0_ffn_w1, L0_ffn_w2, L0_ffn_w3):
    in_maps = _stage_inputs(L0_attn_x, L0_attn_Wq, L0_attn_Wk, L0_attn_Wv,
                            L0_attn_Wo, L0_ffn_w1, L0_ffn_w2, L0_ffn_w3)
    if "nc" not in _PROG_CACHE:
        _PROG_CACHE["nc"] = build_program()
    nc = _PROG_CACHE["nc"]
    res = run_bass_kernel_spmd(nc, in_maps, core_ids=list(range(8)))

    out = np.empty((D, 8192), np.float32)
    for core in range(8):
        shard = np.asarray(res.results[core]["y"])
        out[:, core * TOWN:(core + 1) * TOWN] = shard.transpose(1, 0, 2).reshape(D, TOWN)
    return np.ascontiguousarray(out.T.reshape(2, 4096, D))
